# version 06
# baseline (speedup 1.0000x reference)
import contextlib
import numpy as np
import concourse.bass as bass
import concourse.mybir as mybir
from concourse.alu_op_type import AluOpType as OP
from concourse.bass_utils import run_bass_kernel_spmd

F32 = mybir.dt.float32
BF16 = mybir.dt.bfloat16
AF = mybir.ActivationFunctionType
AX = mybir.AxisListType

T = 2048
D = 1024
DK = 8
NCORES = 8
LN_EPS = 1e-5


class Sem:
    __slots__ = ("h", "v")

    def __init__(self, h):
        self.h = h
        self.v = 0

    def inc(self, ins, k=1):
        ins.then_inc(self.h, k)
        self.v += k

    def wait(self, eng, v=None):
        eng.wait_ge(self.h, self.v if v is None else v)


class Ctx:
    def __init__(self, nc, st):
        self.nc = nc
        self.st = st
        self.uid = 0

    def name(self, base):
        self.uid += 1
        return f"{base}_{self.uid}"

    def sem(self, base):
        return Sem(self.st.enter_context(self.nc.semaphore(self.name(base))))


def emit_ln(ctx, blk, cst, rc_ap, xh_ap, W, ps, sq, rows, xhf_ap=None, gcol=None, bcol=None):
    pe, act, dve = cst["pe"], cst["act"], cst["dve"]
    lnrow, nmb, rb = ps["lnrow"], ps["nmb"], ps["rb"]

    @blk.tensor
    def _(t):
        for k in range(DK):
            ins = t.matmul(lnrow[:, 0:W], lhsT=cst["ones_col"][:], rhs=rc_ap[:, k, :],
                           start=(k == 0), stop=(k == DK - 1))
        pe.inc(ins)

    @blk.scalar
    def _(s):
        act.inc(s.activation(out=sq[:, :, 0:W], in_=rc_ap, func=AF.Square))

    @blk.tensor
    def _(t):
        act.wait(t)
        for k in range(DK):
            ins = t.matmul(lnrow[:, 512:512 + W], lhsT=cst["ones_col"][:], rhs=sq[:, k, 0:W],
                           start=(k == 0), stop=(k == DK - 1))
        pe.inc(ins)

    @blk.vector
    def _(v):
        pe.wait(v)
        nm = rows[:, 0:W]
        m2 = rows[:, 512:512 + W]
        var = rows[:, 1024:1024 + W]
        rv = rows[:, 1536:1536 + W]
        v.tensor_scalar(out=nm, in0=lnrow[:, 0:W], scalar1=-1.0 / D, scalar2=None, op0=OP.mult)
        v.drain()
        v.tensor_tensor(m2, nm, nm, OP.mult)
        v.drain()
        v.scalar_tensor_tensor(out=var, in0=lnrow[:, 512:512 + W], scalar=1.0 / D,
                               in1=m2, op0=OP.mult, op1=OP.subtract)
        v.drain()
        v.tensor_scalar(out=var, in0=var, scalar1=float(LN_EPS), scalar2=None, op0=OP.add)
        v.drain()
        dve.inc(v.reciprocal(rv, var))

    @blk.scalar
    def _(s):
        dve.wait(s)
        act.inc(s.activation(out=rows[:, 512:512 + W], in_=rows[:, 1536:1536 + W], func=AF.Sqrt))

    @blk.tensor
    def _(t):
        act.wait(t)
        t.matmul(nmb[:, 0:W], lhsT=cst["ones_row"][:], rhs=rows[:, 0:W], start=True, stop=True)
        pe.inc(t.matmul(rb[:, 0:W], lhsT=cst["ones_row"][:], rhs=rows[:, 512:512 + W], start=True, stop=True))

    @blk.vector
    def _(v):
        pe.wait(v)
        for k in range(DK):
            v.tensor_tensor(sq[:, k, 0:W], rc_ap[:, k, :], nmb[:, 0:W], OP.add)
        v.drain()
        if gcol is None:
            for k in range(DK):
                ins = v.tensor_tensor(xh_ap[:, k, :], sq[:, k, 0:W], rb[:, 0:W], OP.mult)
                if xhf_ap is not None:
                    ins = v.tensor_tensor(xhf_ap[:, k, :], sq[:, k, 0:W], rb[:, 0:W], OP.mult)
            dve.inc(ins)
        else:
            for k in range(DK):
                v.tensor_tensor(sq[:, k, 0:W], sq[:, k, 0:W], rb[:, 0:W], OP.mult)
            v.drain()
            for k in range(DK):
                ins = v.tensor_scalar(out=xh_ap[:, k, :], in0=sq[:, k, 0:W],
                                      scalar1=gcol[:, k:k + 1], scalar2=bcol[:, k:k + 1],
                                      op0=OP.mult, op1=OP.add)
                if xhf_ap is not None:
                    ins = v.tensor_scalar(out=xhf_ap[:, k, :], in0=sq[:, k, 0:W],
                                          scalar1=gcol[:, k:k + 1], scalar2=bcol[:, k:k + 1],
                                          op0=OP.mult, op1=OP.add)
            dve.inc(ins)


def emit_moe(ctx, cst, src, dst, w, nE, F, topk, actfn=AF.Relu):
    nc = ctx.nc
    NS = F // 1024
    Ms = 8
    NT = T // 512
    pe, act, dve = cst["pe"], cst["act"], cst["dve"]
    dma, wdma, st_sem = cst["dma"], cst["wdma"], cst["st"]

    with contextlib.ExitStack() as sl:
        SB = lambda b, sh, dt: sl.enter_context(nc.sbuf_tensor(ctx.name(b), sh, dt))
        XH = SB("moe_xh", [128, DK, T], BF16)
        ACC = SB("moe_acc", [128, DK, T], F32)
        H1 = SB("moe_h1", [128, Ms, T], BF16)
        SQ = cst["SQ"]
        ROWS = cst["ROWS"]
        WGS = SB("moe_wg", [128, DK, nE], BF16)
        BGC = SB("moe_bgc", [nE, 1], F32)
        BIC = SB("moe_bic", [nE, 1], F32)
        B1C = SB("moe_b1c", [128, F // 128, nE], F32)
        LG = SB("moe_lg", [nE, 512], F32)
        LGT = SB("moe_lgt", [128, 4, nE], F32)
        MST = SB("moe_mst", [128, 16], F32)
        CWT = SB("moe_cwt", [128, 4, nE], F32)
        CWSB = SB("moe_cwsb", [nE, 512], F32)

        with contextlib.ExitStack() as sp:
            P = lambda b, sh: sp.enter_context(nc.psum_tensor(ctx.name(b), sh, F32))
            ps = {"lnrow": P("moe_lnrow", [1, 1024]),
                  "nmb": P("moe_nmb", [128, 512]),
                  "rb": P("moe_rb", [128, 512])}
            LGP = P("moe_lgp", [nE, 512])
            TRP = P("moe_trp", [128, 4 * nE])
            CWP = P("moe_cwp", [nE, 512])
            blk = sp.enter_context(nc.Block(no_gpsimd_drain=True))

            @blk.gpsimd
            def _(g):
                wdma.inc(g.dma_start(out=WGS[:], in_=w["Wg"].rearrange("(k p) e -> p k e", p=128)), 16)
                wdma.inc(g.dma_start(out=BGC[:], in_=w["bg"].rearrange("(e u) -> e u", u=1)), 16)
                wdma.inc(g.dma_start(out=BIC[:], in_=w["bias"].rearrange("(e u) -> e u", u=1)), 16)
                wdma.inc(g.dma_start(out=B1C[:], in_=w["b1c"]), 16)

            for tb in range(NT):
                tw = slice(tb * 512, (tb + 1) * 512)

                @blk.sync
                def _(s, tw=tw):
                    dma.inc(s.dma_start(out=ACC[:, :, tw], in_=src[:, :, tw].transpose([1, 0, 2])), 16)

                @blk.tensor
                def _(t):
                    dma.wait(t)
                    dve.wait(t)

                @blk.scalar
                def _(s):
                    dma.wait(s)
                    dve.wait(s)

                emit_ln(ctx, blk, cst, ACC[:, :, tw], XH[:, :, tw], 512, ps, SQ, ROWS)

                @blk.tensor
                def _(t, tw=tw):
                    dve.wait(t)
                    wdma.wait(t)
                    for k in range(DK):
                        ins = t.matmul(LGP[:], lhsT=WGS[:, k, :], rhs=XH[:, k, tw],
                                       start=(k == 0), stop=(k == DK - 1))
                    pe.inc(ins)

                @blk.vector
                def _(v):
                    pe.wait(v)
                    dve.inc(v.tensor_scalar(out=LG[:], in0=LGP[:], scalar1=BGC[:], scalar2=BIC[:],
                                             op0=OP.add, op1=OP.add))

                @blk.tensor
                def _(t):
                    dve.wait(t)
                    for j in range(4):
                        ins = t.transpose(TRP[:, j * nE:(j + 1) * nE], in_=LG[:, j * 128:(j + 1) * 128],
                                          identity=cst["ident_f32"][0:nE, 0:nE])
                    pe.inc(ins)

                @blk.vector
                def _(v):
                    pe.wait(v)
                    v.tensor_copy(LGT[:], TRP[:].rearrange("p (j e) -> p j e", e=nE))
                    v.drain()
                    v.tensor_reduce(MST[:, 0:4], LGT[:], axis=AX.X, op=OP.max)
                    v.drain()
                    for j in range(4):
                        v.tensor_scalar(out=CWT[:, j, :], in0=LGT[:, j, :], scalar1=MST[:, j:j + 1],
                                        scalar2=None, op0=OP.is_equal)
                    v.drain()
                    if topk == 2:
                        for j in range(4):
                            v.tensor_scalar(out=SQ[:, 0, j * nE:(j + 1) * nE], in0=CWT[:, j, :],
                                            scalar1=1e30, scalar2=None, op0=OP.mult)
                        v.drain()
                        for j in range(4):
                            sc = SQ[:, 0, j * nE:(j + 1) * nE]
                            v.tensor_tensor(sc, LGT[:, j, :], sc, OP.subtract)
                        v.drain()
                        for j in range(4):
                            v.tensor_reduce(MST[:, 4 + j:5 + j], SQ[:, 0, j * nE:(j + 1) * nE],
                                            axis=AX.X, op=OP.max)
                        v.drain()
                        for j in range(4):
                            v.tensor_scalar(out=SQ[:, 1, j * nE:(j + 1) * nE],
                                            in0=SQ[:, 0, j * nE:(j + 1) * nE],
                                            scalar1=MST[:, 4 + j:5 + j], scalar2=None, op0=OP.is_equal)
                        ins = v.tensor_tensor(MST[:, 8:12], MST[:, 4:8], MST[:, 0:4], OP.subtract)
                    else:
                        for j in range(4):
                            ins = v.tensor_scalar(out=SQ[:, 2, j * nE:(j + 1) * nE], in0=LGT[:, j, :],
                                                  scalar1=MST[:, j:j + 1], scalar2=None, op0=OP.subtract)
                    dve.inc(ins)

                @blk.scalar
                def _(s):
                    dve.wait(s)
                    if topk == 2:
                        ins = s.activation(out=MST[:, 12:16], in_=MST[:, 8:12], func=AF.Exp)
                    else:
                        ins = s.activation(out=SQ[:, 3, 0:4 * nE], in_=SQ[:, 2, 0:4 * nE], func=AF.Exp)
                    act.inc(ins)

                @blk.vector
                def _(v):
                    act.wait(v)
                    if topk == 2:
                        e_ = MST[:, 12:16]
                        v.tensor_scalar(out=MST[:, 4:8], in0=e_, scalar1=1.0, scalar2=None, op0=OP.add)
                        v.drain()
                        v.reciprocal(MST[:, 8:12], MST[:, 4:8])
                        v.drain()
                        v.tensor_tensor(MST[:, 12:16], e_, MST[:, 8:12], OP.mult)
                        v.drain()
                        for j in range(4):
                            v.tensor_scalar(out=CWT[:, j, :], in0=CWT[:, j, :],
                                            scalar1=MST[:, 8 + j:9 + j], scalar2=None, op0=OP.mult)
                        v.drain()
                        for j in range(4):
                            m1m = SQ[:, 1, j * nE:(j + 1) * nE]
                            ins = v.scalar_tensor_tensor(out=CWT[:, j, :], in0=m1m,
                                                         scalar=MST[:, 12 + j:13 + j], in1=CWT[:, j, :],
                                                         op0=OP.mult, op1=OP.add)
                    else:
                        v.tensor_reduce(MST[:, 4:8], SQ[:, 3, 0:4 * nE].rearrange("p (j e) -> p j e", e=nE),
                                        axis=AX.X, op=OP.add)
                        v.drain()
                        v.reciprocal(MST[:, 8:12], MST[:, 4:8])
                        v.drain()
                        for j in range(4):
                            ins = v.tensor_scalar(out=CWT[:, j, :], in0=CWT[:, j, :],
                                                  scalar1=MST[:, 8 + j:9 + j], scalar2=None, op0=OP.mult)
                    dve.inc(ins)

                @blk.tensor
                def _(t):
                    dve.wait(t)
                    for j in range(4):
                        ins = t.transpose(CWP[:, j * 128:(j + 1) * 128], in_=CWT[:, j, :],
                                          identity=cst["ident_f32"][:])
                    pe.inc(ins)

                @blk.vector
                def _(v):
                    pe.wait(v)
                    st_sem.wait(v)
                    dve.inc(v.tensor_copy(CWSB[:], CWP[:]))

                @blk.sync
                def _(s, tw=tw):
                    dve.wait(s)
                    st_sem.inc(s.dma_start(out=w["cw_s"][:, tw], in_=CWSB[:]), 16)

            @blk.sync
            def _(s):
                st_sem.wait(s)

        with contextlib.ExitStack() as sp:
            P = lambda b, sh: sp.enter_context(nc.psum_tensor(ctx.name(b), sh, F32))
            SB2 = lambda b, sh, dt: sp.enter_context(nc.sbuf_tensor(ctx.name(b), sh, dt))
            CWBP = P("moe_cwbp", [128, 2048])
            G1P = [P("moe_g1p", [128, 512]) for _ in range(2)]
            G2P = [P("moe_g2p", [128, 512]) for _ in range(2)]
            WSH = SB2("moe_wsh", [128, DK, 1024], BF16)
            W1H = WSH
            W2H = WSH[:].rearrange("p a b -> p (a b)").rearrange("p (a b) -> p a b", b=D)
            CWRB = SB2("moe_cwrb", [1, T], BF16)
            B2E = SB2("moe_b2e", [1, D], BF16)
            CWBS = SB2("moe_cwbs", [128, T], BF16)
            blk = sp.enter_context(nc.Block(no_gpsimd_drain=True))

            for e in range(nE):
                @blk.gpsimd
                def _(g, e=e):
                    pe.wait(g)
                    wdma.inc(g.dma_start(out=CWRB[:], in_=w["cw_s"][e:e + 1, :]), 16)
                    wdma.inc(g.dma_start(out=B2E[:], in_=w["b2"][e, :].rearrange("(u d) -> u d", u=1)), 16)
                    wdma.inc(g.dma_start(
                        out=W1H[:], in_=w["W1"][e, :, 0:1024].rearrange("(k p) m -> p k m", p=128)), 16)

                @blk.tensor
                def _(t, e=e):
                    dve.wait(t)
                    wdma.wait(t)
                    for tb in range(NT):
                        ins = t.matmul(CWBP[:, tb * 512:(tb + 1) * 512], lhsT=cst["ones_row_bf"][:],
                                       rhs=CWRB[:, tb * 512:(tb + 1) * 512], start=True, stop=True)
                    pe.inc(ins)

                @blk.vector
                def _(v):
                    pe.wait(v)
                    dve.inc(v.tensor_copy(CWBS[:], CWBP[:]))

                for sp_i in range(NS):
                    hof = sp_i * Ms
                    if sp_i > 0:
                        @blk.gpsimd
                        def _(g, e=e, sp_i=sp_i):
                            pe.wait(g)
                            c0 = sp_i * 1024
                            wdma.inc(g.dma_start(
                                out=W1H[:], in_=w["W1"][e, :, c0:c0 + 1024].rearrange("(k p) m -> p k m", p=128)), 16)

                    act_base = act.v
                    g1_pe_base = pe.v

                    @blk.tensor
                    def _(t, e=e, act_base=act_base):
                        wdma.wait(t)
                        dve.wait(t)
                        ci = 0
                        for m in range(Ms):
                            for tb in range(NT):
                                if ci >= 2:
                                    t.wait_ge(act.h, act_base + ci - 1)
                                for k in range(DK):
                                    ins = t.matmul(G1P[ci % 2][:], lhsT=W1H[:, k, m * 128:(m + 1) * 128],
                                                   rhs=XH[:, k, tb * 512:(tb + 1) * 512],
                                                   start=(k == 0), stop=(k == DK - 1))
                                pe.inc(ins)
                                ci += 1

                    @blk.scalar
                    def _(s, e=e, hof=hof, g1_pe_base=g1_pe_base):
                        ci = 0
                        for m in range(Ms):
                            for tb in range(NT):
                                s.wait_ge(pe.h, g1_pe_base + ci + 1)
                                act.inc(s.activation(out=H1[:, m, tb * 512:(tb + 1) * 512], in_=G1P[ci % 2][:],
                                                     func=actfn, bias=B1C[:, hof + m, e:e + 1]))
                                ci += 1

                    @blk.vector
                    def _(v, e=e, g1_act_base=act_base):
                        v.drain()
                        ci = 0
                        for m in range(Ms):
                            for tb in range(NT):
                                v.wait_ge(act.h, g1_act_base + ci + 1)
                                tw = slice(tb * 512, (tb + 1) * 512)
                                ins = v.tensor_tensor(H1[:, m, tw], H1[:, m, tw], CWBS[:, tw], OP.mult)
                                ci += 1
                        dve.inc(ins)

                    @blk.gpsimd
                    def _(g, e=e, sp_i=sp_i):
                        pe.wait(g)
                        c0 = sp_i * 1024
                        wdma.inc(g.dma_start(
                            out=W2H[:], in_=w["W2"][e, c0:c0 + 1024, :].rearrange("(m p) d -> p m d", p=128)), 16)

                    dve_base = dve.v
                    g2_pe_base = pe.v

                    @blk.tensor
                    def _(t, e=e, sp_i=sp_i, dve_base=dve_base):
                        wdma.wait(t)
                        dve.wait(t)
                        ci = 0
                        for tb in range(NT):
                            for mo in range(DK):
                                if ci >= 2:
                                    t.wait_ge(dve.h, dve_base + ci - 1)
                                for hk in range(Ms):
                                    last = (sp_i > 0 and hk == Ms - 1)
                                    ins = t.matmul(G2P[ci % 2][:], lhsT=W2H[:, hk, mo * 128:(mo + 1) * 128],
                                                   rhs=H1[:, hk, tb * 512:(tb + 1) * 512],
                                                   start=(hk == 0), stop=last)
                                if sp_i == 0:
                                    ins = t.matmul(G2P[ci % 2][:], lhsT=B2E[:, mo * 128:(mo + 1) * 128],
                                                   rhs=CWRB[:, tb * 512:(tb + 1) * 512], start=False, stop=True)
                                pe.inc(ins)
                                ci += 1

                    @blk.vector
                    def _(v, e=e, g2_pe_base=g2_pe_base):
                        v.drain()
                        ci = 0
                        for tb in range(NT):
                            for mo in range(DK):
                                v.wait_ge(pe.h, g2_pe_base + ci + 1)
                                tw = slice(tb * 512, (tb + 1) * 512)
                                dve.inc(v.tensor_tensor(ACC[:, mo, tw], ACC[:, mo, tw], G2P[ci % 2][:], OP.add))
                                ci += 1

            @blk.sync
            def _(s):
                dve.wait(s)
                for tb in range(NT):
                    tw = slice(tb * 512, (tb + 1) * 512)
                    st_sem.inc(s.dma_start(out=dst[:, :, tw].transpose([1, 0, 2]), in_=ACC[:, :, tw]), 16)
                st_sem.wait(s)


def emit_attn_qc(ctx, cst, blk, bufs, qcb, L, scale, causal):
    pe, act, dve = cst["pe"], cst["act"], cst["dve"]
    QCB, KF, VT = bufs["QCB"], bufs["KF"], bufs["VT"]
    PB, PT = bufs["PB"], bufs["PT"]
    SPS, AVP = bufs["SPS"], bufs["AVP"]
    COLS = bufs["COLS"]
    YSLAB, qoff = bufs["YSLAB"], bufs["qoff"]
    nb = (L + 127) // 128
    for h in [hh for hh in range(16) if hh % 2 == 0] + [hh for hh in range(16) if hh % 2 == 1]:
        hb = 64 * (h % 2)
        hm = h // 2

        @blk.tensor
        def _(t, h=h, hb=hb, hm=hm):
            act.wait(t)
            dve.wait(t)
            lq = QCB[hb:hb + 64, hm, :]
            pos = 0
            while pos < L:
                w = min(512, L - pos)
                ins = t.matmul(SPS[:, pos:pos + w], lhsT=lq,
                               rhs=KF[hb:hb + 64, hm, pos:pos + w], start=True, stop=True)
                pos += w
            pe.inc(ins)

        @blk.vector
        def _(v, h=h):
            pe.wait(v)
            if causal:
                v.tensor_tensor(SPS[:, L - 128:L], SPS[:, L - 128:L], cst["cmask"][:], OP.add)
                v.drain()
            v.tensor_reduce(COLS[:, 0:1], SPS[:, 0:L], axis=AX.X, op=OP.max)
            v.drain()
            dve.inc(v.tensor_scalar(out=COLS[:, 3:4], in0=COLS[:, 0:1], scalar1=-scale,
                                    scalar2=None, op0=OP.mult))

        @blk.scalar
        def _(s, h=h):
            dve.wait(s)
            act.inc(s.activation(out=PB[:, 0:L], in_=SPS[:, 0:L], func=AF.Exp,
                                 scale=scale, bias=COLS[:, 3:4], accum_out=COLS[:, 1:2]))

        @blk.vector
        def _(v, h=h):
            act.wait(v)
            v.reciprocal(COLS[:, 2:3], COLS[:, 1:2])
            v.drain()
            dve.inc(v.tensor_scalar(out=PB[:, 0:L], in0=PB[:, 0:L], scalar1=COLS[:, 2:3],
                                    scalar2=None, op0=OP.mult))

        nwav = (nb + 7) // 8
        for wv in range(nwav):
            b0, b1 = wv * 8, min((wv + 1) * 8, nb)

            @blk.tensor
            def _(t, h=h, b0=b0, b1=b1):
                dve.wait(t)
                for b in range(b0, b1):
                    ins = t.transpose(bufs["TPPb"][:, (b - b0) * 128:(b - b0 + 1) * 128],
                                      in_=PB[:, b * 128:(b + 1) * 128], identity=cst["ident_bf"][:])
                pe.inc(ins)

            @blk.vector
            def _(v, h=h, b0=b0, b1=b1):
                pe.wait(v)
                dve.inc(v.tensor_copy(PT[:, b0:b1, :].rearrange("p b q -> p (b q)"),
                                      bufs["TPPb"][:, 0:(b1 - b0) * 128]))

        @blk.tensor
        def _(t, h=h, hb=hb, hm=hm):
            dve.wait(t)
            for b in range(nb):
                tkc = b
                ins = t.matmul(AVP[:], lhsT=VT[:, tkc, h * 64:(h + 1) * 64],
                               rhs=PT[:, tkc, :], start=(b == 0), stop=(b == nb - 1))
            pe.inc(ins)

        @blk.vector
        def _(v, h=h, hb=hb, hm=hm):
            pe.wait(v)
            dve.inc(v.tensor_copy(YSLAB[hb:hb + 64, hm, qoff:qoff + 128], AVP[:]))


def emit_l0_attn_b(ctx, cst, src, dst, w):
    nc = ctx.nc
    pe, act, dve = cst["pe"], cst["act"], cst["dve"]
    dma, wdma, st_sem = cst["dma"], cst["wdma"], cst["st"]
    with contextlib.ExitStack() as sl:
        SB = lambda b, sh, dt: sl.enter_context(nc.sbuf_tensor(ctx.name(b), sh, dt))
        XH = SB("a3_xh", [128, DK, T], BF16)
        KF = SB("a3_kf", [128, DK, T], BF16)
        VT = SB("a3_vt", [128, 16, 1024], BF16)
        SQ = cst["SQ"]
        ROWS = cst["ROWS"]
        WKVD = SB("a3_wkvd", [128, DK, 128], BF16)
        WKVU = SB("a3_wkvu", [128, 2048], BF16)
        WQ = SB("a3_wq", [128, DK, 1024], BF16)
        WOUT = SB("a3_wout", [128, DK, 1024], BF16)
        LATB = SB("a3_latb", [128, 512], BF16)

        with contextlib.ExitStack() as sp:
            P = lambda b, sh, dt=F32: sp.enter_context(nc.psum_tensor(ctx.name(b), sh, dt))
            ps = {"lnrow": P("a3_lnrow", [1, 1024]),
                  "nmb": P("a3_nmb", [128, 512]),
                  "rb": P("a3_rb", [128, 512])}
            LATP = P("a3_latp", [128, 512])
            KVP = [P("a3_kvp", [128, 512]) for _ in range(2)]
            blk = sp.enter_context(nc.Block(no_gpsimd_drain=True))

            @blk.gpsimd
            def _(g):
                wdma.inc(g.dma_start(out=WKVD[:], in_=w["w_kv_down"].rearrange("(k p) m -> p k m", p=128)), 16)
                wdma.inc(g.dma_start(out=WKVU[:], in_=w["w_kv_up"]), 16)
                wdma.inc(g.dma_start(out=WQ[:], in_=w["w_q"].rearrange("(k p) m -> p k m", p=128)), 16)
                wdma.inc(g.dma_start(out=WOUT[:], in_=w["w_out"].rearrange("(k p) m -> p k m", p=128)), 16)

            for tb in range(4):
                tw = slice(tb * 512, (tb + 1) * 512)

                @blk.sync
                def _(s, tw=tw):
                    pe.wait(s)
                    act.wait(s)
                    dve.wait(s)
                    dma.inc(s.dma_start(out=cst["RC"][:],
                                        in_=src[:, :, tw].transpose([1, 0, 2])), 16)

                @blk.tensor
                def _(t):
                    dma.wait(t)
                    dve.wait(t)

                @blk.scalar
                def _(s):
                    dma.wait(s)
                    dve.wait(s)

                emit_ln(ctx, blk, cst, cst["RC"][:], XH[:, :, tw], 512, ps, SQ, ROWS)

                @blk.tensor
                def _(t, tw=tw):
                    dve.wait(t)
                    wdma.wait(t)
                    for k in range(DK):
                        ins = t.matmul(LATP[:], lhsT=WKVD[:, k, :], rhs=XH[:, k, tw],
                                       start=(k == 0), stop=(k == DK - 1))
                    pe.inc(ins)

                @blk.vector
                def _(v):
                    pe.wait(v)
                    dve.inc(v.tensor_copy(LATB[:], LATP[:]))

                dve_base_k = dve.v

                @blk.tensor
                def _(t, tb=tb, tw=tw, dve_base_k=dve_base_k):
                    dve.wait(t)
                    for m in range(DK):
                        if m >= 2:
                            t.wait_ge(dve.h, dve_base_k + m - 1)
                        ins = t.matmul(KVP[m % 2][:], lhsT=WKVU[:, m * 128:(m + 1) * 128],
                                       rhs=LATB[:], start=True, stop=True)
                        pe.inc(ins)

                @blk.vector
                def _(v, tw=tw):
                    base = pe.v - DK
                    for m in range(DK):
                        v.wait_ge(pe.h, base + m + 1)
                        dve.inc(v.tensor_copy(KF[:, m, tw], KVP[m % 2][:]))

                dve_base_v = dve.v

                @blk.tensor
                def _(t, tb=tb, dve_base_v=dve_base_v):
                    dve.wait(t)
                    for qi in range(4):
                        for vh in range(2):
                            j = qi * 2 + vh
                            if j >= 2:
                                t.wait_ge(dve.h, dve_base_v + j - 1)
                            ins = t.matmul(KVP[j % 2][:],
                                           lhsT=LATB[:, qi * 128:(qi + 1) * 128],
                                           rhs=WKVU[:, 1024 + vh * 512:1024 + (vh + 1) * 512],
                                           start=True, stop=True)
                            pe.inc(ins)

                @blk.vector
                def _(v, tb=tb):
                    base = pe.v - 8
                    for qi in range(4):
                        for vh in range(2):
                            v.wait_ge(pe.h, base + qi * 2 + vh + 1)
                            dve.inc(v.tensor_copy(VT[:, tb * 4 + qi, vh * 512:(vh + 1) * 512],
                                                  KVP[(qi * 2 + vh) % 2][:]))

        with contextlib.ExitStack() as sp:
            P = lambda b, sh, dt=F32: sp.enter_context(nc.psum_tensor(ctx.name(b), sh, dt))
            SB2 = lambda b, sh, dt: sp.enter_context(nc.sbuf_tensor(ctx.name(b), sh, dt))
            SPS = P("a3_sps", [128, 2048])
            TPPb = P("a3_tpp", [128, 1024], BF16)
            AVP = P("a3_avp", [64, 128])
            PP2 = [P("a3_pp2", [128, 512]) for _ in range(2)]
            QP = [PP2[0][:, 0:128], PP2[1][:, 0:128]]
            OPP = PP2
            QCB = SB2("a3_qcb", [128, DK, 128], BF16)
            PB = SB2("a3_pb", [128, T], BF16)
            PT = SB2("a3_pt", [128, 16, 128], BF16)
            COLS = SB2("a3_cols", [128, 4], F32)
            YSLAB = SB2("a3_yslab", [128, DK, 512], BF16)
            blk = sp.enter_context(nc.Block(no_gpsimd_drain=True))

            bufs = {"KF": KF, "VT": VT, "PB": PB, "PT": PT, "SPS": SPS,
                    "TPPb": TPPb, "AVP": AVP, "COLS": COLS, "QCB": QCB,
                    "YSLAB": YSLAB, "qoff": 0}

            for qc in range(16):
                tb = qc // 4
                L = (qc + 1) * 128 if True else T
                bufs["qoff"] = (qc % 4) * 128

                qdve_base = dve.v

                @blk.tensor
                def _(t, qc=qc, qdve_base=qdve_base):
                    dve.wait(t)
                    act.wait(t)
                    for m in range(DK):
                        if m >= 2:
                            t.wait_ge(dve.h, qdve_base + m - 1)
                        for k in range(DK):
                            ins = t.matmul(QP[m % 2], lhsT=WQ[:, k, m * 128:(m + 1) * 128],
                                           rhs=XH[:, k, qc * 128:(qc + 1) * 128],
                                           start=(k == 0), stop=(k == DK - 1))
                        pe.inc(ins)

                @blk.vector
                def _(v, qc=qc):
                    base = pe.v - DK
                    for m in range(DK):
                        v.wait_ge(pe.h, base + m + 1)
                        dve.inc(v.tensor_copy(QCB[:, m, :], QP[m % 2]))

                emit_attn_qc(ctx, cst, blk, bufs, qc, L, 0.125, True)

                if qc % 4 == 3:
                    @blk.sync
                    def _(s, tb=tb):
                        st_sem.wait(s)
                        dma.inc(s.dma_start(out=cst["RC"][:],
                                            in_=src[:, :, tb * 512:(tb + 1) * 512].transpose([1, 0, 2])), 16)

                    odve_base = dve.v

                    @blk.tensor
                    def _(t, tb=tb, odve_base=odve_base):
                        dve.wait(t)
                        for mo in range(DK):
                            if mo >= 2:
                                t.wait_ge(dve.h, odve_base + mo - 1)
                            for k in range(DK):
                                ins = t.matmul(OPP[mo % 2][:], lhsT=WOUT[:, k, mo * 128:(mo + 1) * 128],
                                               rhs=YSLAB[:, k, :], start=(k == 0), stop=(k == DK - 1))
                            pe.inc(ins)

                    @blk.vector
                    def _(v, tb=tb):
                        dma.wait(v)
                        base = pe.v - DK
                        v.drain()
                        for mo in range(DK):
                            v.wait_ge(pe.h, base + mo + 1)
                            dve.inc(v.tensor_tensor(cst["RC"][:, mo, :], cst["RC"][:, mo, :],
                                                    OPP[mo % 2][:], OP.add))

                    @blk.sync
                    def _(s, tb=tb):
                        dve.wait(s)
                        st_sem.inc(s.dma_start(out=dst[:, :, tb * 512:(tb + 1) * 512].transpose([1, 0, 2]),
                                               in_=cst["RC"][:]), 16)

            @blk.sync
            def _(s):
                st_sem.wait(s)


def emit_rope_side(ctx, cst, blk, tb, out_f, ct, wu, bu_col, wr, br_col, wr_rhs, wr_nk,
                   ps, sbufs, w, tabname):
    pe, act, dve, dma = cst["pe"], cst["act"], cst["dve"], cst["dma"]
    PU, PR, PSW = ps["PU"], ps["PR"], ps["PSW"]
    QRF, T1, T2, C2B, S2B = sbufs["QRF"], sbufs["T1"], sbufs["T2"], sbufs["C2B"], sbufs["S2B"]
    tw = slice(tb * 512, (tb + 1) * 512)
    for m in range(DK):
        @blk.sync
        def _(s, m=m):
            pe.wait(s)
            dve.wait(s)
            dma.inc(s.dma_start(out=C2B[:], in_=w[tabname + "_cos"][m, :, tw]), 16)
            dma.inc(s.dma_start(out=S2B[:], in_=w[tabname + "_sin"][m, :, tw]), 16)

        @blk.tensor
        def _(t, m=m):
            dve.wait(t)
            act.wait(t)
            for kk in range(4):
                insu = t.matmul(PU[:], lhsT=wu[:, kk, m * 128:(m + 1) * 128],
                                rhs=ct[:, kk, :], start=(kk == 0), stop=(kk == 3))
            for kk in range(wr_nk):
                insr = t.matmul(PR[:], lhsT=wr[:, kk, m * 128:(m + 1) * 128],
                                rhs=wr_rhs[:, kk, :], start=(kk == 0), stop=(kk == wr_nk - 1))
            pe.inc(insr)

        @blk.vector
        def _(v, m=m):
            pe.wait(v)
            dve.inc(v.tensor_scalar(out=QRF[:], in0=PR[:], scalar1=br_col[:, m:m + 1],
                                    scalar2=None, op0=OP.add))

        @blk.tensor
        def _(t, m=m):
            dve.wait(t)
            pe.inc(t.matmul(PSW[:], lhsT=cst["perm_f32"][:], rhs=QRF[:], start=True, stop=True))

        @blk.vector
        def _(v, m=m):
            pe.wait(v)
            dma.wait(v)
            v.tensor_tensor(T1[:], QRF[:], C2B[:], OP.mult)
            v.tensor_tensor(T2[:], PSW[:], S2B[:], OP.mult)
            v.drain()
            v.tensor_tensor(T1[:], T1[:], T2[:], OP.add)
            v.drain()
            dve.inc(v.scalar_tensor_tensor(out=out_f[:, m, tw], in0=PU[:],
                                           scalar=bu_col[:, m:m + 1], in1=T1[:],
                                           op0=OP.add, op1=OP.add))


def emit_l1_attn_a(ctx, cst, src, dst, w):
    nc = ctx.nc
    pe, act, dve = cst["pe"], cst["act"], cst["dve"]
    dma, wdma, st_sem = cst["dma"], cst["wdma"], cst["st"]
    with contextlib.ExitStack() as sl:
        SB = lambda b, sh, dt: sl.enter_context(nc.sbuf_tensor(ctx.name(b), sh, dt))
        QF = SB("a5_qf", [128, DK, T], BF16)
        KF = SB("a5_kf", [128, DK, T], BF16)
        VT = SB("a5_vt", [128, 16, 1024], BF16)
        SQ = cst["SQ"]
        ROWS = cst["ROWS"]

        for side in ("q", "kv"):
            with contextlib.ExitStack() as sp:
                P = lambda b, sh, dt=F32: sp.enter_context(nc.psum_tensor(ctx.name(b), sh, dt))
                SB2 = lambda b, sh, dt: sp.enter_context(nc.sbuf_tensor(ctx.name(b), sh, dt))
                CP = [P("a5_cp", [128, 512]) for _ in range(2)]
                ps = {"lnrow": P("a5_lnrow", [1, 1024]),
                      "nmb": CP[0][:], "rb": CP[1][:]}
                rps = {"PU": P("a5_pu", [128, 512]), "PR": P("a5_pr", [128, 512]),
                       "PSW": P("a5_psw", [128, 512])}
                XHc = SB2("a5_xhc", [128, DK, 512], BF16)
                CT = SB2("a5_ct", [128, 4, 512], BF16)
                WD = SB2("a5_wd", [128, DK, 512], BF16)
                WU = SB2("a5_wu", [128, 4, 1024], BF16)
                WR = SB2("a5_wr", [128, DK if side == "kv" else 4, 1024], BF16)
                WUV = SB2("a5_wuv", [128, 4, 1024], BF16) if side == "kv" else None
                BD = SB2("a5_bd", [128, 4], F32)
                BU = SB2("a5_bu", [128, DK], F32)
                BR = SB2("a5_br", [128, DK], F32)
                BUVR = SB2("a5_buvr", [1, 1024], BF16) if side == "kv" else None
                rsb = {"QRF": SB2("a5_qrf", [128, 512], F32),
                       "T1": SB2("a5_t1", [128, 512], F32),
                       "T2": SB2("a5_t2", [128, 512], F32),
                       "C2B": SB2("a5_c2b", [128, 512], F32),
                       "S2B": SB2("a5_s2b", [128, 512], F32)}
                blk = sp.enter_context(nc.Block(no_gpsimd_drain=True))

                @blk.gpsimd
                def _(g, side=side):
                    if side == "q":
                        wdma.inc(g.dma_start(out=WD[:], in_=w["Wd_q_w"].rearrange("(k p) m -> p k m", p=128)), 16)
                        wdma.inc(g.dma_start(out=WU[:], in_=w["Wu_q_w"].rearrange("(k p) m -> p k m", p=128)), 16)
                        wdma.inc(g.dma_start(out=WR[:], in_=w["Wq_r_w"].rearrange("(k p) m -> p k m", p=128)), 16)
                        wdma.inc(g.dma_start(out=BD[:], in_=w["bdq"]), 16)
                        wdma.inc(g.dma_start(out=BU[:], in_=w["buq"]), 16)
                        wdma.inc(g.dma_start(out=BR[:], in_=w["bqr"]), 16)
                    else:
                        wdma.inc(g.dma_start(out=WD[:], in_=w["Wd_kv_w"].rearrange("(k p) m -> p k m", p=128)), 16)
                        wdma.inc(g.dma_start(out=WU[:], in_=w["Wu_k_w"].rearrange("(k p) m -> p k m", p=128)), 16)
                        wdma.inc(g.dma_start(out=WR[:], in_=w["Wk_r_w"].rearrange("(k p) m -> p k m", p=128)), 16)
                        wdma.inc(g.dma_start(out=WUV[:], in_=w["Wu_v_w"].rearrange("(k p) m -> p k m", p=128)), 16)
                        wdma.inc(g.dma_start(out=BD[:], in_=w["bdkv"]), 16)
                        wdma.inc(g.dma_start(out=BU[:], in_=w["buk"]), 16)
                        wdma.inc(g.dma_start(out=BR[:], in_=w["bkr"]), 16)
                        wdma.inc(g.dma_start(out=BUVR[:], in_=w["buv_row"]), 16)

                for tb in range(4):
                    tw = slice(tb * 512, (tb + 1) * 512)

                    @blk.sync
                    def _(s, tw=tw):
                        pe.wait(s)
                        act.wait(s)
                        dve.wait(s)
                        dma.inc(s.dma_start(out=cst["RC"][:],
                                            in_=src[:, :, tw].transpose([1, 0, 2])), 16)

                    @blk.tensor
                    def _(t):
                        dma.wait(t)
                        dve.wait(t)
                        wdma.wait(t)

                    @blk.scalar
                    def _(s):
                        dma.wait(s)
                        dve.wait(s)

                    emit_ln(ctx, blk, cst, cst["RC"][:], XHc[:], 512, ps, SQ, ROWS)

                    cdve_base = dve.v

                    @blk.tensor
                    def _(t, cdve_base=cdve_base):
                        dve.wait(t)
                        for m in range(4):
                            if m >= 2:
                                t.wait_ge(dve.h, cdve_base + m - 1)
                            for k in range(DK):
                                ins = t.matmul(CP[m % 2][:], lhsT=WD[:, k, m * 128:(m + 1) * 128],
                                               rhs=XHc[:, k, :], start=(k == 0), stop=(k == DK - 1))
                            pe.inc(ins)

                    @blk.vector
                    def _(v):
                        base = pe.v - 4
                        for m in range(4):
                            v.wait_ge(pe.h, base + m + 1)
                            dve.inc(v.tensor_scalar(out=CT[:, m, :], in0=CP[m % 2][:],
                                                    scalar1=BD[:, m:m + 1], scalar2=None, op0=OP.add))

                    if side == "q":
                        emit_rope_side(ctx, cst, blk, tb, QF, CT, WU, BU, WR, BR,
                                       CT, 4, rps, rsb, w, "rope")
                    else:
                        emit_rope_side(ctx, cst, blk, tb, KF, CT, WU, BU, WR, BR,
                                       XHc, DK, rps, rsb, w, "rope")
                        vdve_base = dve.v

                        @blk.tensor
                        def _(t, tb=tb, vdve_base=vdve_base):
                            dve.wait(t)
                            act.wait(t)
                            for qi in range(4):
                                for vh in range(2):
                                    j = qi * 2 + vh
                                    if j >= 2:
                                        t.wait_ge(dve.h, vdve_base + j - 1)
                                    for kk in range(4):
                                        ins = t.matmul(CP[j % 2][:],
                                                       lhsT=CT[:, kk, qi * 128:(qi + 1) * 128],
                                                       rhs=WUV[:, kk, vh * 512:(vh + 1) * 512],
                                                       start=(kk == 0), stop=False)
                                    ins = t.matmul(CP[j % 2][:], lhsT=cst["ones_row_bf"][:],
                                                   rhs=BUVR[:, vh * 512:(vh + 1) * 512],
                                                   start=False, stop=True)
                                    pe.inc(ins)

                        @blk.vector
                        def _(v, tb=tb):
                            base = pe.v - 8
                            for qi in range(4):
                                for vh in range(2):
                                    j = qi * 2 + vh
                                    v.wait_ge(pe.h, base + j + 1)
                                    dve.inc(v.tensor_copy(VT[:, tb * 4 + qi, vh * 512:(vh + 1) * 512],
                                                          CP[j % 2][:]))

        with contextlib.ExitStack() as sp:
            P = lambda b, sh, dt=F32: sp.enter_context(nc.psum_tensor(ctx.name(b), sh, dt))
            SB2 = lambda b, sh, dt: sp.enter_context(nc.sbuf_tensor(ctx.name(b), sh, dt))
            SPS = P("a5_sps", [128, 2048])
            TPPb = P("a5_tpp", [128, 1024], BF16)
            AVP = P("a5_avp", [64, 128])
            OPP = [P("a5_opp", [128, 512]) for _ in range(2)]
            WOUT = SB2("a5_wo", [128, DK, 1024], BF16)
            WOB = SB2("a5_wob", [128, DK], F32)
            PB = SB2("a5_pb", [128, T], BF16)
            PT = SB2("a5_pt", [128, 16, 128], BF16)
            COLS = SB2("a5_cols", [128, 4], F32)
            YSLAB = SB2("a5_yslab", [128, DK, 512], BF16)
            blk = sp.enter_context(nc.Block(no_gpsimd_drain=True))

            @blk.gpsimd
            def _(g):
                wdma.inc(g.dma_start(out=WOUT[:], in_=w["Wo_w"].rearrange("(k p) m -> p k m", p=128)), 16)
                wdma.inc(g.dma_start(out=WOB[:], in_=w["wob"]), 16)

            bufs = {"KF": KF, "VT": VT, "PB": PB, "PT": PT, "SPS": SPS,
                    "TPPb": TPPb, "AVP": AVP, "COLS": COLS, "QCB": None,
                    "YSLAB": YSLAB, "qoff": 0}
            scale = float(1.0 / np.sqrt(np.float32(1040.0)))

            for qc in range(16):
                tb = qc // 4
                bufs["qoff"] = (qc % 4) * 128
                bufs["QCB"] = QF[:, :, qc * 128:(qc + 1) * 128]

                @blk.tensor
                def _(t):
                    wdma.wait(t)

                emit_attn_qc(ctx, cst, blk, bufs, qc, T, scale, False)

                if qc % 4 == 3:
                    @blk.sync
                    def _(s, tb=tb):
                        st_sem.wait(s)
                        dma.inc(s.dma_start(out=cst["RC"][:],
                                            in_=src[:, :, tb * 512:(tb + 1) * 512].transpose([1, 0, 2])), 16)

                    odve_base = dve.v

                    @blk.tensor
                    def _(t, tb=tb, odve_base=odve_base):
                        dve.wait(t)
                        for mo in range(DK):
                            if mo >= 2:
                                t.wait_ge(dve.h, odve_base + mo - 1)
                            for k in range(DK):
                                ins = t.matmul(OPP[mo % 2][:], lhsT=WOUT[:, k, mo * 128:(mo + 1) * 128],
                                               rhs=YSLAB[:, k, :], start=(k == 0), stop=(k == DK - 1))
                            pe.inc(ins)

                    @blk.vector
                    def _(v, tb=tb):
                        dma.wait(v)
                        base = pe.v - DK
                        v.drain()
                        for mo in range(DK):
                            v.wait_ge(pe.h, base + mo + 1)
                            dve.inc(v.scalar_tensor_tensor(out=cst["RC"][:, mo, :], in0=OPP[mo % 2][:],
                                                           scalar=WOB[:, mo:mo + 1], in1=cst["RC"][:, mo, :],
                                                           op0=OP.add, op1=OP.add))

                    @blk.sync
                    def _(s, tb=tb):
                        dve.wait(s)
                        st_sem.inc(s.dma_start(out=dst[:, :, tb * 512:(tb + 1) * 512].transpose([1, 0, 2]),
                                               in_=cst["RC"][:]), 16)

            @blk.sync
            def _(s):
                st_sem.wait(s)


def emit_l0_attn_a(ctx, cst, src, dst, w):
    nc = ctx.nc
    Tc = 128
    NCH = T // Tc
    DI = 2048
    NDC = 16
    pe, act, dve = cst["pe"], cst["act"], cst["dve"]
    dma, wdma, st_sem = cst["dma"], cst["wdma"], cst["st"]

    with contextlib.ExitStack() as sl:
        SB = lambda b, sh, dt: sl.enter_context(nc.sbuf_tensor(ctx.name(b), sh, dt))
        WIN = SB("m_win", [128, DK, 4096], BF16)
        OUTP = SB("m_outp", [128, NDC, 1024], BF16)
        XPROJ = SB("m_xproj", [128, NDC, 96], BF16)
        DTW = SB("m_dtw", [65, DI], BF16)
        A_SB = SB("m_asb", [128, NDC, 16], F32)
        CONVW = SB("m_convw", [128, NDC, 4], F32)
        CONVB = SB("m_convb", [128, NDC], F32)
        DCOL = SB("m_dcol", [128, NDC], F32)
        XZC = SB("m_xzc", [128, NDC, Tc + 3], F32)
        ZBF = SB("m_zbf", [128, NDC, Tc], BF16)
        XC2 = SB("m_xc2", [128, NDC, Tc], F32)
        SILBF = SB("m_silbf", [128, NDC, Tc], BF16)
        DTBF = SB("m_dtbf", [65, Tc], BF16)
        BC32 = SB("m_bc32", [32, Tc], F32)
        BCFL = SB("m_bcfl", [1, 8 * Tc], F32)
        BBRD = SB("m_bbrd", [128, 8, Tc], F32)
        CBRD = SB("m_cbrd", [128, 8, Tc], F32)
        XB = SB("m_xb", [128, NDC, Tc], F32)
        DELTA = SB("m_delta", [128, NDC, Tc], F32)
        YBFC = SB("m_ybfc", [128, NDC, Tc], BF16)
        XHc = SB("m_xhc", [128, DK, Tc], BF16)
        SQ, ROWS = cst["SQ"], cst["ROWS"]
        MOC = cst["RC"][:, :, 384:384 + Tc]
        DUDC = cst["RC"][:, 0, 128:128 + Tc]
        SILF = cst["RC"][:, 0, 256:256 + Tc]
        SILZF = cst["RC"][:, 1, 128:128 + Tc]
        STATE = cst["RC"][:, 2, 128:384].rearrange("p (a b) -> p a b", b=16)
        XC2F = XC2[:].rearrange("p a b -> p (a b)")
        SCR1 = XC2F[:, 0:8 * Tc].rearrange("p (a b) -> p a b", b=Tc)
        SCR2 = XC2F[:, 8 * Tc:16 * Tc].rearrange("p (a b) -> p a b", b=Tc)
        SCR3 = XB[:].rearrange("p a b -> p (a b)")[:, 0:8 * Tc].rearrange("p (a b) -> p a b", b=Tc)

        with contextlib.ExitStack() as sp:
            P = lambda b, sh, dt=F32: sp.enter_context(nc.psum_tensor(ctx.name(b), sh, dt))
            ps = {"lnrow": P("m_lnrow", [1, 1024]),
                  "nmb": P("m_nmb", [128, 512]),
                  "rb": P("m_rb", [128, 512])}
            IP2 = [P("m_ip2", [128, 512]) for _ in range(2)]
            DBLP = P("m_dblp", [96, Tc])
            blk = sp.enter_context(nc.Block(no_gpsimd_drain=True))

            @blk.gpsimd
            def _(g):
                wdma.inc(g.dma_start(out=WIN[:], in_=w["in_proj_w"].rearrange("(k p) m -> p k m", p=128)), 16)
                wdma.inc(g.dma_start(out=OUTP[:], in_=w["out_proj_w"].rearrange("(k p) m -> p k m", p=128)), 16)
                wdma.inc(g.dma_start(out=XPROJ[:], in_=w["x_proj_w"].rearrange("(k p) m -> p k m", p=128)), 16)
                wdma.inc(g.dma_start(out=DTW[:], in_=w["dt_cat"]), 16)
                wdma.inc(g.dma_start(out=A_SB[:], in_=w["alog_fm"]), 16)
                wdma.inc(g.dma_start(out=CONVW[:], in_=w["conv_fm"]), 16)
                wdma.inc(g.dma_start(out=CONVB[:], in_=w["convb_fm"]), 16)
                wdma.inc(g.dma_start(out=DCOL[:], in_=w["d_fm"]), 16)
                cst["pool"].inc(g.memset(DTBF[64:65, :], 1.0))

            @blk.scalar
            def _(s):
                wdma.wait(s)
                act.inc(s.activation(out=A_SB[:], in_=A_SB[:], func=AF.Exp))

            @blk.vector
            def _(v):
                act.wait(v)
                dve.inc(v.tensor_scalar(out=A_SB[:], in0=A_SB[:], scalar1=-1.0,
                                        scalar2=None, op0=OP.mult))

            for direc in range(2):
                fwd = (direc == 0)
                xzo = 3 if fwd else 0

                @blk.vector
                def _(v, fwd=fwd):
                    dve.wait(v)
                    v.memset(STATE[:], 0.0)
                    if fwd:
                        ins = v.memset(XZC[:, :, 0:3], 0.0)
                    else:
                        ins = v.memset(XZC[:, :, Tc:Tc + 3], 0.0)
                    dve.inc(ins)

                chunks = range(NCH) if fwd else range(NCH - 1, -1, -1)
                for c in chunks:
                    tw = slice(c * Tc, (c + 1) * Tc)

                    @blk.sync
                    def _(s, tw=tw, c=c, fwd=fwd):
                        pe.wait(s)
                        act.wait(s)
                        dve.wait(s)
                        st_sem.wait(s)
                        dma.inc(s.dma_start(out=cst["RC"][:, :, 0:Tc],
                                            in_=src[:, :, tw].transpose([1, 0, 2])), 16)
                        if not fwd:
                            dma.inc(s.dma_start(out=MOC[:], in_=w["mo_s"][:, :, tw].transpose([1, 0, 2])), 16)

                    @blk.tensor
                    def _(t):
                        dma.wait(t)
                        dve.wait(t)

                    @blk.scalar
                    def _(s):
                        dma.wait(s)
                        dve.wait(s)

                    emit_ln(ctx, blk, cst, cst["RC"][:, :, 0:Tc], XHc[:], Tc, ps, SQ, ROWS)

                    dve_eb = dve.v
                    act_eb = act.v
                    cons = []
                    ndve = 0
                    nact = 0
                    for m in range(32):
                        if m < 16:
                            ndve += 1
                            cons.append(("d", dve_eb + ndve))
                        else:
                            nact += 1
                            cons.append(("a", act_eb + nact))

                    @blk.tensor
                    def _(t, cons=cons):
                        dve.wait(t)
                        act.wait(t)
                        for m in range(32):
                            if m >= 2:
                                kind, val = cons[m - 2]
                                t.wait_ge(dve.h if kind == "d" else act.h, val)
                            for k in range(DK):
                                ins = t.matmul(IP2[m % 2][:, 0:Tc], lhsT=WIN[:, k, m * 128:(m + 1) * 128],
                                               rhs=XHc[:, k, :], start=(k == 0), stop=(k == DK - 1))
                            pe.inc(ins)

                    g_pe_base = pe.v - 32

                    @blk.vector
                    def _(v, g_pe_base=g_pe_base):
                        for m in range(16):
                            v.wait_ge(pe.h, g_pe_base + m + 1)
                            dve.inc(v.tensor_copy(XZC[:, m, xzo:xzo + Tc], IP2[m % 2][:, 0:Tc]))

                    @blk.scalar
                    def _(s, g_pe_base=g_pe_base):
                        for m in range(16, 32):
                            s.wait_ge(pe.h, g_pe_base + m + 1)
                            act.inc(s.copy(out=ZBF[:, m - 16, :], in_=IP2[m % 2][:, 0:Tc]))

                    @blk.vector
                    def _(v, fwd=fwd):
                        dve.wait(v)
                        if fwd:
                            wins = [0, 1, 2, 3]
                            wts = [0, 1, 2, 3]
                        else:
                            wins = [3, 2, 1, 0]
                            wts = [0, 1, 2, 3]
                        for dc in range(NDC):
                            v.tensor_scalar(out=XC2[:, dc, :], in0=XZC[:, dc, wins[0]:wins[0] + Tc],
                                            scalar1=CONVW[:, dc, wts[0]:wts[0] + 1],
                                            scalar2=CONVB[:, dc:dc + 1], op0=OP.mult, op1=OP.add)
                        for tap in range(1, 4):
                            v.drain()
                            for dc in range(NDC):
                                ins = v.scalar_tensor_tensor(out=XC2[:, dc, :],
                                                             in0=XZC[:, dc, wins[tap]:wins[tap] + Tc],
                                                             scalar=CONVW[:, dc, wts[tap]:wts[tap] + 1],
                                                             in1=XC2[:, dc, :], op0=OP.mult, op1=OP.add)
                        v.drain()
                        if fwd:
                            ins = v.tensor_copy(XZC[:, :, 0:3], XZC[:, :, Tc:Tc + 3])
                        else:
                            ins = v.tensor_copy(XZC[:, :, Tc:Tc + 3], XZC[:, :, 0:3])
                        dve.inc(ins)

                    @blk.scalar
                    def _(s):
                        dve.wait(s)
                        act.inc(s.activation(out=SILBF[:], in_=XC2[:], func=AF.Silu))


                    @blk.tensor
                    def _(t):
                        dve.wait(t)
                        act.wait(t)
                        for k in range(NDC):
                            ins = t.matmul(DBLP[:], lhsT=XPROJ[:, k, :], rhs=SILBF[:, k, :],
                                           start=(k == 0), stop=(k == NDC - 1))
                        pe.inc(ins)

                    @blk.vector
                    def _(v):
                        pe.wait(v)
                        v.tensor_copy(DTBF[0:64, :], DBLP[0:64, :])
                        dve.inc(v.tensor_copy(BC32[:], DBLP[64:96, :]))

                    @blk.sync
                    def _(s):
                        dve.wait(s)
                        st_sem.inc(s.dma_start(out=w["bc_s"][:], in_=BC32[:]), 16)

                    dve_eb2 = dve.v

                    @blk.tensor
                    def _(t, dve_eb2=dve_eb2):
                        dve.wait(t)
                        cst["pool"].wait(t)
                        for dcq in range(NDC):
                            if dcq >= 2:
                                t.wait_ge(dve.h, dve_eb2 + dcq - 1)
                            ins = t.matmul(IP2[dcq % 2][:, 0:Tc], lhsT=DTW[:, dcq * 128:(dcq + 1) * 128],
                                           rhs=DTBF[:], start=True, stop=True)
                            pe.inc(ins)

                    @blk.vector
                    def _(v):
                        base = pe.v - NDC
                        for dcq in range(NDC):
                            v.wait_ge(pe.h, base + dcq + 1)
                            dve.inc(v.tensor_copy(XB[:, dcq, :], IP2[dcq % 2][:, 0:Tc]))

                    XBF = XB[:].rearrange("p a b -> p (a b)")
                    DLF = DELTA[:].rearrange("p a b -> p (a b)")
                    SQF = SQ[:].rearrange("p a b -> p (a b)")

                    @blk.scalar
                    def _(s, XBF=XBF, DLF=DLF, SQF=SQF):
                        dve.wait(s)
                        s.activation(out=SQF[:, 0:NDC * Tc], in_=XBF, func=AF.Abs)
                        s.drain()
                        s.activation(out=DLF, in_=SQF[:, 0:NDC * Tc], func=AF.Exp, scale=-1.0)
                        s.drain()
                        act.inc(s.activation(out=SQF[:, 0:NDC * Tc], in_=DLF, func=AF.Ln, bias=1.0))

                    @blk.vector
                    def _(v, XBF=XBF, DLF=DLF, SQF=SQF):
                        act.wait(v)
                        v.tensor_scalar(out=DLF, in0=XBF, scalar1=0.0, scalar2=None, op0=OP.max)
                        v.drain()
                        dve.inc(v.tensor_tensor(DLF, DLF, SQF[:, 0:NDC * Tc], OP.add))

                    for nh in range(2):
                        @blk.sync
                        def _(s, nh=nh):
                            pe.wait(s)
                            dve.wait(s)
                            st_sem.wait(s)
                            bcf = w["bc_s"].rearrange("(u a) b -> u (a b)", u=1)
                            dma.inc(s.dma_start(out=BCFL[:], in_=bcf[:, nh * 8 * Tc:(nh + 1) * 8 * Tc]), 16)

                        @blk.tensor
                        def _(t, nh=nh):
                            dma.wait(t)
                            dve.wait(t)
                            nseg = (8 * Tc) // 512
                            for g in range(nseg):
                                ins = t.matmul(IP2[g % 2][:], lhsT=cst["ones_row"][:],
                                               rhs=BCFL[:, g * 512:(g + 1) * 512], start=True, stop=True)
                                pe.inc(ins)

                        @blk.vector
                        def _(v, nh=nh):
                            nseg = (8 * Tc) // 512
                            base = pe.v - nseg
                            bf = BBRD[:].rearrange("p n t -> p (n t)")
                            for g in range(nseg):
                                v.wait_ge(pe.h, base + g + 1)
                                dve.inc(v.tensor_copy(bf[:, g * 512:(g + 1) * 512], IP2[g % 2][:]))

                        @blk.sync
                        def _(s, nh=nh):
                            dve.wait(s)
                            bcf = w["bc_s"].rearrange("(u a) b -> u (a b)", u=1)
                            dma.inc(s.dma_start(out=BCFL[:], in_=bcf[:, (2 + nh) * 8 * Tc:(3 + nh) * 8 * Tc]), 16)

                        @blk.tensor
                        def _(t, nh=nh):
                            dma.wait(t)
                            nseg = (8 * Tc) // 512
                            for g in range(nseg):
                                ins = t.matmul(IP2[g % 2][:], lhsT=cst["ones_row"][:],
                                               rhs=BCFL[:, g * 512:(g + 1) * 512], start=True, stop=True)
                                pe.inc(ins)

                        @blk.vector
                        def _(v, nh=nh):
                            nseg = (8 * Tc) // 512
                            base = pe.v - nseg
                            cf = CBRD[:].rearrange("p n t -> p (n t)")
                            for g in range(nseg):
                                v.wait_ge(pe.h, base + g + 1)
                                dve.inc(v.tensor_copy(cf[:, g * 512:(g + 1) * 512], IP2[g % 2][:]))

                        pool = cst["pool"]
                        pre_dve = dve.v
                        pre_pool = pool.v
                        scan_done = []
                        y_done = []
                        arg_done = []
                        exp_done = []
                        tree_done = []
                        for dc in range(NDC):
                            u = dc

                            @blk.gpsimd
                            def _(g, dc=dc, nh=nh, u=u):
                                g.wait_ge(dve.h, scan_done[u - 1] if u > 0 else pre_dve)
                                d_b = DELTA[:, dc, :].unsqueeze(1).broadcast_to([128, 8, Tc])
                                a_b = A_SB[:, dc, nh * 8:(nh + 1) * 8].unsqueeze(2).broadcast_to([128, 8, Tc])
                                pool.inc(g.tensor_tensor(SCR1[:], d_b, a_b, OP.mult))

                            arg_done.append(pool.v)

                            @blk.scalar
                            def _(s, dc=dc, nh=nh, u=u):
                                s.wait_ge(dve.h, y_done[u - 1] if u > 0 else pre_dve)
                                s.activation(out=SILF, in_=SILBF[:, dc, :], func=AF.Copy)
                                s.drain()
                                s.activation(out=SILZF, in_=ZBF[:, dc, :], func=AF.Silu)
                                s.wait_ge(pool.h, arg_done[u])
                                act.inc(s.activation(out=SCR2[:], in_=SCR1[:], func=AF.Exp))

                            exp_done.append(act.v)

                            @blk.vector
                            def _(v, dc=dc, nh=nh, fwd=fwd, u=u):
                                v.wait_ge(act.h, exp_done[u])
                                v.wait_ge(pool.h, tree_done[u - 1] if u > 0 else pre_pool)
                                v.tensor_tensor(DUDC, DELTA[:, dc, :], SILF, OP.mult)
                                v.drain()
                                du_b = DUDC.unsqueeze(1).broadcast_to([128, 8, Tc])
                                v.tensor_tensor(SCR1[:], du_b, BBRD[:], OP.mult)
                                v.drain()
                                for n in range(8):
                                    if fwd:
                                        v.tensor_tensor_scan(SCR3[:, n, :], SCR2[:, n, :], SCR1[:, n, :],
                                                             STATE[:, dc, nh * 8 + n:nh * 8 + n + 1], OP.mult, OP.add)
                                    else:
                                        v.tensor_tensor_scan(SCR3[:, n, ::-1], SCR2[:, n, ::-1], SCR1[:, n, ::-1],
                                                             STATE[:, dc, nh * 8 + n:nh * 8 + n + 1], OP.mult, OP.add)
                                v.drain()
                                if fwd:
                                    ins = v.tensor_copy(STATE[:, dc, nh * 8:(nh + 1) * 8], SCR3[:, :, Tc - 1])
                                else:
                                    ins = v.tensor_copy(STATE[:, dc, nh * 8:(nh + 1) * 8], SCR3[:, :, 0])
                                dve.inc(ins)

                            scan_done.append(dve.v)

                            @blk.gpsimd
                            def _(g, dc=dc, nh=nh, u=u):
                                g.wait_ge(dve.h, scan_done[u])
                                g.tensor_tensor(SCR2[:], SCR3[:], CBRD[:], OP.mult)
                                g.drain()
                                g.tensor_tensor(SCR2[:, 0:4, :], SCR2[:, 0:4, :], SCR2[:, 4:8, :], OP.add)
                                g.drain()
                                g.tensor_tensor(SCR2[:, 0:2, :], SCR2[:, 0:2, :], SCR2[:, 2:4, :], OP.add)
                                g.drain()
                                pool.inc(g.tensor_tensor(SCR2[:, 0:1, :], SCR2[:, 0:1, :], SCR2[:, 1:2, :], OP.add))

                            tree_done.append(pool.v)

                            @blk.vector
                            def _(v, dc=dc, nh=nh, u=u):
                                v.wait_ge(pool.h, tree_done[u])
                                if nh == 0:
                                    dve.inc(v.tensor_tensor(YBFC[:, dc, :], SCR2[:, 0, :], SILZF, OP.mult))
                                else:
                                    v.scalar_tensor_tensor(out=DUDC, in0=SILF, scalar=DCOL[:, dc:dc + 1],
                                                           in1=SCR2[:, 0, :], op0=OP.mult, op1=OP.add)
                                    v.drain()
                                    dve.inc(v.tensor_tensor(YBFC[:, dc, :], DUDC, SILZF, OP.mult))

                            y_done.append(dve.v)

                        dve_eb3 = dve.v

                        @blk.tensor
                        def _(t, dve_eb3=dve_eb3):
                            dve.wait(t)
                            for mo in range(DK):
                                if mo >= 2:
                                    t.wait_ge(dve.h, dve_eb3 + mo - 1)
                                for k in range(NDC):
                                    ins = t.matmul(IP2[mo % 2][:, 0:Tc], lhsT=OUTP[:, k, mo * 128:(mo + 1) * 128],
                                                   rhs=YBFC[:, k, :], start=(k == 0), stop=(k == NDC - 1))
                                pe.inc(ins)

                        @blk.vector
                        def _(v, fwd=fwd, nh=nh):
                            base = pe.v - DK
                            dma.wait(v)
                            st_sem.wait(v)
                            for mo in range(DK):
                                v.wait_ge(pe.h, base + mo + 1)
                                if fwd and nh == 0:
                                    dve.inc(v.tensor_copy(MOC[:, mo, :], IP2[mo % 2][:, 0:Tc]))
                                else:
                                    dve.inc(v.tensor_tensor(MOC[:, mo, :], MOC[:, mo, :], IP2[mo % 2][:, 0:Tc], OP.add))

                    @blk.sync
                    def _(s, tw=tw):
                        dve.wait(s)
                        st_sem.inc(s.dma_start(out=w["mo_s"][:, :, tw].transpose([1, 0, 2]), in_=MOC[:]), 16)

            @blk.sync
            def _(s):
                st_sem.wait(s)

    with contextlib.ExitStack() as sl:
        SB = lambda b, sh, dt: sl.enter_context(nc.sbuf_tensor(ctx.name(b), sh, dt))
        FFW1 = SB("m_ffw1", [128, DK, 4096], BF16)
        FFW2 = SB("m_ffw2", [128, 32, 1024], BF16)
        FFB1 = SB("m_ffb1", [128, 32], F32)
        FFB2 = SB("m_ffb2", [128, DK], F32)
        LNG = SB("m_lng", [128, DK], F32)
        LNB = SB("m_lnb", [128, DK], F32)
        MOC = SB("m_moc2", [128, DK, 256], F32)
        XHF = SB("m_xhf", [128, DK, 256], BF16)
        H1F = SB("m_h1f", [128, 32, 256], BF16)
        SQ, ROWS = cst["SQ"], cst["ROWS"]

        with contextlib.ExitStack() as sp:
            P = lambda b, sh, dt=F32: sp.enter_context(nc.psum_tensor(ctx.name(b), sh, dt))
            ps = {"lnrow": P("f_lnrow", [1, 1024]),
                  "nmb": P("f_nmb", [128, 512]),
                  "rb": P("f_rb", [128, 512])}
            GP = [P("f_gp", [128, 512]) for _ in range(2)]
            blk = sp.enter_context(nc.Block(no_gpsimd_drain=True))

            @blk.gpsimd
            def _(g):
                wdma.inc(g.dma_start(out=FFW1[:], in_=w["ff_w1"].rearrange("(k p) m -> p k m", p=128)), 16)
                wdma.inc(g.dma_start(out=FFW2[:], in_=w["ff_w2"].rearrange("(k p) m -> p k m", p=128)), 16)
                wdma.inc(g.dma_start(out=FFB1[:], in_=w["ffb1_fm"]), 16)
                wdma.inc(g.dma_start(out=FFB2[:], in_=w["ffb2_fm"]), 16)
                wdma.inc(g.dma_start(out=LNG[:], in_=w["lng_fm"]), 16)
                wdma.inc(g.dma_start(out=LNB[:], in_=w["lnb_fm"]), 16)

            for c in range(8):
                tw = slice(c * 256, (c + 1) * 256)

                @blk.sync
                def _(s, tw=tw):
                    pe.wait(s)
                    act.wait(s)
                    dve.wait(s)
                    st_sem.wait(s)
                    dma.inc(s.dma_start(out=MOC[:], in_=w["mo_s"][:, :, tw].transpose([1, 0, 2])), 16)
                    dma.inc(s.dma_start(out=cst["RC"][:, :, 0:256],
                                        in_=src[:, :, tw].transpose([1, 0, 2])), 16)

                @blk.tensor
                def _(t):
                    dma.wait(t)
                    dve.wait(t)
                    wdma.wait(t)

                @blk.scalar
                def _(s):
                    dma.wait(s)
                    dve.wait(s)

                emit_ln(ctx, blk, cst, MOC[:], XHF[:], 256, ps, SQ, ROWS,
                        gcol=LNG, bcol=LNB)

                act_b1 = act.v

                @blk.tensor
                def _(t, act_b1=act_b1):
                    dve.wait(t)
                    act.wait(t)
                    for m in range(32):
                        if m >= 2:
                            t.wait_ge(act.h, act_b1 + m - 1)
                        for k in range(DK):
                            ins = t.matmul(GP[m % 2][:, 0:256], lhsT=FFW1[:, k, m * 128:(m + 1) * 128],
                                           rhs=XHF[:, k, :], start=(k == 0), stop=(k == DK - 1))
                        pe.inc(ins)

                @blk.scalar
                def _(s):
                    base = pe.v - 32
                    for m in range(32):
                        s.wait_ge(pe.h, base + m + 1)
                        act.inc(s.activation(out=H1F[:, m, :], in_=GP[m % 2][:, 0:256],
                                             func=AF.Gelu, bias=FFB1[:, m:m + 1]))

                dve_b2 = dve.v

                @blk.tensor
                def _(t, dve_b2=dve_b2):
                    act.wait(t)
                    dve.wait(t)
                    for mo in range(DK):
                        if mo >= 2:
                            t.wait_ge(dve.h, dve_b2 + mo - 1)
                        for k in range(32):
                            ins = t.matmul(GP[mo % 2][:, 0:256], lhsT=FFW2[:, k, mo * 128:(mo + 1) * 128],
                                           rhs=H1F[:, k, :], start=(k == 0), stop=(k == 31))
                        pe.inc(ins)

                @blk.vector
                def _(v):
                    base = pe.v - DK
                    for mo in range(DK):
                        v.wait_ge(pe.h, base + mo + 1)
                        dve.inc(v.scalar_tensor_tensor(out=cst["RC"][:, mo, 0:256], in0=GP[mo % 2][:, 0:256],
                                                       scalar=FFB2[:, mo:mo + 1], in1=cst["RC"][:, mo, 0:256],
                                                       op0=OP.add, op1=OP.add))

                @blk.sync
                def _(s, tw=tw):
                    dve.wait(s)
                    st_sem.inc(s.dma_start(out=dst[:, :, tw].transpose([1, 0, 2]),
                                           in_=cst["RC"][:, :, 0:256]), 16)

            @blk.sync
            def _(s):
                st_sem.wait(s)


S7_PHASES = "ABCDE"
S7_D_LEVEL = 6
S7_SC_LEVEL = 7


def emit_l1_attn_b(ctx, cst, src, dst, w):
    nc = ctx.nc
    H = 16
    HD = 64
    Tc = 128
    NCH = T // Tc
    SC = 64
    NSC = Tc // SC
    pe, act, dve = cst["pe"], cst["act"], cst["dve"]
    dma, wdma, st_sem = cst["dma"], cst["wdma"], cst["st"]
    SQ, ROWS = cst["SQ"], cst["ROWS"]

    with contextlib.ExitStack() as sl:
        SB = lambda b, sh, dt: sl.enter_context(nc.sbuf_tensor(ctx.name(b), sh, dt))
        DBALL = SB("b7_dball", [64, T], F32)
        P20N = SB("b7_p20n", [128, DK, T], BF16)

        if "A" in S7_PHASES:
         with contextlib.ExitStack() as sp:
            P = lambda b, sh, dt=F32: sp.enter_context(nc.psum_tensor(ctx.name(b), sh, dt))
            SB2 = lambda b, sh, dt: sp.enter_context(nc.sbuf_tensor(ctx.name(b), sh, dt))
            CP = [P("b7_cp", [128, 512]) for _ in range(2)]
            ps = {"lnrow": P("b7_lnrow", [1, 1024]), "nmb": CP[0][:, 0:Tc], "rb": CP[1][:, 0:Tc]}
            DBP = P("b7_dbp", [64, Tc])
            CTRLW = SB2("b7_ctrlw", [128, DK, 4160], BF16)
            CBQ = SB2("b7_cbq", [128, 24], F32)
            CBG = SB2("b7_cbg", [128, 8], F32)
            CBDB = SB2("b7_cbdb", [64, 1], F32)
            QCONV = SB2("b7_qconv", [128, 24, 4], F32)
            XHc = SB2("b7_xhc", [128, DK, Tc], BF16)
            QKV = SB2("b7_qkv", [128, 24, Tc + 3], F32)
            QKVC = SB2("b7_qkvc", [128, 24, Tc], F32)
            SGC = SB2("b7_sgc", [128, 8, Tc], BF16)
            blk = sp.enter_context(nc.Block(no_gpsimd_drain=True))

            @blk.gpsimd
            def _(g):
                wdma.inc(g.dma_start(out=CTRLW[:], in_=w["ctrlw_re"].rearrange("(k p) m -> p k m", p=128)), 16)
                wdma.inc(g.dma_start(out=CBQ[:], in_=w["cb_qkv"]), 16)
                wdma.inc(g.dma_start(out=CBG[:], in_=w["cb_gate"]), 16)
                wdma.inc(g.dma_start(out=CBDB[:], in_=w["cb_db"]), 16)
                wdma.inc(g.dma_start(out=QCONV[:], in_=w["conv3_fm"]), 16)

            TAIL = SB2("b7_tail", [128, 24, 3], F32)

            @blk.vector
            def _(v):
                dve.inc(v.memset(TAIL[:], 0.0))

            for c in range(NCH):
                tw = slice(c * Tc, (c + 1) * Tc)

                @blk.sync
                def _(s, tw=tw):
                    pe.wait(s)
                    act.wait(s)
                    dve.wait(s)
                    st_sem.wait(s)
                    dma.inc(s.dma_start(out=cst["RC"][:, :, 0:Tc],
                                        in_=src[:, :, tw].transpose([1, 0, 2])), 16)

                @blk.tensor
                def _(t):
                    dma.wait(t)
                    dve.wait(t)
                    wdma.wait(t)

                @blk.scalar
                def _(s):
                    dma.wait(s)
                    dve.wait(s)

                emit_ln(ctx, blk, cst, cst["RC"][:, :, 0:Tc], XHc[:], Tc, ps, SQ, ROWS)

                @blk.vector
                def _(v):
                    dve.wait(v)
                    act.wait(v)
                    dve.inc(v.tensor_copy(QKV[:, :, 0:3], TAIL[:]))

                dve_eb = dve.v
                act_eb = act.v
                cons = []
                nd = 0
                na = 0
                for m in range(33):
                    if m < 24 or m == 32:
                        nd += 1
                        cons.append(("d", dve_eb + nd))
                    else:
                        na += 1
                        cons.append(("a", act_eb + na))

                @blk.tensor
                def _(t, cons=cons):
                    dve.wait(t)
                    act.wait(t)
                    for m in range(33):
                        if m >= 2:
                            kind, val = cons[m - 2]
                            t.wait_ge(dve.h if kind == "d" else act.h, val)
                        dstp = DBP[:] if m == 32 else CP[m % 2][:, 0:Tc]
                        wslice = CTRLW[:, :, m * 128:m * 128 + 64] if m == 32 else CTRLW[:, :, m * 128:(m + 1) * 128]
                        for k in range(DK):
                            ins = t.matmul(dstp, lhsT=wslice[:, k, :], rhs=XHc[:, k, :],
                                           start=(k == 0), stop=(k == DK - 1))
                        pe.inc(ins)

                g_pe = pe.v - 33

                @blk.vector
                def _(v, g_pe=g_pe):
                    for m in range(24):
                        v.wait_ge(pe.h, g_pe + m + 1)
                        dve.inc(v.tensor_scalar(out=QKV[:, m, 3:3 + Tc], in0=CP[m % 2][:, 0:Tc],
                                                scalar1=CBQ[:, m:m + 1], scalar2=None, op0=OP.add))
                    v.wait_ge(pe.h, g_pe + 33)
                    dve.inc(v.tensor_scalar(out=DBALL[:, c * Tc:(c + 1) * Tc], in0=DBP[:],
                                            scalar1=CBDB[:], scalar2=None, op0=OP.add))

                @blk.scalar
                def _(s, g_pe=g_pe):
                    for m in range(24, 32):
                        s.wait_ge(pe.h, g_pe + m + 1)
                        act.inc(s.activation(out=SGC[:, m - 24, :], in_=CP[m % 2][:, 0:Tc],
                                             func=AF.Sigmoid, bias=CBG[:, m - 24:m - 24 + 1]))

                @blk.vector
                def _(v):
                    dve.wait(v)
                    for m in range(24):
                        v.tensor_scalar(out=QKVC[:, m, :], in0=QKV[:, m, 0:Tc],
                                        scalar1=QCONV[:, m, 0:1], scalar2=None, op0=OP.mult)
                    for tap in range(1, 4):
                        v.drain()
                        for m in range(24):
                            ins = v.scalar_tensor_tensor(out=QKVC[:, m, :],
                                                         in0=QKV[:, m, tap:tap + Tc],
                                                         scalar=QCONV[:, m, tap:tap + 1],
                                                         in1=QKVC[:, m, :], op0=OP.mult, op1=OP.add)
                    v.drain()
                    dve.inc(v.tensor_copy(TAIL[:], QKV[:, :, Tc:Tc + 3]))

                @blk.scalar
                def _(s):
                    dve.wait(s)
                    act.inc(s.activation(out=QKV[:, :, 3:3 + Tc], in_=QKVC[:], func=AF.Silu))

                @blk.sync
                def _(s, tw=tw, c=c):
                    act.wait(s)
                    st_sem.inc(s.dma_start(out=w["qkvc_s"][:, :, tw].transpose([1, 0, 2]),
                                           in_=QKV[:, :, 3:3 + Tc]), 16)
                    st_sem.inc(s.dma_start(out=w["sg_s"][:, :, tw].transpose([1, 0, 2]), in_=SGC[:]), 16)

            @blk.sync
            def _(s):
                st_sem.wait(s)

        if "B" in S7_PHASES:
         with contextlib.ExitStack() as spb:
            P20 = spb.enter_context(nc.sbuf_tensor(ctx.name("b7_p20"), [128, DK, T], F32))
            with contextlib.ExitStack() as sp:
                P = lambda b, sh, dt=F32: sp.enter_context(nc.psum_tensor(ctx.name(b), sh, dt))
                SB2 = lambda b, sh, dt: sp.enter_context(nc.sbuf_tensor(ctx.name(b), sh, dt))
                ps = {"lnrow": P("b7_lnrow2", [1, 1024]), "nmb": P("b7_nmb2", [128, 512]),
                      "rb": P("b7_rb2", [128, 512])}
                SWP = P("b7_swp", [128, 2048])
                AROW = SB2("b7_arow", [128, 8], F32)
                AR2 = SB2("b7_ar2", [128, 8], F32)
                AI2 = SB2("b7_ai2", [128, 8], F32)
                ARP = SB2("b7_arp", [128, 11, 8], F32)
                AIP = SB2("b7_aip", [128, 11, 8], F32)
                SBB = SB2("b7_sbb", [128, 2048], F32)
                blk = sp.enter_context(nc.Block(no_gpsimd_drain=True))

                @blk.gpsimd
                def _(g):
                    wdma.inc(g.dma_start(out=AROW[:], in_=w["dl2_fm"]), 16)
                    wdma.inc(g.dma_start(out=AR2[:], in_=w["th2_fm"]), 16)
                    wdma.inc(g.dma_start(out=AI2[:], in_=w["nsign_fm"]), 16)

                @blk.scalar
                def _(s):
                    wdma.wait(s)
                    s.activation(out=AROW[:], in_=AROW[:], func=AF.Exp)
                    s.drain()
                    s.activation(out=AROW[:], in_=AROW[:], func=AF.Exp, scale=-1.0)
                    s.drain()
                    act.inc(s.activation(out=SQ[:, 1, 0:8], in_=AR2[:], func=AF.Sin, scale=0.5))

                @blk.vector
                def _(v):
                    act.wait(v)
                    dve.inc(v.tensor_scalar(out=SQ[:, 2, 0:8], in0=AR2[:], scalar1=0.5,
                                            scalar2=-float(np.pi) / 2.0, op0=OP.mult, op1=OP.add))

                @blk.scalar
                def _(s):
                    dve.wait(s)
                    act.inc(s.activation(out=SQ[:, 0, 0:8], in_=SQ[:, 2, 0:8], func=AF.Sin))

                @blk.vector
                def _(v):
                    act.wait(v)
                    sh = SQ[:, 0, 0:8]
                    ch = SQ[:, 1, 0:8]
                    v.tensor_tensor(SQ[:, 2, 0:8], sh, ch, OP.mult)
                    v.tensor_tensor(SQ[:, 3, 0:8], sh, sh, OP.mult)
                    v.drain()
                    v.tensor_scalar(out=SQ[:, 2, 0:8], in0=SQ[:, 2, 0:8], scalar1=2.0,
                                    scalar2=None, op0=OP.mult)
                    v.tensor_scalar(out=SQ[:, 3, 0:8], in0=SQ[:, 3, 0:8], scalar1=-2.0,
                                    scalar2=1.0, op0=OP.mult, op1=OP.add)
                    v.drain()
                    v.scalar_tensor_tensor(out=AR2[:], in0=AROW[:], scalar=-1.0, in1=SQ[:, 3, 0:8],
                                           op0=OP.mult, op1=OP.mult)
                    v.tensor_tensor(SQ[:, 2, 0:8], AROW[:], SQ[:, 2, 0:8], OP.mult)
                    v.drain()
                    v.tensor_tensor(AI2[:], AI2[:], SQ[:, 2, 0:8], OP.mult)
                    v.drain()
                    v.tensor_copy(ARP[:, 0, :], AR2[:])
                    v.tensor_copy(AIP[:, 0, :], AI2[:])
                    for kk in range(1, 11):
                        v.drain()
                        v.tensor_tensor(SQ[:, 4, 0:8], ARP[:, kk - 1, :], ARP[:, kk - 1, :], OP.mult)
                        v.tensor_tensor(SQ[:, 5, 0:8], AIP[:, kk - 1, :], AIP[:, kk - 1, :], OP.mult)
                        v.tensor_tensor(SQ[:, 6, 0:8], ARP[:, kk - 1, :], AIP[:, kk - 1, :], OP.mult)
                        v.drain()
                        v.tensor_tensor(ARP[:, kk, :], SQ[:, 4, 0:8], SQ[:, 5, 0:8], OP.subtract)
                        v.tensor_scalar(out=AIP[:, kk, :], in0=SQ[:, 6, 0:8], scalar1=2.0,
                                        scalar2=None, op0=OP.mult)
                    v.drain()
                    dve.inc(v.tensor_copy(SQ[:, 7, 0:8], ARP[:, 10, :]))

                for tb in range(4):
                    tw = slice(tb * 512, (tb + 1) * 512)

                    @blk.sync
                    def _(s, tw=tw):
                        pe.wait(s)
                        act.wait(s)
                        dve.wait(s)
                        dma.inc(s.dma_start(out=cst["RC"][:], in_=src[:, :, tw].transpose([1, 0, 2])), 16)

                    @blk.tensor
                    def _(t):
                        dma.wait(t)
                        dve.wait(t)

                    @blk.scalar
                    def _(s):
                        dma.wait(s)
                        dve.wait(s)

                    emit_ln(ctx, blk, cst, cst["RC"][:], P20[:, :, tw], 512, ps, SQ, ROWS)

                for dk in range(DK):
                    for p in range(11):
                        delta = 1 << p
                        srcb = P20[:, dk, :] if p % 2 == 0 else SBB[:]
                        dstb = SBB[:] if p % 2 == 0 else P20[:, dk, :]

                        @blk.tensor
                        def _(t, srcb=srcb, p=p, dk=dk):
                            dve.wait(t)
                            for g in range(4):
                                ins = t.matmul(SWP[:, g * 512:(g + 1) * 512], lhsT=cst["perm_f32"][:],
                                               rhs=srcb[:, g * 512:(g + 1) * 512], start=True, stop=True)
                            pe.inc(ins)

                        @blk.vector
                        def _(v, srcb=srcb, dstb=dstb, delta=delta, dk=dk, p=p):
                            pe.wait(v)
                            v.drain()
                            v.tensor_copy(dstb[:, 0:delta], srcb[:, 0:delta])
                            v.drain()
                            v.scalar_tensor_tensor(out=dstb[:, delta:T], in0=srcb[:, 0:T - delta],
                                                   scalar=ARP[:, p, dk:dk + 1], in1=srcb[:, delta:T],
                                                   op0=OP.mult, op1=OP.add)
                            v.drain()
                            dve.inc(v.scalar_tensor_tensor(out=dstb[:, delta:T], in0=SWP[:, 0:T - delta],
                                                           scalar=AIP[:, p, dk:dk + 1], in1=dstb[:, delta:T],
                                                           op0=OP.mult, op1=OP.add))

                    @blk.vector
                    def _(v, dk=dk):
                        v.drain()
                        dve.inc(v.tensor_copy(P20[:, dk, :], SBB[:]))

            with contextlib.ExitStack() as sp:
                P = lambda b, sh, dt=F32: sp.enter_context(nc.psum_tensor(ctx.name(b), sh, dt))
                RRP = P("b7_rrp", [1, 1024])
                RBB = P("b7_rbb", [128, 512])
                VNC = sp.enter_context(nc.sbuf_tensor(ctx.name("b7_vnc"), [128, 8], F32))
                blk = sp.enter_context(nc.Block(no_gpsimd_drain=True))

                @blk.gpsimd
                def _(g):
                    wdma.inc(g.dma_start(out=VNC[:], in_=w["vnorm_fm"]), 16)

                for tb in range(4):
                    tw = slice(tb * 512, (tb + 1) * 512)

                    @blk.scalar
                    def _(s, tw=tw):
                        dve.wait(s)
                        act.inc(s.activation(out=SQ[:], in_=P20[:, :, tw], func=AF.Square))

                    @blk.tensor
                    def _(t, tw=tw):
                        act.wait(t)
                        dve.wait(t)
                        for k in range(DK):
                            ins = t.matmul(RRP[:, 0:512], lhsT=cst["ones_col"][:], rhs=SQ[:, k, :],
                                           start=(k == 0), stop=(k == DK - 1))
                        pe.inc(ins)

                    @blk.vector
                    def _(v):
                        pe.wait(v)
                        v.tensor_scalar(out=ROWS[:, 0:512], in0=RRP[:, 0:512], scalar1=1.0 / D,
                                        scalar2=1e-6, op0=OP.mult, op1=OP.add)
                        v.drain()
                        dve.inc(v.reciprocal(ROWS[:, 512:1024], ROWS[:, 0:512]))

                    @blk.scalar
                    def _(s):
                        dve.wait(s)
                        act.inc(s.activation(out=ROWS[:, 0:512], in_=ROWS[:, 512:1024], func=AF.Sqrt))

                    @blk.tensor
                    def _(t):
                        act.wait(t)
                        pe.inc(t.matmul(RBB[:], lhsT=cst["ones_row"][:], rhs=ROWS[:, 0:512],
                                        start=True, stop=True))

                    @blk.vector
                    def _(v, tw=tw):
                        pe.wait(v)
                        wdma.wait(v)
                        for k in range(DK):
                            v.tensor_tensor(SQ[:, k, :], P20[:, k, tw], RBB[:], OP.mult)
                        v.drain()
                        for k in range(DK):
                            ins = v.tensor_scalar(out=P20N[:, k, tw], in0=SQ[:, k, :],
                                                  scalar1=VNC[:, k:k + 1], scalar2=None, op0=OP.mult)
                        dve.inc(ins)

        QHB = SB("b7_qhb", [128, DK, T], BF16)
        KHB = SB("b7_khb", [128, DK, T], BF16)
        if "C" in S7_PHASES:
         with contextlib.ExitStack() as sp:
            P = lambda b, sh, dt=F32: sp.enter_context(nc.psum_tensor(ctx.name(b), sh, dt))
            SB2 = lambda b, sh, dt: sp.enter_context(nc.sbuf_tensor(ctx.name(b), sh, dt))
            PQ = [P("b7_pq", [128, 512]) for _ in range(2)]
            SSQP = P("b7_ssqp", [2, 1024])
            RNBP = P("b7_rnbp", [128, 1024])
            P20CW = SB2("b7_p20cw", [128, DK, 3072], BF16)
            RAMP = SB2("b7_ramp", [128, 8], F32)
            QKVC = SB2("b7_qkvc2", [128, 8, Tc], BF16)
            QQ = SB2("b7_qq", [128, 8, Tc], F32)
            VBC = SB2("b7_vbc", [128, 8, Tc], BF16)
            blk = sp.enter_context(nc.Block(no_gpsimd_drain=True))

            @blk.gpsimd
            def _(g):
                wdma.inc(g.dma_start(out=P20CW[:], in_=w["p20c_W"].rearrange("(k p) m -> p k m", p=128)), 16)
                wdma.inc(g.dma_start(out=RAMP[:], in_=w["rampl_fm"]), 16)

            @blk.scalar
            def _(s):
                wdma.wait(s)
                act.inc(s.activation(out=RAMP[:], in_=RAMP[:], func=AF.Sigmoid))

            for c in range(NCH):
                tw = slice(c * Tc, (c + 1) * Tc)

                for grp in range(3):
                    mof = grp * 8

                    @blk.gpsimd
                    def _(g, tw=tw, mof=mof):
                        pe.wait(g)
                        act.wait(g)
                        dve.wait(g)
                        st_sem.wait(g)
                        wdma.inc(g.dma_start(out=QKVC[:],
                                             in_=w["qkvc_s"][mof:mof + 8, :, tw].transpose([1, 0, 2])), 16)

                    dve_eb = dve.v

                    @blk.tensor
                    def _(t, tw=tw, mof=mof, dve_eb=dve_eb):
                        dma.wait(t)
                        wdma.wait(t)
                        dve.wait(t)
                        act.wait(t)
                        for m in range(8):
                            if m >= 2:
                                t.wait_ge(dve.h, dve_eb + m - 1)
                            for k in range(DK):
                                ins = t.matmul(PQ[m % 2][:, 0:Tc],
                                               lhsT=P20CW[:, k, (mof + m) * 128:(mof + m + 1) * 128],
                                               rhs=P20N[:, k, tw], start=(k == 0), stop=(k == DK - 1))
                            pe.inc(ins)

                    @blk.vector
                    def _(v, mof=mof):
                        base = pe.v - 8
                        for m in range(8):
                            v.wait_ge(pe.h, base + m + 1)
                            dve.inc(v.scalar_tensor_tensor(out=QQ[:, m, :], in0=PQ[m % 2][:, 0:Tc],
                                                           scalar=RAMP[:, m:m + 1],
                                                           in1=QKVC[:, m, :], op0=OP.mult, op1=OP.add))

                    if grp < 2:
                        @blk.scalar
                        def _(s):
                            dve.wait(s)
                            act.inc(s.activation(out=SQ[:].rearrange("p a b -> p (a b)")[:, 0:8 * Tc],
                                                 in_=QQ[:].rearrange("p a b -> p (a b)"),
                                                 func=AF.Square))

                        @blk.tensor
                        def _(t):
                            act.wait(t)
                            dve.wait(t)
                            sqf = SQ[:].rearrange("p a b -> p (a b)")
                            t.matmul(SSQP[:, 0:512], lhsT=cst["selh"][:], rhs=sqf[:, 0:512],
                                     start=True, stop=True)
                            pe.inc(t.matmul(SSQP[:, 512:1024], lhsT=cst["selh"][:], rhs=sqf[:, 512:1024],
                                            start=True, stop=True))

                        @blk.vector
                        def _(v):
                            pe.wait(v)
                            sq2a = SQ[0:2, 0:2, :].rearrange("p a b -> p (a b)")
                            sq2b = SQ[0:2, 2:4, :].rearrange("p a b -> p (a b)")
                            v.tensor_scalar(out=sq2a, in0=SSQP[:], scalar1=1e-6,
                                            scalar2=None, op0=OP.add)
                            v.drain()
                            dve.inc(v.reciprocal(sq2b, sq2a))

                        @blk.scalar
                        def _(s):
                            dve.wait(s)
                            act.inc(s.activation(out=SQ[0:2, 0:2, :].rearrange("p a b -> p (a b)"),
                                                 in_=SQ[0:2, 2:4, :].rearrange("p a b -> p (a b)"),
                                                 func=AF.Sqrt))

                        @blk.tensor
                        def _(t):
                            act.wait(t)
                            sq2a = SQ[0:2, 0:2, :].rearrange("p a b -> p (a b)")
                            t.matmul(RNBP[:, 0:512], lhsT=cst["selb2"][:], rhs=sq2a[:, 0:512],
                                     start=True, stop=True)
                            pe.inc(t.matmul(RNBP[:, 512:1024], lhsT=cst["selb2"][:], rhs=sq2a[:, 512:1024],
                                            start=True, stop=True))

                        @blk.vector
                        def _(v, grp=grp, tw=tw):
                            pe.wait(v)
                            dst_b = QHB if grp == 0 else KHB
                            rnb = RNBP[:].rearrange("p (a b) -> p a b", b=Tc)
                            for mm_ in range(8):
                                ins = v.tensor_tensor(dst_b[:, mm_, tw], QQ[:, mm_, :],
                                                      rnb[:, mm_, :], OP.mult)
                            dve.inc(ins)
                    else:
                        @blk.vector
                        def _(v, tw=tw):
                            dve.wait(v)
                            st_sem.wait(v)
                            for mm_ in range(8):
                                ins = v.tensor_copy(VBC[:, mm_, :], QQ[:, mm_, :])
                            dve.inc(ins)

                        @blk.sync
                        def _(s, tw=tw):
                            dve.wait(s)
                            st_sem.inc(s.dma_start(out=w["v_s"][:, :, tw].transpose([1, 0, 2]),
                                                   in_=VBC[:]), 16)

            @blk.sync
            def _(s):
                st_sem.wait(s)

        if "D" in S7_PHASES:
         with contextlib.ExitStack() as sp:
            P = lambda b, sh, dt=F32: sp.enter_context(nc.psum_tensor(ctx.name(b), sh, dt))
            SB2 = lambda b, sh, dt: sp.enter_context(nc.sbuf_tensor(ctx.name(b), sh, dt))
            GBP = P("b7_gbp", [64, 1024])
            TRP2 = P("b7_trp2", [64, 64])
            GPS = [P("b7_gps", [64, 512]) for _ in range(2)]
            GPB = P("b7_gpb", [64, 512], BF16)
            OD2 = [P("b7_od2", [128, 128]) for _ in range(2)]
            MSB = SB2("b7_msb", [64, 64], F32)
            ONES16 = SB2("b7_ones16", [16, Tc], F32)
            SST = SB2("b7_sst", [64, H, 64], F32)
            SSTB = SB2("b7_sstb", [64, H, 64], BF16)
            GROWS = SB2("b7_grows", [16, Tc], F32)
            BROWS = SB2("b7_brows", [16, Tc], F32)
            GF = SB2("b7_gf", [1, 16 * 65], F32)
            BF_ = SB2("b7_bf", [1, 16 * SC], F32)
            GSBS = SB2("b7_gsbs", [64, 16, SC], F32)
            BSBS = SB2("b7_bsbs", [64, 16, SC], F32)
            CSTS = SB2("b7_csts", [64, 16], F32)
            BTTS = SB2("b7_btts", [64, 16], F32)
            PREVB = SB2("b7_prevb", [64, 16], F32)
            ENDR = SB2("b7_endr", [64, 96], F32)
            DIFF = SQ[0:64, 4:6, :].rearrange("p a b -> p (a b)").rearrange("p (a b) -> p a b", b=SC)
            DIFFT = SQ[0:64, 6:8, :].rearrange("p a b -> p (a b)").rearrange("p (a b) -> p a b", b=SC)
            X1R = SQ[0:64, 0:2, :].rearrange("p a b -> p (a b)").rearrange("p (a b) -> p a b", b=SC)
            X2R = SQ[0:64, 2:4, :].rearrange("p a b -> p (a b)").rearrange("p (a b) -> p a b", b=SC)
            DECS = DIFF
            DECTS = DIFFT
            DECTI = X1R
            TFS = X2R
            OALL = X1R
            LAB = SB2("b7_lab", [64, 16, SC], BF16)
            LTAB = SB2("b7_ltab", [64, 16, SC], BF16)
            MTAB = SB2("b7_mtab", [64, 16, SC], BF16)
            LPOW = SB2("b7_lpow", [64, 9, SC], BF16)
            XU = [SB2("b7_xu", [64, SC], BF16) for _ in range(2)]
            KTOK = SB2("b7_ktok", [64, SC], BF16)
            KALL0 = SB2("b7_kall0", [64, H, SC], BF16)
            QALL0 = SB2("b7_qall0", [64, H, SC], BF16)
            VSL0 = SB2("b7_vsl0", [64, SC], BF16)
            VTOK = SB2("b7_vtok", [64, SC], F32)
            DT0 = SB2("b7_dt0", [64, SC], F32)
            ONB = SB2("b7_onb", [64, H, SC], BF16)
            NRF = SB2("b7_nrf", [128, DK, Tc], BF16)
            VC = SB2("b7_vc", [128, 8, Tc], BF16)
            blk = sp.enter_context(nc.Block(no_gpsimd_drain=True))

            @blk.gpsimd
            def _(g):
                wdma.inc(g.dma_start(out=cst["mnorm_row"][:], in_=w["mnorm_in"]), 16)
                g.memset(ONES16[:], 1.0)
                cst["pool"].inc(g.memset(SST[:], 0.0))

            @blk.vector
            def _(v):
                cst["pool"].wait(v)
                dve.inc(v.tensor_copy(SSTB[:], SST[:]))

            @blk.tensor
            def _(t):
                wdma.wait(t)
                cst["pool"].wait(t)
                pe.inc(t.matmul(GPS[0][:, 0:64], lhsT=cst["ones_row"][0:1, 0:64],
                                rhs=cst["mnorm_row"][:], start=True, stop=True))

            @blk.vector
            def _(v):
                pe.wait(v)
                dve.inc(v.tensor_copy(MSB[:], GPS[0][:, 0:64]))

            for c in range(NCH):
                tw = slice(c * Tc, (c + 1) * Tc)

                @blk.sync
                def _(s, tw=tw):
                    pe.wait(s)
                    act.wait(s)
                    dve.wait(s)
                    st_sem.wait(s)
                    dma.inc(s.dma_start(out=VC[:], in_=w["v_s"][:, :, tw].transpose([1, 0, 2])), 16)

                @blk.scalar
                def _(s, c=c):
                    dve.wait(s)
                    dma.wait(s)
                    dr = DBALL[0:16, c * Tc:(c + 1) * Tc]
                    s.activation(out=GROWS[:], in_=dr, func=AF.Abs)
                    s.drain()
                    s.activation(out=GROWS[:], in_=GROWS[:], func=AF.Exp, scale=-1.0)
                    s.drain()
                    s.activation(out=GROWS[:], in_=GROWS[:], func=AF.Ln, bias=1.0)
                    s.drain()
                    act.inc(s.activation(out=BROWS[:], in_=DBALL[32:48, c * Tc:(c + 1) * Tc],
                                         func=AF.Sigmoid))

                @blk.vector
                def _(v, c=c):
                    act.wait(v)
                    dr = DBALL[0:16, c * Tc:(c + 1) * Tc]
                    v.tensor_scalar(out=SQ[0:16, 0, 0:Tc], in0=dr, scalar1=0.0, scalar2=None, op0=OP.min)
                    v.drain()
                    v.tensor_tensor(GROWS[:], SQ[0:16, 0, 0:Tc], GROWS[:], OP.subtract)
                    v.drain()
                    dve.inc(v.tensor_tensor_scan(GROWS[:], ONES16[:], GROWS[:], 0.0, OP.mult, OP.add))

                @blk.sync
                def _(s, c=c):
                    dve.wait(s)
                    st_sem.inc(s.dma_start(out=w["gb_s"][:], in_=GROWS[:]), 16)
                    st_sem.inc(s.dma_start(out=w["bb_s"][:], in_=BROWS[:]), 16)
                    st_sem.wait(s)

                for sc_i in range(NSC if S7_SC_LEVEL >= 1 else 0):
                    base = sc_i * SC
                    hasprev = (sc_i > 0)

                    @blk.sync
                    def _(s, base=base, hasprev=hasprev):
                        pe.wait(s)
                        dve.wait(s)
                        if hasprev:
                            dma.inc(s.dma_start(out=GF[:, 0:16 * 65].rearrange("u (h t) -> u h t", t=65),
                                                in_=w["gb_s"][:, base - 1:base + SC].unsqueeze(0)), 16)
                        else:
                            dma.inc(s.dma_start(out=GF[:, 0:16 * SC].rearrange("u (h t) -> u h t", t=SC),
                                                in_=w["gb_s"][:, base:base + SC].unsqueeze(0)), 16)
                        dma.inc(s.dma_start(out=BF_[:].rearrange("u (h t) -> u h t", t=SC),
                                            in_=w["bb_s"][:, base:base + SC].unsqueeze(0)), 16)

                    wstr = 65 if hasprev else SC
                    off0 = 1 if hasprev else 0
                    gfv = GF[:, 0:16 * wstr].rearrange("u (h t) -> u h t", t=wstr)[:, :, off0:off0 + SC]
                    bfv = BF_[:].rearrange("u (h t) -> u h t", t=SC)

                    if S7_SC_LEVEL < 2:
                        continue
                    @blk.tensor
                    def _(t, gfv=gfv):
                        dma.wait(t)
                        dve.wait(t)
                        act.wait(t)
                        t.matmul(GBP[:, 0:512], lhsT=cst["ones_row"][0:1, 0:64],
                                 rhs=gfv[:, 0:8, :], start=True, stop=True)
                        pe.inc(t.matmul(GBP[:, 512:1024], lhsT=cst["ones_row"][0:1, 0:64],
                                        rhs=gfv[:, 8:16, :], start=True, stop=True))

                    @blk.vector
                    def _(v):
                        pe.wait(v)
                        dve.inc(v.tensor_copy(GSBS[:].rearrange("p a b -> p (a b)"), GBP[:]))

                    @blk.tensor
                    def _(t, bfv=bfv):
                        dve.wait(t)
                        t.matmul(GBP[:, 0:512], lhsT=cst["ones_row"][0:1, 0:64], rhs=bfv[:, 0:8, :],
                                 start=True, stop=True)
                        pe.inc(t.matmul(GBP[:, 512:1024], lhsT=cst["ones_row"][0:1, 0:64],
                                        rhs=bfv[:, 8:16, :], start=True, stop=True))

                    @blk.vector
                    def _(v):
                        pe.wait(v)
                        dve.inc(v.tensor_copy(BSBS[:].rearrange("p a b -> p (a b)"), GBP[:]))

                    if S7_SC_LEVEL < 3:
                        continue
                    @blk.tensor
                    def _(t, base=base, sc_i=sc_i):
                        dve.wait(t)
                        t.transpose(TRP2[:, 0:16], in_=GROWS[:, base:base + SC],
                                    identity=cst["ident_f32"][0:16, 0:16])
                        pe.inc(t.transpose(TRP2[:, 16:32], in_=BROWS[:, base:base + SC],
                                           identity=cst["ident_f32"][0:16, 0:16]))

                    @blk.vector
                    def _(v):
                        pe.wait(v)
                        v.tensor_copy(CSTS[:], TRP2[:, 0:16])
                        dve.inc(v.tensor_copy(BTTS[:], TRP2[:, 16:32]))

                    if not hasprev:
                        @blk.vector
                        def _(v):
                            dve.inc(v.memset(PREVB[:], 0.0))
                    else:
                        @blk.tensor
                        def _(t):
                            dve.wait(t)
                            pe.inc(t.matmul(TRP2[:, 0:16], lhsT=cst["ones_row"][0:1, 0:64],
                                            rhs=GF[:, 0:16 * 65].rearrange("u (h t) -> u h t", t=65)[:, :, 0],
                                            start=True, stop=True))

                        @blk.vector
                        def _(v):
                            pe.wait(v)
                            dve.inc(v.tensor_copy(PREVB[:], TRP2[:, 0:16]))

                    if S7_SC_LEVEL < 4:
                        continue
                    @blk.vector
                    def _(v):
                        dve.wait(v)
                        endb = GSBS[:, :, SC - 1]
                        v.tensor_tensor(ENDR[:, 0:16], endb, CSTS[:], OP.subtract)
                        v.tensor_tensor(ENDR[:, 16:32], CSTS[:], PREVB[:], OP.subtract)
                        dve.inc(v.tensor_tensor(ENDR[:, 32:48], endb, PREVB[:], OP.subtract))

                    @blk.scalar
                    def _(s):
                        dve.wait(s)
                        act.inc(s.activation(out=ENDR[:, 48:96], in_=ENDR[:, 0:48], func=AF.Exp))

                    WEND = ENDR[:, 48:64]
                    GAMT = ENDR[:, 64:80]
                    GNR = ENDR[:, 80:96]

                    if S7_SC_LEVEL < 5:
                        continue
                    @blk.vector
                    def _(v):
                        act.wait(v)
                        cstb = CSTS[:].unsqueeze(2).broadcast_to([64, 16, SC])
                        v.tensor_tensor(DIFF, cstb, GSBS[:], OP.subtract)
                        dve.inc(v.tensor_tensor(DIFFT, GSBS[:], cstb, OP.subtract))

                    @blk.scalar
                    def _(s):
                        dve.wait(s)
                        s.activation(out=X1R.rearrange("p a b -> p (a b)"),
                                     in_=DIFF.rearrange("p a b -> p (a b)"), func=AF.Exp)
                        s.drain()
                        act.inc(s.activation(out=X2R.rearrange("p a b -> p (a b)"),
                                             in_=DIFFT.rearrange("p a b -> p (a b)"), func=AF.Exp))

                    @blk.vector
                    def _(v):
                        act.wait(v)
                        mlt = cst["masklt"][:].unsqueeze(1).broadcast_to([64, 16, SC])
                        mut = cst["maskut"][:].unsqueeze(1).broadcast_to([64, 16, SC])
                        muti = cst["maskuti"][:].unsqueeze(1).broadcast_to([64, 16, SC])
                        v.tensor_tensor(MTAB[:], X2R, muti, OP.mult)
                        v.drain()
                        v.tensor_tensor(DECS, X1R, mlt, OP.mult)
                        dve.inc(v.tensor_tensor(DECTS, X2R, mut, OP.mult))

                    if S7_SC_LEVEL < 6:
                        continue
                    @blk.vector
                    def _(v, base=base, tw=tw):
                        dve.wait(v)
                        for h in range(H):
                            hb = 64 * (h % 2)
                            hm = h // 2
                            v.tensor_copy(KALL0[:, h, :], KHB[hb:hb + 64, hm, tw][:, base:base + SC])
                            ins = v.tensor_copy(QALL0[:, h, :], QHB[hb:hb + 64, hm, tw][:, base:base + SC])
                        dve.inc(ins)

                    @blk.tensor
                    def _(t):
                        dve.wait(t)
                        for h in range(H):
                            ins = t.matmul(GBP[:, h * 64:(h + 1) * 64], lhsT=KALL0[:, h, :],
                                           rhs=KALL0[:, h, :], start=True, stop=True)
                        pe.inc(ins)

                    @blk.vector
                    def _(v):
                        pe.wait(v)
                        btb = BTTS[:].unsqueeze(2).broadcast_to([64, 16, SC])
                        gb = GBP[:].rearrange("p (a b) -> p a b", b=SC)
                        v.tensor_tensor(TFS, gb, DECS, OP.mult)
                        v.drain()
                        v.tensor_tensor(LAB[:], TFS, btb, OP.mult)
                        v.drain()
                        v.tensor_tensor(TFS, gb, DECTS, OP.mult)
                        v.drain()
                        dve.inc(v.tensor_tensor(LTAB[:], TFS, BSBS[:], OP.mult))

                    if S7_SC_LEVEL < 7:
                        continue
                    @blk.tensor
                    def _(t):
                        dve.wait(t)
                        for h in range(H):
                            ins = t.matmul(GBP[:, h * 64:(h + 1) * 64], lhsT=KALL0[:, h, :],
                                           rhs=QALL0[:, h, :], start=True, stop=True)
                        pe.inc(ins)

                    @blk.vector
                    def _(v):
                        pe.wait(v)
                        gb = GBP[:].rearrange("p (a b) -> p a b", b=SC)
                        dve.inc(v.tensor_tensor(MTAB[:], gb, MTAB[:], OP.mult))

                    for h in range(H if S7_D_LEVEL >= 2 else 0):
                        hb = 64 * (h % 2)
                        hm = h // 2
                        ksl = KHB[hb:hb + 64, hm, tw][:, base:base + SC]
                        qsl = QHB[hb:hb + 64, hm, tw][:, base:base + SC]

                        @blk.vector
                        def _(v, hb=hb, hm=hm, base=base):
                            dve.wait(v)
                            dve.inc(v.tensor_copy(VSL0[:], VC[hb:hb + 64, hm, base:base + SC]))

                        lcur_t = LAB[:, h, :]
                        lcur_s = LTAB[:, h, :]
                        for lv in range(5):
                            last_lv = (lv == 4)

                            @blk.tensor
                            def _(t, lv=lv, lcur_t=lcur_t, lcur_s=lcur_s, last_lv=last_lv):
                                dve.wait(t)
                                if last_lv:
                                    pe.inc(t.matmul(GPS[1][:, 0:SC], lhsT=lcur_t, rhs=lcur_s,
                                                    start=True, stop=True))
                                else:
                                    t.matmul(GPS[0][:, 0:SC], lhsT=lcur_s, rhs=lcur_t, start=True, stop=True)
                                    pe.inc(t.matmul(GPS[1][:, 0:SC], lhsT=lcur_t, rhs=lcur_s,
                                                    start=True, stop=True))

                            @blk.vector
                            def _(v, lv=lv, last_lv=last_lv):
                                pe.wait(v)
                                if last_lv:
                                    dve.inc(v.tensor_copy(LPOW[:, 8, :], GPS[1][:, 0:SC]))
                                else:
                                    v.tensor_copy(LPOW[:, 2 * lv, :], GPS[0][:, 0:SC])
                                    dve.inc(v.tensor_copy(LPOW[:, 2 * lv + 1, :], GPS[1][:, 0:SC]))

                            if lv < 4:
                                lcur_t = LPOW[:, 2 * lv, :]
                                lcur_s = LPOW[:, 2 * lv + 1, :]

                        if S7_D_LEVEL < 3:
                            continue
                        @blk.tensor
                        def _(t, h=h):
                            dve.wait(t)
                            t.transpose(GPB[:, 0:SC], in_=KALL0[:, h, :], identity=cst["ident_bf"][0:64, 0:64])
                            pe.inc(t.transpose(GPB[:, 64:64 + SC], in_=VSL0[:],
                                               identity=cst["ident_bf"][0:64, 0:64]))

                        @blk.vector
                        def _(v, h=h):
                            pe.wait(v)
                            v.tensor_scalar(out=KTOK[:], in0=GPB[:, 0:SC], scalar1=WEND[:, h:h + 1],
                                            scalar2=None, op0=OP.mult)
                            dve.inc(v.tensor_copy(VTOK[:], GPB[:, 64:64 + SC]))

                        @blk.tensor
                        def _(t, h=h):
                            dve.wait(t)
                            pe.inc(t.matmul(GPS[0][:, 0:SC], lhsT=KALL0[:, h, :], rhs=SSTB[:, h, :],
                                            start=True, stop=True))

                        @blk.vector
                        def _(v, h=h):
                            pe.wait(v)
                            v.tensor_scalar(out=DT0[:], in0=GPS[0][:, 0:SC], scalar1=GAMT[:, h:h + 1],
                                            scalar2=None, op0=OP.mult)
                            v.drain()
                            v.tensor_tensor(DT0[:], VTOK[:], DT0[:], OP.subtract)
                            v.drain()
                            dve.inc(v.tensor_scalar(out=XU[0][:], in0=DT0[:], scalar1=BTTS[:, h:h + 1],
                                                    scalar2=None, op0=OP.mult))

                        if S7_D_LEVEL < 4:
                            continue
                        cur = 0
                        for fj, (slot, sub) in enumerate([(8, False), (7, False), (5, False),
                                                          (3, False), (1, False), (-1, True)]):
                            @blk.tensor
                            def _(t, slot=slot, cur=cur, h=h):
                                dve.wait(t)
                                lt = LTAB[:, h, :] if slot == -1 else LPOW[:, slot, :]
                                pe.inc(t.matmul(GPS[1][:, 0:SC], lhsT=lt, rhs=XU[cur][:],
                                                start=True, stop=True))

                            @blk.vector
                            def _(v, cur=cur, sub=sub):
                                pe.wait(v)
                                dve.inc(v.tensor_tensor(XU[1 - cur][:], XU[cur][:], GPS[1][:, 0:SC],
                                                        OP.subtract if sub else OP.add))

                            cur = 1 - cur

                        if S7_D_LEVEL < 5:
                            continue
                        @blk.tensor
                        def _(t, h=h, cur=cur):
                            dve.wait(t)
                            t.matmul(GPS[0][:, 0:SC], lhsT=QALL0[:, h, :], rhs=SSTB[:, h, :], start=True, stop=True)
                            pe.inc(t.matmul(GPS[1][:, 0:SC], lhsT=MTAB[:, h, :], rhs=XU[cur][:],
                                            start=True, stop=True))

                        @blk.vector
                        def _(v, h=h):
                            pe.wait(v)
                            v.tensor_scalar(out=DT0[:], in0=GPS[0][:, 0:SC], scalar1=GAMT[:, h:h + 1],
                                            scalar2=None, op0=OP.mult)
                            v.drain()
                            dve.inc(v.tensor_tensor(OALL[:, h, :], DT0[:], GPS[1][:, 0:SC], OP.add))

                        @blk.tensor
                        def _(t, h=h, cur=cur):
                            dve.wait(t)
                            pe.inc(t.matmul(GPS[0][:, 0:SC], lhsT=KTOK[:], rhs=XU[cur][:],
                                            start=True, stop=True))

                        @blk.vector
                        def _(v, h=h):
                            pe.wait(v)
                            v.scalar_tensor_tensor(out=SST[:, h, :], in0=SST[:, h, :],
                                                   scalar=GNR[:, h:h + 1], in1=GPS[0][:, 0:SC],
                                                   op0=OP.mult, op1=OP.add)
                            v.drain()
                            dve.inc(v.tensor_copy(SSTB[:, h, :], SST[:, h, :]))

                    if S7_D_LEVEL < 6:
                        continue
                    @blk.vector
                    def _(v):
                        dve.wait(v)
                        dve.inc(v.tensor_scalar(out=OALL.rearrange("p a b -> p (a b)"),
                                                in0=OALL.rearrange("p a b -> p (a b)"),
                                                scalar1=0.125, scalar2=None, op0=OP.mult))

                    @blk.scalar
                    def _(s):
                        dve.wait(s)
                        act.inc(s.activation(out=SQ[0:64, 4:6, :].rearrange("p a b -> p (a b)"),
                                             in_=OALL.rearrange("p a b -> p (a b)"), func=AF.Square))

                    @blk.vector
                    def _(v):
                        act.wait(v)
                        sqv = SQ[0:64, 4:6, :].rearrange("p a b -> p (a b)").rearrange("p (a b) -> p a b", b=SC)
                        v.tensor_reduce(ENDR[:, 0:16], sqv, axis=AX.X, op=OP.add)
                        v.drain()
                        v.tensor_scalar(out=ENDR[:, 0:16], in0=ENDR[:, 0:16], scalar1=1.0 / HD,
                                        scalar2=1e-6, op0=OP.mult, op1=OP.add)
                        v.drain()
                        dve.inc(v.reciprocal(ENDR[:, 16:32], ENDR[:, 0:16]))

                    @blk.scalar
                    def _(s):
                        dve.wait(s)
                        act.inc(s.activation(out=ENDR[:, 0:16], in_=ENDR[:, 16:32], func=AF.Sqrt))

                    @blk.vector
                    def _(v):
                        act.wait(v)
                        rnv = ENDR[:, 0:16].unsqueeze(2).broadcast_to([64, 16, SC])
                        msv = MSB[:].unsqueeze(1).broadcast_to([64, 16, SC])
                        v.tensor_tensor(OALL, OALL, rnv, OP.mult)
                        v.drain()
                        dve.inc(v.tensor_tensor(ONB[:], OALL, msv, OP.mult))

                    for hw in range(2):
                        @blk.tensor
                        def _(t, hw=hw):
                            dve.wait(t)
                            for hh in range(8):
                                h2 = hw * 8 + hh
                                ins = t.transpose(GPB[:, hh * 64:(hh + 1) * 64], in_=ONB[:, h2, :],
                                                  identity=cst["ident_bf"][0:64, 0:64])
                            pe.inc(ins)

                        @blk.vector
                        def _(v, hw=hw, base=base):
                            pe.wait(v)
                            for hh in range(8):
                                h2 = hw * 8 + hh
                                ins = v.tensor_copy(NRF[64 * (h2 % 2):64 * (h2 % 2) + 64, h2 // 2,
                                                        base:base + SC],
                                                    GPB[:, hh * 64:(hh + 1) * 64])
                            dve.inc(ins)

                @blk.sync
                def _(s, tw=tw):
                    dve.wait(s)
                    st_sem.inc(s.dma_start(out=w["nrf_s"][:, :, tw].transpose([1, 0, 2]), in_=NRF[:]), 16)

            @blk.sync
            def _(s):
                st_sem.wait(s)

        if "E" in S7_PHASES:
         with contextlib.ExitStack() as sp:
            P = lambda b, sh, dt=F32: sp.enter_context(nc.psum_tensor(ctx.name(b), sh, dt))
            SB2 = lambda b, sh, dt: sp.enter_context(nc.sbuf_tensor(ctx.name(b), sh, dt))
            OD2 = [P("b7_od2", [128, 128]) for _ in range(2)]
            OUTW = SB2("b7_outw", [128, 16, 1024], BF16)
            OBC = SB2("b7_obc", [128, 8], F32)
            NRF2 = SB2("b7_nrf2", [128, DK, Tc], BF16)
            SGC2 = SB2("b7_sgc2", [128, 8, Tc], BF16)
            blk = sp.enter_context(nc.Block(no_gpsimd_drain=True))

            @blk.gpsimd
            def _(g):
                wdma.inc(g.dma_start(out=OUTW[:], in_=w["out_W"].rearrange("(k p) m -> p k m", p=128)), 16)
                wdma.inc(g.dma_start(out=OBC[:], in_=w["outb_fm"]), 16)

            for c in range(NCH):
                tw = slice(c * Tc, (c + 1) * Tc)

                @blk.sync
                def _(s, tw=tw):
                    pe.wait(s)
                    dve.wait(s)
                    st_sem.wait(s)
                    dma.inc(s.dma_start(out=NRF2[:], in_=w["nrf_s"][:, :, tw].transpose([1, 0, 2])), 16)
                    dma.inc(s.dma_start(out=SGC2[:], in_=w["sg_s"][:, :, tw].transpose([1, 0, 2])), 16)
                    dma.inc(s.dma_start(out=cst["RC"][:, :, 0:Tc],
                                        in_=src[:, :, tw].transpose([1, 0, 2])), 16)

                dve_od = dve.v

                @blk.tensor
                def _(t, tw=tw, dve_od=dve_od):
                    dma.wait(t)
                    dve.wait(t)
                    wdma.wait(t)
                    for mo in range(DK):
                        if mo >= 2:
                            t.wait_ge(dve.h, dve_od + mo - 1)
                        for k in range(DK):
                            t.matmul(OD2[mo % 2][:], lhsT=OUTW[:, k, mo * 128:(mo + 1) * 128],
                                     rhs=NRF2[:, k, :], start=(k == 0), stop=False)
                        for k in range(DK):
                            ins = t.matmul(OD2[mo % 2][:], lhsT=OUTW[:, 8 + k, mo * 128:(mo + 1) * 128],
                                           rhs=P20N[:, k, tw], start=False, stop=(k == DK - 1))
                        pe.inc(ins)

                @blk.vector
                def _(v, tw=tw):
                    base = pe.v - DK
                    for mo in range(DK):
                        v.wait_ge(pe.h, base + mo + 1)
                        if mo > 0:
                            v.drain()
                        v.scalar_tensor_tensor(out=SQ[:, 0, 0:Tc], in0=OD2[mo % 2][:],
                                               scalar=OBC[:, mo:mo + 1], in1=SGC2[:, mo, :],
                                               op0=OP.add, op1=OP.mult)
                        v.drain()
                        dve.inc(v.tensor_tensor(cst["RC"][:, mo, 0:Tc], cst["RC"][:, mo, 0:Tc],
                                                SQ[:, 0, 0:Tc], OP.add))

                @blk.sync
                def _(s, tw=tw):
                    dve.wait(s)
                    st_sem.inc(s.dma_start(out=dst[:, :, tw].transpose([1, 0, 2]),
                                           in_=cst["RC"][:, :, 0:Tc]), 16)

            @blk.sync
            def _(s):
                st_sem.wait(s)


SUBLAYERS = ["s1", "s2", "s3", "s4", "s5", "s6", "s7", "s8"]


def build_nc(subs):
    nc = bass.Bass("TRN2", target_bir_lowering=False, debug=False)

    dram = {}

    def din(name, shape, dtype=F32):
        dram[name] = nc.dram_tensor(name, shape, dtype, kind="ExternalInput").ap()
        return dram[name]

    res_in = din("res_in", [DK, 128, T])
    res_out = nc.dram_tensor("res_out", [DK, 128, T], F32, kind="ExternalOutput").ap()
    ident = din("ident128", [128, 128])
    cmask_in = din("cmask_in", [128, 128])
    perm_in = din("perm_in", [128, 128])

    need = set(subs)
    if "s2" in need or "s4" in need:
        din("l0f_Wg", [D, 16]); din("l0f_bg", [16]); din("l0f_bias", [16])
        din("l0f_W1", [16, D, 2048]); din("l0f_b1c", [128, 16, 16])
        din("l0f_W2", [16, 2048, D]); din("l0f_b2", [16, D])
    if "s1" in need:
        din("l0a_in_proj_w", [D, 4096]); din("l0a_out_proj_w", [2048, D])
        din("l0a_x_proj_w", [2048, 96]); din("l0a_dt_cat", [65, 2048])
        din("l0a_alog_fm", [128, 16, 16]); din("l0a_conv_fm", [128, 16, 4])
        din("l0a_convb_fm", [128, 16]); din("l0a_d_fm", [128, 16])
        din("l0a_ff_w1", [D, 4096]); din("l0a_ff_w2", [4096, D])
        din("l0a_ffb1_fm", [128, 32]); din("l0a_ffb2_fm", [128, 8])
        din("l0a_lng_fm", [128, 8]); din("l0a_lnb_fm", [128, 8])
    if "s3" in need:
        din("l0b_w_kv_down", [D, 128]); din("l0b_w_kv_up", [128, 2048])
        din("l0b_w_q", [D, D]); din("l0b_w_out", [D, D])
    if "s5" in need:
        din("l1a_Wd_q_w", [D, 512]); din("l1a_Wu_q_w", [512, D]); din("l1a_Wq_r_w", [512, D])
        din("l1a_Wk_r_w", [D, D]); din("l1a_Wd_kv_w", [D, 512]); din("l1a_Wu_k_w", [512, D])
        din("l1a_Wu_v_w", [512, D]); din("l1a_Wo_w", [D, D])
        din("l1a_bdq", [128, 4]); din("l1a_buq", [128, 8]); din("l1a_bqr", [128, 8])
        din("l1a_bdkv", [128, 4]); din("l1a_buk", [128, 8]); din("l1a_bkr", [128, 8])
        din("l1a_buv_row", [1, D]); din("l1a_wob", [128, 8])
        din("rope_cos", [8, 128, T]); din("rope_sin", [8, 128, T])
    if "s7" in need:
        din("l1b_ctrlw_re", [D, 4160]); din("l1b_cb_qkv", [128, 24]); din("l1b_cb_gate", [128, 8])
        din("l1b_cb_db", [64, 1]); din("l1b_conv3_fm", [128, 24, 4])
        din("l1b_dl2_fm", [128, 8]); din("l1b_th2_fm", [128, 8]); din("l1b_nsign_fm", [128, 8])
        din("l1b_vnorm_fm", [128, 8]); din("l1b_rampl_fm", [128, 8])
        din("l1b_p20c_W", [D, 3072]); din("l1b_out_W", [2048, D]); din("l1b_outb_fm", [128, 8])
        din("l1b_mnorm_in", [1, 64])
        din("selh_in", [128, 2]); din("selb2_in", [2, 128])
        din("masklt_in", [64, 64]); din("maskut_in", [64, 64]); din("maskuti_in", [64, 64])
    if "s6" in need or "s8" in need:
        din("l1f_Wr", [D, 8]); din("l1f_br", [8]); din("l1f_bias0", [8])
        din("l1f_W1", [8, D, 4096]); din("l1f_b1c", [128, 32, 8])
        din("l1f_W2", [8, 4096, D]); din("l1f_b2", [8, D])

    scratch = {}
    scratch["qkvc_s"] = nc.dram_tensor("qkvc_s", [24, 128, T], F32, kind="Internal").ap()
    scratch["sg_s"] = nc.dram_tensor("sg_s", [8, 128, T], BF16, kind="Internal").ap()
    scratch["v_s"] = nc.dram_tensor("v_s", [8, 128, T], BF16, kind="Internal").ap()
    scratch["nrf_s"] = nc.dram_tensor("nrf_s", [8, 128, T], BF16, kind="Internal").ap()
    for nm, sh in [("resA", [DK, 128, T]), ("resB", [DK, 128, T]),
                   ("cw16", [16, T]), ("cw8", [8, T]),
                   ("mo_s", [DK, 128, T]), ("bc_s", [32, 128]),
                   ("gb_s", [16, 128]), ("bb_s", [16, 128])]:
        scratch[nm] = nc.dram_tensor(nm, sh, F32, kind="Internal").ap()

    with contextlib.ExitStack() as st:
        ctx = Ctx(nc, st)
        E = st.enter_context
        cst = {}
        for s in ["pe", "act", "dve", "pool", "dma", "qdma", "wdma", "st"]:
            cst[s] = ctx.sem(s)
        cst["ones_col"] = E(nc.sbuf_tensor("ones_col", [128, 1], F32))
        cst["ones_row"] = E(nc.sbuf_tensor("ones_row", [1, 128], F32))
        cst["ones_row_bf"] = E(nc.sbuf_tensor("ones_row_bf", [1, 128], BF16))
        cst["ident_f32"] = E(nc.sbuf_tensor("ident_f32", [128, 128], F32))
        cst["ident_bf"] = E(nc.sbuf_tensor("ident_bf", [128, 128], BF16))
        cst["cmask"] = E(nc.sbuf_tensor("cmask", [128, 128], F32))
        cst["perm_f32"] = E(nc.sbuf_tensor("perm_f32", [128, 128], F32))
        cst["selh"] = E(nc.sbuf_tensor("selh", [128, 2], F32))
        cst["selb2"] = E(nc.sbuf_tensor("selb2", [2, 128], F32))
        cst["masklt"] = E(nc.sbuf_tensor("masklt", [64, 64], F32))
        cst["maskut"] = E(nc.sbuf_tensor("maskut", [64, 64], F32))
        cst["maskuti"] = E(nc.sbuf_tensor("maskuti", [64, 64], F32))
        cst["mnorm_row"] = E(nc.sbuf_tensor("mnorm_row", [1, 64], F32))
        cst["RC"] = E(nc.sbuf_tensor("rc_shared", [128, DK, 512], F32))
        cst["SQ"] = E(nc.sbuf_tensor("sq_shared", [128, DK, 512], F32))
        cst["ROWS"] = E(nc.sbuf_tensor("rows_shared", [1, 2048], F32))

        with nc.Block(no_gpsimd_drain=True) as blk:
            @blk.gpsimd
            def _(g):
                g.memset(cst["ones_col"][:], 1.0)
                g.memset(cst["ones_row"][:], 1.0)
                g.memset(cst["ones_row_bf"][:], 1.0)
                cst["wdma"].inc(g.dma_start(out=cst["ident_f32"][:], in_=ident), 16)
                cst["wdma"].inc(g.dma_start(out=cst["ident_bf"][:], in_=ident), 16)
                cst["wdma"].inc(g.dma_start(out=cst["cmask"][:], in_=cmask_in), 16)
                cst["wdma"].inc(g.dma_start(out=cst["perm_f32"][:], in_=perm_in), 16)
                if "selh_in" in dram:
                    cst["wdma"].inc(g.dma_start(out=cst["selh"][:], in_=dram["selh_in"]), 16)
                    cst["wdma"].inc(g.dma_start(out=cst["selb2"][:], in_=dram["selb2_in"]), 16)
                    cst["wdma"].inc(g.dma_start(out=cst["masklt"][:], in_=dram["masklt_in"]), 16)
                    cst["wdma"].inc(g.dma_start(out=cst["maskut"][:], in_=dram["maskut_in"]), 16)
                    cst["wdma"].inc(g.dma_start(out=cst["maskuti"][:], in_=dram["maskuti_in"]), 16)
                cst["wdma"].wait(g)

        cur = res_in
        bufs = [scratch["resA"], scratch["resB"]]
        for i, s in enumerate(subs):
            dst = res_out if i == len(subs) - 1 else bufs[i % 2]
            if s in ("s2", "s4"):
                w = {"Wg": dram["l0f_Wg"], "bg": dram["l0f_bg"], "bias": dram["l0f_bias"],
                     "W1": dram["l0f_W1"], "b1c": dram["l0f_b1c"], "W2": dram["l0f_W2"],
                     "b2": dram["l0f_b2"], "cw_s": scratch["cw16"]}
                emit_moe(ctx, cst, cur, dst, w, nE=16, F=2048, topk=2, actfn=AF.Relu)
            elif s in ("s6", "s8"):
                w = {"Wg": dram["l1f_Wr"], "bg": dram["l1f_br"], "bias": dram["l1f_bias0"],
                     "W1": dram["l1f_W1"], "b1c": dram["l1f_b1c"], "W2": dram["l1f_W2"],
                     "b2": dram["l1f_b2"], "cw_s": scratch["cw8"]}
                emit_moe(ctx, cst, cur, dst, w, nE=8, F=4096, topk=1, actfn=AF.Gelu)
            elif s == "s1":
                w = {"in_proj_w": dram["l0a_in_proj_w"], "out_proj_w": dram["l0a_out_proj_w"],
                     "x_proj_w": dram["l0a_x_proj_w"], "dt_cat": dram["l0a_dt_cat"],
                     "alog_fm": dram["l0a_alog_fm"], "conv_fm": dram["l0a_conv_fm"],
                     "convb_fm": dram["l0a_convb_fm"], "d_fm": dram["l0a_d_fm"],
                     "ff_w1": dram["l0a_ff_w1"], "ff_w2": dram["l0a_ff_w2"],
                     "ffb1_fm": dram["l0a_ffb1_fm"], "ffb2_fm": dram["l0a_ffb2_fm"],
                     "lng_fm": dram["l0a_lng_fm"], "lnb_fm": dram["l0a_lnb_fm"],
                     "mo_s": scratch["mo_s"], "bc_s": scratch["bc_s"]}
                emit_l0_attn_a(ctx, cst, cur, dst, w)
            elif s == "s3":
                w = {"w_kv_down": dram["l0b_w_kv_down"], "w_kv_up": dram["l0b_w_kv_up"],
                     "w_q": dram["l0b_w_q"], "w_out": dram["l0b_w_out"]}
                emit_l0_attn_b(ctx, cst, cur, dst, w)
            elif s == "s7":
                w = {"ctrlw_re": dram["l1b_ctrlw_re"], "cb_qkv": dram["l1b_cb_qkv"],
                     "cb_gate": dram["l1b_cb_gate"], "cb_db": dram["l1b_cb_db"],
                     "conv3_fm": dram["l1b_conv3_fm"], "dl2_fm": dram["l1b_dl2_fm"],
                     "th2_fm": dram["l1b_th2_fm"], "nsign_fm": dram["l1b_nsign_fm"],
                     "vnorm_fm": dram["l1b_vnorm_fm"], "rampl_fm": dram["l1b_rampl_fm"],
                     "p20c_W": dram["l1b_p20c_W"], "out_W": dram["l1b_out_W"],
                     "outb_fm": dram["l1b_outb_fm"], "mnorm_in": dram["l1b_mnorm_in"],
                     "qkvc_s": scratch["qkvc_s"], "sg_s": scratch["sg_s"], "v_s": scratch["v_s"],
                     "gb_s": scratch["gb_s"], "bb_s": scratch["bb_s"], "nrf_s": scratch["nrf_s"]}
                emit_l1_attn_b(ctx, cst, cur, dst, w)
            elif s == "s5":
                w = {"Wd_q_w": dram["l1a_Wd_q_w"], "Wu_q_w": dram["l1a_Wu_q_w"],
                     "Wq_r_w": dram["l1a_Wq_r_w"], "Wk_r_w": dram["l1a_Wk_r_w"],
                     "Wd_kv_w": dram["l1a_Wd_kv_w"], "Wu_k_w": dram["l1a_Wu_k_w"],
                     "Wu_v_w": dram["l1a_Wu_v_w"], "Wo_w": dram["l1a_Wo_w"],
                     "bdq": dram["l1a_bdq"], "buq": dram["l1a_buq"], "bqr": dram["l1a_bqr"],
                     "bdkv": dram["l1a_bdkv"], "buk": dram["l1a_buk"], "bkr": dram["l1a_bkr"],
                     "buv_row": dram["l1a_buv_row"], "wob": dram["l1a_wob"],
                     "rope_cos": dram["rope_cos"], "rope_sin": dram["rope_sin"]}
                emit_l1_attn_a(ctx, cst, cur, dst, w)
            else:
                raise NotImplementedError(s)
            cur = dst

    return nc


def prep_weight_maps(inputs):
    m = {}
    m["ident128"] = np.eye(128, dtype=np.float32)
    cm = np.where(np.arange(128)[None, :] <= np.arange(128)[:, None], 0.0, -1e30)
    m["cmask_in"] = cm.astype(np.float32)
    pm = np.zeros((128, 128), np.float32)
    for i in range(0, 128, 2):
        pm[i, i + 1] = 1.0
        pm[i + 1, i] = 1.0
    m["perm_in"] = pm
    if "L1_attn_a_Wo_w" in inputs:
        m["l1a_Wd_q_w"] = inputs["L1_attn_a_Wd_q_w"]
        m["l1a_Wu_q_w"] = inputs["L1_attn_a_Wu_q_w"]
        m["l1a_Wq_r_w"] = inputs["L1_attn_a_Wq_r_w"]
        m["l1a_Wk_r_w"] = inputs["L1_attn_a_Wk_r_w"]
        m["l1a_Wd_kv_w"] = inputs["L1_attn_a_Wd_kv_w"]
        m["l1a_Wu_k_w"] = inputs["L1_attn_a_Wu_k_w"]
        m["l1a_Wu_v_w"] = inputs["L1_attn_a_Wu_v_w"]
        m["l1a_Wo_w"] = inputs["L1_attn_a_Wo_w"]
        fm = lambda v, c: np.ascontiguousarray(np.asarray(v, np.float32).reshape(c, 128).T)
        m["l1a_bdq"] = fm(inputs["L1_attn_a_Wd_q_b"], 4)
        m["l1a_buq"] = fm(inputs["L1_attn_a_Wu_q_b"], 8)
        m["l1a_bqr"] = fm(inputs["L1_attn_a_Wq_r_b"], 8)
        m["l1a_bdkv"] = fm(inputs["L1_attn_a_Wd_kv_b"], 4)
        m["l1a_buk"] = fm(inputs["L1_attn_a_Wu_k_b"], 8)
        m["l1a_bkr"] = fm(inputs["L1_attn_a_Wk_r_b"], 8)
        m["l1a_buv_row"] = np.asarray(inputs["L1_attn_a_Wu_v_b"], np.float32).reshape(1, D)
        m["l1a_wob"] = fm(inputs["L1_attn_a_Wo_b"], 8)
        ids = np.arange(512, dtype=np.float64)
        theta = (10000.0 ** (-2.0 * ids)) / 512.0
        r = np.arange(T, dtype=np.float64)[:, None] * theta[None, :]
        cosr = np.cos(r); sinr = np.sin(r)
        cos2 = np.empty((D, T), np.float32); sin2 = np.empty((D, T), np.float32)
        cos2[0::2, :] = cosr.T; cos2[1::2, :] = cosr.T
        sin2[0::2, :] = -sinr.T; sin2[1::2, :] = sinr.T
        m["rope_cos"] = np.ascontiguousarray(cos2.reshape(8, 128, T))
        m["rope_sin"] = np.ascontiguousarray(sin2.reshape(8, 128, T))
    if "L0_attn_a_in_proj_w" in inputs:
        fm16 = lambda v, c: np.ascontiguousarray(np.asarray(v, np.float32).reshape(c, 128).T)
        m["l0a_in_proj_w"] = inputs["L0_attn_a_in_proj_w"]
        m["l0a_out_proj_w"] = inputs["L0_attn_a_out_proj_w"]
        m["l0a_x_proj_w"] = inputs["L0_attn_a_x_proj_w"]
        m["l0a_dt_cat"] = np.ascontiguousarray(np.concatenate(
            [np.asarray(inputs["L0_attn_a_dt_proj_w"], np.float32),
             np.asarray(inputs["L0_attn_a_dt_proj_b"], np.float32)[None, :]], 0))
        m["l0a_alog_fm"] = np.ascontiguousarray(
            np.asarray(inputs["L0_attn_a_A_log"], np.float32).reshape(16, 128, 16).transpose(1, 0, 2))
        m["l0a_conv_fm"] = np.ascontiguousarray(
            np.asarray(inputs["L0_attn_a_conv_w"], np.float32).reshape(16, 128, 4).transpose(1, 0, 2))
        m["l0a_convb_fm"] = fm16(inputs["L0_attn_a_conv_b"], 16)
        m["l0a_d_fm"] = fm16(inputs["L0_attn_a_D"], 16)
        m["l0a_ff_w1"] = inputs["L0_attn_a_ff_w1"]
        m["l0a_ff_w2"] = inputs["L0_attn_a_ff_w2"]
        m["l0a_ffb1_fm"] = fm16(inputs["L0_attn_a_ff_b1"], 32)
        m["l0a_ffb2_fm"] = fm16(inputs["L0_attn_a_ff_b2"], 8)
        m["l0a_lng_fm"] = fm16(inputs["L0_attn_a_ln_g"], 8)
        m["l0a_lnb_fm"] = fm16(inputs["L0_attn_a_ln_b"], 8)
    if "L1_attn_b_ctrl_W" in inputs:
        fmc = lambda v, c: np.ascontiguousarray(np.asarray(v, np.float32).reshape(c, 128).T)
        cw = np.asarray(inputs["L1_attn_b_ctrl_W"], np.float32)
        cb = np.asarray(inputs["L1_attn_b_ctrl_b"], np.float32)
        perm = np.concatenate([np.arange(0, 3072), np.arange(3104, 4128),
                               np.arange(3072, 3104)])
        cwr = cw[:, perm]
        cbr = cb[perm]
        z16 = np.zeros((D, 16), np.float32)
        m["l1b_ctrlw_re"] = np.ascontiguousarray(np.concatenate(
            [cwr[:, 0:4096], cwr[:, 4096:4112], z16, cwr[:, 4112:4128], z16], 1))
        zb = np.zeros((16,), np.float32)
        dbcat = np.concatenate([cbr[4096:4112], zb, cbr[4112:4128], zb])
        m["l1b_cb_qkv"] = fmc(cbr[0:3072], 24)
        m["l1b_cb_gate"] = fmc(cbr[3072:4096], 8)
        m["l1b_cb_db"] = np.ascontiguousarray(dbcat.reshape(64, 1))
        c3 = np.stack([np.asarray(inputs["L1_attn_b_qconv_w"], np.float32),
                       np.asarray(inputs["L1_attn_b_kconv_w"], np.float32),
                       np.asarray(inputs["L1_attn_b_vconv_w"], np.float32)], 0)
        m["l1b_conv3_fm"] = np.ascontiguousarray(
            c3.reshape(3, 8, 128, 4).transpose(2, 0, 1, 3).reshape(128, 24, 4))
        m["l1b_dl2_fm"] = fmc(np.repeat(np.asarray(inputs["L1_attn_b_p20_decay_log"], np.float32), 2), 8)
        m["l1b_th2_fm"] = fmc(np.repeat(np.asarray(inputs["L1_attn_b_p20_theta"], np.float32), 2), 8)
        nsign = np.tile(np.array([1.0, -1.0], np.float32), 512)
        m["l1b_nsign_fm"] = fmc(nsign, 8)
        m["l1b_vnorm_fm"] = fmc(inputs["L1_attn_b_vnorm_scale"], 8)
        m["l1b_rampl_fm"] = fmc(inputs["L1_attn_b_ramp_logit"], 8)
        m["l1b_p20c_W"] = inputs["L1_attn_b_p20c_W"]
        m["l1b_out_W"] = inputs["L1_attn_b_out_W"]
        m["l1b_outb_fm"] = fmc(inputs["L1_attn_b_out_b"], 8)
        m["l1b_mnorm_in"] = np.asarray(inputs["L1_attn_b_mnorm_scale"], np.float32).reshape(1, 64)
        m["selh_in"] = np.ascontiguousarray(
            (np.arange(128)[:, None] // 64 == np.arange(2)[None, :]).astype(np.float32))
        m["selb2_in"] = np.ascontiguousarray(m["selh_in"].T)
        ii = np.arange(64)
        m["masklt_in"] = (ii[None, :] < ii[:, None]).astype(np.float32)
        m["maskut_in"] = (ii[None, :] > ii[:, None]).astype(np.float32)
        m["maskuti_in"] = (ii[None, :] >= ii[:, None]).astype(np.float32)
    if "L0_attn_b_w_q" in inputs:
        m["l0b_w_kv_down"] = inputs["L0_attn_b_w_kv_down"]
        m["l0b_w_kv_up"] = inputs["L0_attn_b_w_kv_up"]
        m["l0b_w_q"] = inputs["L0_attn_b_w_q"]
        m["l0b_w_out"] = inputs["L0_attn_b_w_out"]
    if "L0_ffn_Wg" in inputs:
        m["l0f_Wg"] = inputs["L0_ffn_Wg"]
        m["l0f_bg"] = inputs["L0_ffn_bg"]
        m["l0f_bias"] = inputs["L0_ffn_bias"]
        m["l0f_W1"] = inputs["L0_ffn_W1"]
        m["l0f_b1c"] = np.ascontiguousarray(
            inputs["L0_ffn_b1"].reshape(16, 16, 128).transpose(2, 1, 0))
        m["l0f_W2"] = inputs["L0_ffn_W2"]
        m["l0f_b2"] = inputs["L0_ffn_b2"]
    if "L1_ffn_W1" in inputs:
        m["l1f_Wr"] = inputs["L1_ffn_Wr"]
        m["l1f_br"] = inputs["L1_ffn_br"]
        m["l1f_bias0"] = np.zeros((8,), np.float32)
        m["l1f_W1"] = inputs["L1_ffn_W1"]
        m["l1f_b1c"] = np.ascontiguousarray(
            inputs["L1_ffn_b1"].reshape(8, 32, 128).transpose(2, 1, 0))
        m["l1f_W2"] = inputs["L1_ffn_W2"]
        m["l1f_b2"] = inputs["L1_ffn_b2"]
    return m


def to_fm(x_core):
    return np.ascontiguousarray(x_core.T.reshape(DK, 128, T))


def from_fm(y_fm):
    return np.ascontiguousarray(y_fm.reshape(D, T).T)


def run_subs(subs, res_in_full, inputs):
    nc = build_nc(subs)
    wm = prep_weight_maps(inputs)
    in_maps = []
    for c in range(NCORES):
        d = dict(wm)
        d["res_in"] = to_fm(np.asarray(res_in_full[c], np.float32))
        in_maps.append(d)
    res = run_bass_kernel_spmd(nc, in_maps, core_ids=list(range(NCORES)))
    return np.stack([from_fm(r["res_out"]) for r in res.results])


def kernel(**inputs):
    cur = np.asarray(inputs["L0_attn_a_x"], np.float32)
    return run_subs(SUBLAYERS, cur, inputs)


# version 07
# speedup vs baseline: 1.0801x; 1.0801x over previous
import contextlib
import numpy as np
import concourse.bass as bass
import concourse.mybir as mybir
from concourse.alu_op_type import AluOpType as OP
from concourse.bass_utils import run_bass_kernel_spmd

F32 = mybir.dt.float32
BF16 = mybir.dt.bfloat16
AF = mybir.ActivationFunctionType
AX = mybir.AxisListType

T = 2048
D = 1024
DK = 8
NCORES = 8
LN_EPS = 1e-5


class Sem:
    __slots__ = ("h", "v")

    def __init__(self, h):
        self.h = h
        self.v = 0

    def inc(self, ins, k=1):
        ins.then_inc(self.h, k)
        self.v += k

    def wait(self, eng, v=None):
        eng.wait_ge(self.h, self.v if v is None else v)


class Ctx:
    def __init__(self, nc, st):
        self.nc = nc
        self.st = st
        self.uid = 0

    def name(self, base):
        self.uid += 1
        return f"{base}_{self.uid}"

    def sem(self, base):
        return Sem(self.st.enter_context(self.nc.semaphore(self.name(base))))


def emit_ln(ctx, blk, cst, rc_ap, xh_ap, W, ps, sq, rows, xhf_ap=None, gcol=None, bcol=None):
    pe, act, dve = cst["pe"], cst["act"], cst["dve"]
    lnrow, nmb, rb = ps["lnrow"], ps["nmb"], ps["rb"]

    @blk.tensor
    def _(t):
        for k in range(DK):
            ins = t.matmul(lnrow[:, 0:W], lhsT=cst["ones_col"][:], rhs=rc_ap[:, k, :],
                           start=(k == 0), stop=(k == DK - 1))
        pe.inc(ins)

    @blk.scalar
    def _(s):
        act.inc(s.activation(out=sq[:, :, 0:W], in_=rc_ap, func=AF.Square))

    @blk.tensor
    def _(t):
        act.wait(t)
        for k in range(DK):
            ins = t.matmul(lnrow[:, 512:512 + W], lhsT=cst["ones_col"][:], rhs=sq[:, k, 0:W],
                           start=(k == 0), stop=(k == DK - 1))
        pe.inc(ins)

    @blk.vector
    def _(v):
        pe.wait(v)
        nm = rows[:, 0:W]
        m2 = rows[:, 512:512 + W]
        var = rows[:, 1024:1024 + W]
        rv = rows[:, 1536:1536 + W]
        v.tensor_scalar(out=nm, in0=lnrow[:, 0:W], scalar1=-1.0 / D, scalar2=None, op0=OP.mult)
        v.drain()
        v.tensor_tensor(m2, nm, nm, OP.mult)
        v.drain()
        v.scalar_tensor_tensor(out=var, in0=lnrow[:, 512:512 + W], scalar=1.0 / D,
                               in1=m2, op0=OP.mult, op1=OP.subtract)
        v.drain()
        v.tensor_scalar(out=var, in0=var, scalar1=float(LN_EPS), scalar2=None, op0=OP.add)
        v.drain()
        dve.inc(v.reciprocal(rv, var))

    @blk.scalar
    def _(s):
        dve.wait(s)
        act.inc(s.activation(out=rows[:, 512:512 + W], in_=rows[:, 1536:1536 + W], func=AF.Sqrt))

    @blk.tensor
    def _(t):
        act.wait(t)
        t.matmul(nmb[:, 0:W], lhsT=cst["ones_row"][:], rhs=rows[:, 0:W], start=True, stop=True)
        pe.inc(t.matmul(rb[:, 0:W], lhsT=cst["ones_row"][:], rhs=rows[:, 512:512 + W], start=True, stop=True))

    @blk.vector
    def _(v):
        pe.wait(v)
        for k in range(DK):
            v.tensor_tensor(sq[:, k, 0:W], rc_ap[:, k, :], nmb[:, 0:W], OP.add)
        v.drain()
        if gcol is None:
            for k in range(DK):
                ins = v.tensor_tensor(xh_ap[:, k, :], sq[:, k, 0:W], rb[:, 0:W], OP.mult)
                if xhf_ap is not None:
                    ins = v.tensor_tensor(xhf_ap[:, k, :], sq[:, k, 0:W], rb[:, 0:W], OP.mult)
            dve.inc(ins)
        else:
            for k in range(DK):
                v.tensor_tensor(sq[:, k, 0:W], sq[:, k, 0:W], rb[:, 0:W], OP.mult)
            v.drain()
            for k in range(DK):
                ins = v.tensor_scalar(out=xh_ap[:, k, :], in0=sq[:, k, 0:W],
                                      scalar1=gcol[:, k:k + 1], scalar2=bcol[:, k:k + 1],
                                      op0=OP.mult, op1=OP.add)
                if xhf_ap is not None:
                    ins = v.tensor_scalar(out=xhf_ap[:, k, :], in0=sq[:, k, 0:W],
                                          scalar1=gcol[:, k:k + 1], scalar2=bcol[:, k:k + 1],
                                          op0=OP.mult, op1=OP.add)
            dve.inc(ins)


def emit_moe(ctx, cst, src, dst, w, nE, F, topk, actfn=AF.Relu):
    nc = ctx.nc
    NS = F // 1024
    Ms = 8
    NT = T // 512
    pe, act, dve = cst["pe"], cst["act"], cst["dve"]
    dma, wdma, st_sem = cst["dma"], cst["wdma"], cst["st"]

    with contextlib.ExitStack() as sl:
        SB = lambda b, sh, dt: sl.enter_context(nc.sbuf_tensor(ctx.name(b), sh, dt))
        XH = SB("moe_xh", [128, DK, T], BF16)
        ACC = SB("moe_acc", [128, DK, T], F32)
        H1 = SB("moe_h1", [128, Ms, T], BF16)
        SQ = cst["SQ"]
        ROWS = cst["ROWS"]
        WGS = SB("moe_wg", [128, DK, nE], BF16)
        BGC = SB("moe_bgc", [nE, 1], F32)
        BIC = SB("moe_bic", [nE, 1], F32)
        B1C = SB("moe_b1c", [128, F // 128, nE], F32)
        LG = SB("moe_lg", [nE, 512], F32)
        LGT = SB("moe_lgt", [128, 4, nE], F32)
        MST = SB("moe_mst", [128, 16], F32)
        CWT = SB("moe_cwt", [128, 4, nE], F32)
        CWSB = SB("moe_cwsb", [nE, 512], F32)

        with contextlib.ExitStack() as sp:
            P = lambda b, sh: sp.enter_context(nc.psum_tensor(ctx.name(b), sh, F32))
            ps = {"lnrow": P("moe_lnrow", [1, 1024]),
                  "nmb": P("moe_nmb", [128, 512]),
                  "rb": P("moe_rb", [128, 512])}
            LGP = P("moe_lgp", [nE, 512])
            TRP = P("moe_trp", [128, 4 * nE])
            CWP = P("moe_cwp", [nE, 512])
            blk = sp.enter_context(nc.Block(no_gpsimd_drain=True))

            @blk.gpsimd
            def _(g):
                wdma.inc(g.dma_start(out=WGS[:], in_=w["Wg"].rearrange("(k p) e -> p k e", p=128)), 16)
                wdma.inc(g.dma_start(out=BGC[:], in_=w["bg"].rearrange("(e u) -> e u", u=1)), 16)
                wdma.inc(g.dma_start(out=BIC[:], in_=w["bias"].rearrange("(e u) -> e u", u=1)), 16)
                wdma.inc(g.dma_start(out=B1C[:], in_=w["b1c"]), 16)

            for tb in range(NT):
                tw = slice(tb * 512, (tb + 1) * 512)

                @blk.sync
                def _(s, tw=tw):
                    dma.inc(s.dma_start(out=ACC[:, :, tw], in_=src[:, :, tw].transpose([1, 0, 2])), 16)

                @blk.tensor
                def _(t):
                    dma.wait(t)
                    dve.wait(t)

                @blk.scalar
                def _(s):
                    dma.wait(s)
                    dve.wait(s)

                emit_ln(ctx, blk, cst, ACC[:, :, tw], XH[:, :, tw], 512, ps, SQ, ROWS)

                @blk.tensor
                def _(t, tw=tw):
                    dve.wait(t)
                    wdma.wait(t)
                    for k in range(DK):
                        ins = t.matmul(LGP[:], lhsT=WGS[:, k, :], rhs=XH[:, k, tw],
                                       start=(k == 0), stop=(k == DK - 1))
                    pe.inc(ins)

                @blk.vector
                def _(v):
                    pe.wait(v)
                    dve.inc(v.tensor_scalar(out=LG[:], in0=LGP[:], scalar1=BGC[:], scalar2=BIC[:],
                                             op0=OP.add, op1=OP.add))

                @blk.tensor
                def _(t):
                    dve.wait(t)
                    for j in range(4):
                        ins = t.transpose(TRP[:, j * nE:(j + 1) * nE], in_=LG[:, j * 128:(j + 1) * 128],
                                          identity=cst["ident_f32"][0:nE, 0:nE])
                    pe.inc(ins)

                @blk.vector
                def _(v):
                    pe.wait(v)
                    v.tensor_copy(LGT[:], TRP[:].rearrange("p (j e) -> p j e", e=nE))
                    v.drain()
                    v.tensor_reduce(MST[:, 0:4], LGT[:], axis=AX.X, op=OP.max)
                    v.drain()
                    for j in range(4):
                        v.tensor_scalar(out=CWT[:, j, :], in0=LGT[:, j, :], scalar1=MST[:, j:j + 1],
                                        scalar2=None, op0=OP.is_equal)
                    v.drain()
                    if topk == 2:
                        for j in range(4):
                            v.tensor_scalar(out=SQ[:, 0, j * nE:(j + 1) * nE], in0=CWT[:, j, :],
                                            scalar1=1e30, scalar2=None, op0=OP.mult)
                        v.drain()
                        for j in range(4):
                            sc = SQ[:, 0, j * nE:(j + 1) * nE]
                            v.tensor_tensor(sc, LGT[:, j, :], sc, OP.subtract)
                        v.drain()
                        for j in range(4):
                            v.tensor_reduce(MST[:, 4 + j:5 + j], SQ[:, 0, j * nE:(j + 1) * nE],
                                            axis=AX.X, op=OP.max)
                        v.drain()
                        for j in range(4):
                            v.tensor_scalar(out=SQ[:, 1, j * nE:(j + 1) * nE],
                                            in0=SQ[:, 0, j * nE:(j + 1) * nE],
                                            scalar1=MST[:, 4 + j:5 + j], scalar2=None, op0=OP.is_equal)
                        ins = v.tensor_tensor(MST[:, 8:12], MST[:, 4:8], MST[:, 0:4], OP.subtract)
                    else:
                        for j in range(4):
                            ins = v.tensor_scalar(out=SQ[:, 2, j * nE:(j + 1) * nE], in0=LGT[:, j, :],
                                                  scalar1=MST[:, j:j + 1], scalar2=None, op0=OP.subtract)
                    dve.inc(ins)

                @blk.scalar
                def _(s):
                    dve.wait(s)
                    if topk == 2:
                        ins = s.activation(out=MST[:, 12:16], in_=MST[:, 8:12], func=AF.Exp)
                    else:
                        ins = s.activation(out=SQ[:, 3, 0:4 * nE], in_=SQ[:, 2, 0:4 * nE], func=AF.Exp)
                    act.inc(ins)

                @blk.vector
                def _(v):
                    act.wait(v)
                    if topk == 2:
                        e_ = MST[:, 12:16]
                        v.tensor_scalar(out=MST[:, 4:8], in0=e_, scalar1=1.0, scalar2=None, op0=OP.add)
                        v.drain()
                        v.reciprocal(MST[:, 8:12], MST[:, 4:8])
                        v.drain()
                        v.tensor_tensor(MST[:, 12:16], e_, MST[:, 8:12], OP.mult)
                        v.drain()
                        for j in range(4):
                            v.tensor_scalar(out=CWT[:, j, :], in0=CWT[:, j, :],
                                            scalar1=MST[:, 8 + j:9 + j], scalar2=None, op0=OP.mult)
                        v.drain()
                        for j in range(4):
                            m1m = SQ[:, 1, j * nE:(j + 1) * nE]
                            ins = v.scalar_tensor_tensor(out=CWT[:, j, :], in0=m1m,
                                                         scalar=MST[:, 12 + j:13 + j], in1=CWT[:, j, :],
                                                         op0=OP.mult, op1=OP.add)
                    else:
                        v.tensor_reduce(MST[:, 4:8], SQ[:, 3, 0:4 * nE].rearrange("p (j e) -> p j e", e=nE),
                                        axis=AX.X, op=OP.add)
                        v.drain()
                        v.reciprocal(MST[:, 8:12], MST[:, 4:8])
                        v.drain()
                        for j in range(4):
                            ins = v.tensor_scalar(out=CWT[:, j, :], in0=CWT[:, j, :],
                                                  scalar1=MST[:, 8 + j:9 + j], scalar2=None, op0=OP.mult)
                    dve.inc(ins)

                @blk.tensor
                def _(t):
                    dve.wait(t)
                    for j in range(4):
                        ins = t.transpose(CWP[:, j * 128:(j + 1) * 128], in_=CWT[:, j, :],
                                          identity=cst["ident_f32"][:])
                    pe.inc(ins)

                @blk.vector
                def _(v):
                    pe.wait(v)
                    st_sem.wait(v)
                    dve.inc(v.tensor_copy(CWSB[:], CWP[:]))

                @blk.sync
                def _(s, tw=tw):
                    dve.wait(s)
                    st_sem.inc(s.dma_start(out=w["cw_s"][:, tw], in_=CWSB[:]), 16)

            @blk.sync
            def _(s):
                st_sem.wait(s)

        with contextlib.ExitStack() as sp:
            P = lambda b, sh: sp.enter_context(nc.psum_tensor(ctx.name(b), sh, F32))
            SB2 = lambda b, sh, dt: sp.enter_context(nc.sbuf_tensor(ctx.name(b), sh, dt))
            CWBP = P("moe_cwbp", [128, 2048])
            G1P = [P("moe_g1p", [128, 512]) for _ in range(2)]
            G2P = [P("moe_g2p", [128, 512]) for _ in range(2)]
            W1H = SB2("moe_w1h", [128, DK, 1024], BF16)
            W2H = cst["RC"][:].rearrange("p a b -> p (a b)").bitcast(BF16).rearrange("p (a b) -> p a b", b=D)
            CWRB = SB2("moe_cwrb", [1, T], BF16)
            B2E = SB2("moe_b2e", [1, D], BF16)
            CWBS = SB2("moe_cwbs", [128, T], BF16)
            blk = sp.enter_context(nc.Block(no_gpsimd_drain=True))

            for e in range(nE):
                @blk.gpsimd
                def _(g, e=e):
                    pe.wait(g)
                    wdma.inc(g.dma_start(out=CWRB[:], in_=w["cw_s"][e:e + 1, :]), 16)
                    wdma.inc(g.dma_start(out=B2E[:], in_=w["b2"][e, :].rearrange("(u d) -> u d", u=1)), 16)
                    wdma.inc(g.dma_start(
                        out=W1H[:], in_=w["W1"][e, :, 0:1024].rearrange("(k p) m -> p k m", p=128)), 16)
                    wdma.inc(g.dma_start(
                        out=W2H[:], in_=w["W2"][e, 0:1024, :].rearrange("(m p) d -> p m d", p=128)), 16)

                @blk.tensor
                def _(t, e=e):
                    dve.wait(t)
                    wdma.wait(t)
                    for tb in range(NT):
                        ins = t.matmul(CWBP[:, tb * 512:(tb + 1) * 512], lhsT=cst["ones_row_bf"][:],
                                       rhs=CWRB[:, tb * 512:(tb + 1) * 512], start=True, stop=True)
                    pe.inc(ins)

                @blk.vector
                def _(v):
                    pe.wait(v)
                    dve.inc(v.tensor_copy(CWBS[:], CWBP[:]))

                for sp_i in range(NS):
                    hof = sp_i * Ms
                    if sp_i > 0:
                        @blk.gpsimd
                        def _(g, e=e, sp_i=sp_i):
                            pe.wait(g)
                            c0 = sp_i * 1024
                            wdma.inc(g.dma_start(
                                out=W1H[:], in_=w["W1"][e, :, c0:c0 + 1024].rearrange("(k p) m -> p k m", p=128)), 16)
                            wdma.inc(g.dma_start(
                                out=W2H[:], in_=w["W2"][e, c0:c0 + 1024, :].rearrange("(m p) d -> p m d", p=128)), 16)

                    act_base = act.v
                    g1_pe_base = pe.v

                    @blk.tensor
                    def _(t, e=e, act_base=act_base):
                        wdma.wait(t)
                        dve.wait(t)
                        ci = 0
                        for m in range(Ms):
                            for tb in range(NT):
                                if ci >= 2:
                                    t.wait_ge(act.h, act_base + ci - 1)
                                for k in range(DK):
                                    ins = t.matmul(G1P[ci % 2][:], lhsT=W1H[:, k, m * 128:(m + 1) * 128],
                                                   rhs=XH[:, k, tb * 512:(tb + 1) * 512],
                                                   start=(k == 0), stop=(k == DK - 1))
                                pe.inc(ins)
                                ci += 1

                    @blk.scalar
                    def _(s, e=e, hof=hof, g1_pe_base=g1_pe_base):
                        ci = 0
                        for m in range(Ms):
                            for tb in range(NT):
                                s.wait_ge(pe.h, g1_pe_base + ci + 1)
                                act.inc(s.activation(out=H1[:, m, tb * 512:(tb + 1) * 512], in_=G1P[ci % 2][:],
                                                     func=actfn, bias=B1C[:, hof + m, e:e + 1]))
                                ci += 1

                    @blk.vector
                    def _(v, e=e, g1_act_base=act_base):
                        v.drain()
                        ci = 0
                        for m in range(Ms):
                            for tb in range(NT):
                                v.wait_ge(act.h, g1_act_base + ci + 1)
                                tw = slice(tb * 512, (tb + 1) * 512)
                                ins = v.tensor_tensor(H1[:, m, tw], H1[:, m, tw], CWBS[:, tw], OP.mult)
                                ci += 1
                        dve.inc(ins)

                    dve_base = dve.v
                    g2_pe_base = pe.v

                    @blk.tensor
                    def _(t, e=e, sp_i=sp_i, dve_base=dve_base):
                        wdma.wait(t)
                        dve.wait(t)
                        ci = 0
                        for tb in range(NT):
                            for mo in range(DK):
                                if ci >= 2:
                                    t.wait_ge(dve.h, dve_base + ci - 1)
                                for hk in range(Ms):
                                    last = (sp_i > 0 and hk == Ms - 1)
                                    ins = t.matmul(G2P[ci % 2][:], lhsT=W2H[:, hk, mo * 128:(mo + 1) * 128],
                                                   rhs=H1[:, hk, tb * 512:(tb + 1) * 512],
                                                   start=(hk == 0), stop=last)
                                if sp_i == 0:
                                    ins = t.matmul(G2P[ci % 2][:], lhsT=B2E[:, mo * 128:(mo + 1) * 128],
                                                   rhs=CWRB[:, tb * 512:(tb + 1) * 512], start=False, stop=True)
                                pe.inc(ins)
                                ci += 1

                    @blk.vector
                    def _(v, e=e, g2_pe_base=g2_pe_base):
                        v.drain()
                        ci = 0
                        for tb in range(NT):
                            for mo in range(DK):
                                v.wait_ge(pe.h, g2_pe_base + ci + 1)
                                tw = slice(tb * 512, (tb + 1) * 512)
                                dve.inc(v.tensor_tensor(ACC[:, mo, tw], ACC[:, mo, tw], G2P[ci % 2][:], OP.add))
                                ci += 1

            @blk.sync
            def _(s):
                dve.wait(s)
                for tb in range(NT):
                    tw = slice(tb * 512, (tb + 1) * 512)
                    st_sem.inc(s.dma_start(out=dst[:, :, tw].transpose([1, 0, 2]), in_=ACC[:, :, tw]), 16)
                st_sem.wait(s)


def emit_attn_qc(ctx, cst, blk, bufs, qcb, L, scale, causal):
    pe, act, dve = cst["pe"], cst["act"], cst["dve"]
    QCB, KF, VT = bufs["QCB"], bufs["KF"], bufs["VT"]
    PB, PT = bufs["PB"], bufs["PT"]
    SPS, AVP = bufs["SPS"], bufs["AVP"]
    COLS = bufs["COLS"]
    YSLAB, qoff = bufs["YSLAB"], bufs["qoff"]
    nb = (L + 127) // 128
    for h in [hh for hh in range(16) if hh % 2 == 0] + [hh for hh in range(16) if hh % 2 == 1]:
        hb = 64 * (h % 2)
        hm = h // 2

        @blk.tensor
        def _(t, h=h, hb=hb, hm=hm):
            act.wait(t)
            dve.wait(t)
            lq = QCB[hb:hb + 64, hm, :]
            pos = 0
            while pos < L:
                w = min(512, L - pos)
                ins = t.matmul(SPS[:, pos:pos + w], lhsT=lq,
                               rhs=KF[hb:hb + 64, hm, pos:pos + w], start=True, stop=True)
                pos += w
            pe.inc(ins)

        @blk.vector
        def _(v, h=h):
            pe.wait(v)
            if causal:
                v.tensor_tensor(SPS[:, L - 128:L], SPS[:, L - 128:L], cst["cmask"][:], OP.add)
                v.drain()
            v.tensor_reduce(COLS[:, 0:1], SPS[:, 0:L], axis=AX.X, op=OP.max)
            v.drain()
            dve.inc(v.tensor_scalar(out=COLS[:, 3:4], in0=COLS[:, 0:1], scalar1=-scale,
                                    scalar2=None, op0=OP.mult))

        @blk.scalar
        def _(s, h=h):
            dve.wait(s)
            act.inc(s.activation(out=PB[:, 0:L], in_=SPS[:, 0:L], func=AF.Exp,
                                 scale=scale, bias=COLS[:, 3:4], accum_out=COLS[:, 1:2]))

        @blk.vector
        def _(v, h=h):
            act.wait(v)
            v.reciprocal(COLS[:, 2:3], COLS[:, 1:2])
            v.drain()
            dve.inc(v.tensor_scalar(out=PB[:, 0:L], in0=PB[:, 0:L], scalar1=COLS[:, 2:3],
                                    scalar2=None, op0=OP.mult))

        nwav = (nb + 7) // 8
        for wv in range(nwav):
            b0, b1 = wv * 8, min((wv + 1) * 8, nb)

            @blk.tensor
            def _(t, h=h, b0=b0, b1=b1):
                dve.wait(t)
                for b in range(b0, b1):
                    ins = t.transpose(bufs["TPPb"][:, (b - b0) * 128:(b - b0 + 1) * 128],
                                      in_=PB[:, b * 128:(b + 1) * 128], identity=cst["ident_bf"][:])
                pe.inc(ins)

            @blk.vector
            def _(v, h=h, b0=b0, b1=b1):
                pe.wait(v)
                dve.inc(v.tensor_copy(PT[:, b0:b1, :].rearrange("p b q -> p (b q)"),
                                      bufs["TPPb"][:, 0:(b1 - b0) * 128]))

        @blk.tensor
        def _(t, h=h, hb=hb, hm=hm):
            dve.wait(t)
            for b in range(nb):
                tkc = b
                ins = t.matmul(AVP[:], lhsT=VT[:, tkc, h * 64:(h + 1) * 64],
                               rhs=PT[:, tkc, :], start=(b == 0), stop=(b == nb - 1))
            pe.inc(ins)

        @blk.vector
        def _(v, h=h, hb=hb, hm=hm):
            pe.wait(v)
            dve.inc(v.tensor_copy(YSLAB[hb:hb + 64, hm, qoff:qoff + 128], AVP[:]))


def emit_l0_attn_b(ctx, cst, src, dst, w):
    nc = ctx.nc
    pe, act, dve = cst["pe"], cst["act"], cst["dve"]
    dma, wdma, st_sem = cst["dma"], cst["wdma"], cst["st"]
    with contextlib.ExitStack() as sl:
        SB = lambda b, sh, dt: sl.enter_context(nc.sbuf_tensor(ctx.name(b), sh, dt))
        XH = SB("a3_xh", [128, DK, T], BF16)
        KF = SB("a3_kf", [128, DK, T], BF16)
        VT = SB("a3_vt", [128, 16, 1024], BF16)
        SQ = cst["SQ"]
        ROWS = cst["ROWS"]
        WKVD = SB("a3_wkvd", [128, DK, 128], BF16)
        WKVU = SB("a3_wkvu", [128, 2048], BF16)
        WQ = SB("a3_wq", [128, DK, 1024], BF16)
        WOUT = SB("a3_wout", [128, DK, 1024], BF16)
        LATB = SB("a3_latb", [128, 512], BF16)

        with contextlib.ExitStack() as sp:
            P = lambda b, sh, dt=F32: sp.enter_context(nc.psum_tensor(ctx.name(b), sh, dt))
            ps = {"lnrow": P("a3_lnrow", [1, 1024]),
                  "nmb": P("a3_nmb", [128, 512]),
                  "rb": P("a3_rb", [128, 512])}
            LATP = P("a3_latp", [128, 512])
            KVP = [P("a3_kvp", [128, 512]) for _ in range(2)]
            blk = sp.enter_context(nc.Block(no_gpsimd_drain=True))

            @blk.gpsimd
            def _(g):
                wdma.inc(g.dma_start(out=WKVD[:], in_=w["w_kv_down"].rearrange("(k p) m -> p k m", p=128)), 16)
                wdma.inc(g.dma_start(out=WKVU[:], in_=w["w_kv_up"]), 16)
                wdma.inc(g.dma_start(out=WQ[:], in_=w["w_q"].rearrange("(k p) m -> p k m", p=128)), 16)
                wdma.inc(g.dma_start(out=WOUT[:], in_=w["w_out"].rearrange("(k p) m -> p k m", p=128)), 16)

            for tb in range(4):
                tw = slice(tb * 512, (tb + 1) * 512)

                @blk.sync
                def _(s, tw=tw):
                    pe.wait(s)
                    act.wait(s)
                    dve.wait(s)
                    dma.inc(s.dma_start(out=cst["RC"][:],
                                        in_=src[:, :, tw].transpose([1, 0, 2])), 16)

                @blk.tensor
                def _(t):
                    dma.wait(t)
                    dve.wait(t)

                @blk.scalar
                def _(s):
                    dma.wait(s)
                    dve.wait(s)

                emit_ln(ctx, blk, cst, cst["RC"][:], XH[:, :, tw], 512, ps, SQ, ROWS)

                @blk.tensor
                def _(t, tw=tw):
                    dve.wait(t)
                    wdma.wait(t)
                    for k in range(DK):
                        ins = t.matmul(LATP[:], lhsT=WKVD[:, k, :], rhs=XH[:, k, tw],
                                       start=(k == 0), stop=(k == DK - 1))
                    pe.inc(ins)

                @blk.vector
                def _(v):
                    pe.wait(v)
                    dve.inc(v.tensor_copy(LATB[:], LATP[:]))

                dve_base_k = dve.v

                @blk.tensor
                def _(t, tb=tb, tw=tw, dve_base_k=dve_base_k):
                    dve.wait(t)
                    for m in range(DK):
                        if m >= 2:
                            t.wait_ge(dve.h, dve_base_k + m - 1)
                        ins = t.matmul(KVP[m % 2][:], lhsT=WKVU[:, m * 128:(m + 1) * 128],
                                       rhs=LATB[:], start=True, stop=True)
                        pe.inc(ins)

                @blk.vector
                def _(v, tw=tw):
                    base = pe.v - DK
                    for m in range(DK):
                        v.wait_ge(pe.h, base + m + 1)
                        dve.inc(v.tensor_copy(KF[:, m, tw], KVP[m % 2][:]))

                dve_base_v = dve.v

                @blk.tensor
                def _(t, tb=tb, dve_base_v=dve_base_v):
                    dve.wait(t)
                    for qi in range(4):
                        for vh in range(2):
                            j = qi * 2 + vh
                            if j >= 2:
                                t.wait_ge(dve.h, dve_base_v + j - 1)
                            ins = t.matmul(KVP[j % 2][:],
                                           lhsT=LATB[:, qi * 128:(qi + 1) * 128],
                                           rhs=WKVU[:, 1024 + vh * 512:1024 + (vh + 1) * 512],
                                           start=True, stop=True)
                            pe.inc(ins)

                @blk.vector
                def _(v, tb=tb):
                    base = pe.v - 8
                    for qi in range(4):
                        for vh in range(2):
                            v.wait_ge(pe.h, base + qi * 2 + vh + 1)
                            dve.inc(v.tensor_copy(VT[:, tb * 4 + qi, vh * 512:(vh + 1) * 512],
                                                  KVP[(qi * 2 + vh) % 2][:]))

        with contextlib.ExitStack() as sp:
            P = lambda b, sh, dt=F32: sp.enter_context(nc.psum_tensor(ctx.name(b), sh, dt))
            SB2 = lambda b, sh, dt: sp.enter_context(nc.sbuf_tensor(ctx.name(b), sh, dt))
            SPS = P("a3_sps", [128, 2048])
            TPPb = P("a3_tpp", [128, 1024], BF16)
            AVP = P("a3_avp", [64, 128])
            PP2 = [P("a3_pp2", [128, 512]) for _ in range(2)]
            QP = [PP2[0][:, 0:128], PP2[1][:, 0:128]]
            OPP = PP2
            QCB = SB2("a3_qcb", [128, DK, 128], BF16)
            PB = SB2("a3_pb", [128, T], BF16)
            PT = SB2("a3_pt", [128, 16, 128], BF16)
            COLS = SB2("a3_cols", [128, 4], F32)
            YSLAB = SB2("a3_yslab", [128, DK, 512], BF16)
            blk = sp.enter_context(nc.Block(no_gpsimd_drain=True))

            bufs = {"KF": KF, "VT": VT, "PB": PB, "PT": PT, "SPS": SPS,
                    "TPPb": TPPb, "AVP": AVP, "COLS": COLS, "QCB": QCB,
                    "YSLAB": YSLAB, "qoff": 0}

            for qc in range(16):
                tb = qc // 4
                L = (qc + 1) * 128 if True else T
                bufs["qoff"] = (qc % 4) * 128

                qdve_base = dve.v

                @blk.tensor
                def _(t, qc=qc, qdve_base=qdve_base):
                    dve.wait(t)
                    act.wait(t)
                    for m in range(DK):
                        if m >= 2:
                            t.wait_ge(dve.h, qdve_base + m - 1)
                        for k in range(DK):
                            ins = t.matmul(QP[m % 2], lhsT=WQ[:, k, m * 128:(m + 1) * 128],
                                           rhs=XH[:, k, qc * 128:(qc + 1) * 128],
                                           start=(k == 0), stop=(k == DK - 1))
                        pe.inc(ins)

                @blk.vector
                def _(v, qc=qc):
                    base = pe.v - DK
                    for m in range(DK):
                        v.wait_ge(pe.h, base + m + 1)
                        dve.inc(v.tensor_copy(QCB[:, m, :], QP[m % 2]))

                emit_attn_qc(ctx, cst, blk, bufs, qc, L, 0.125, True)

                if qc % 4 == 3:
                    @blk.sync
                    def _(s, tb=tb):
                        st_sem.wait(s)
                        dma.inc(s.dma_start(out=cst["RC"][:],
                                            in_=src[:, :, tb * 512:(tb + 1) * 512].transpose([1, 0, 2])), 16)

                    odve_base = dve.v

                    @blk.tensor
                    def _(t, tb=tb, odve_base=odve_base):
                        dve.wait(t)
                        for mo in range(DK):
                            if mo >= 2:
                                t.wait_ge(dve.h, odve_base + mo - 1)
                            for k in range(DK):
                                ins = t.matmul(OPP[mo % 2][:], lhsT=WOUT[:, k, mo * 128:(mo + 1) * 128],
                                               rhs=YSLAB[:, k, :], start=(k == 0), stop=(k == DK - 1))
                            pe.inc(ins)

                    @blk.vector
                    def _(v, tb=tb):
                        dma.wait(v)
                        base = pe.v - DK
                        v.drain()
                        for mo in range(DK):
                            v.wait_ge(pe.h, base + mo + 1)
                            dve.inc(v.tensor_tensor(cst["RC"][:, mo, :], cst["RC"][:, mo, :],
                                                    OPP[mo % 2][:], OP.add))

                    @blk.sync
                    def _(s, tb=tb):
                        dve.wait(s)
                        st_sem.inc(s.dma_start(out=dst[:, :, tb * 512:(tb + 1) * 512].transpose([1, 0, 2]),
                                               in_=cst["RC"][:]), 16)

            @blk.sync
            def _(s):
                st_sem.wait(s)


def emit_rope_side(ctx, cst, blk, tb, out_f, ct, wu, bu_col, wr, br_col, wr_rhs, wr_nk,
                   ps, sbufs, w, tabname):
    pe, act, dve, dma = cst["pe"], cst["act"], cst["dve"], cst["dma"]
    PU, PR, PSW = ps["PU"], ps["PR"], ps["PSW"]
    QRF, T1, T2, C2B, S2B = sbufs["QRF"], sbufs["T1"], sbufs["T2"], sbufs["C2B"], sbufs["S2B"]
    tw = slice(tb * 512, (tb + 1) * 512)
    for m in range(DK):
        @blk.sync
        def _(s, m=m):
            pe.wait(s)
            dve.wait(s)
            dma.inc(s.dma_start(out=C2B[:], in_=w[tabname + "_cos"][m, :, tw]), 16)
            dma.inc(s.dma_start(out=S2B[:], in_=w[tabname + "_sin"][m, :, tw]), 16)

        @blk.tensor
        def _(t, m=m):
            dve.wait(t)
            act.wait(t)
            for kk in range(4):
                insu = t.matmul(PU[:], lhsT=wu[:, kk, m * 128:(m + 1) * 128],
                                rhs=ct[:, kk, :], start=(kk == 0), stop=(kk == 3))
            for kk in range(wr_nk):
                insr = t.matmul(PR[:], lhsT=wr[:, kk, m * 128:(m + 1) * 128],
                                rhs=wr_rhs[:, kk, :], start=(kk == 0), stop=(kk == wr_nk - 1))
            pe.inc(insr)

        @blk.vector
        def _(v, m=m):
            pe.wait(v)
            dve.inc(v.tensor_scalar(out=QRF[:], in0=PR[:], scalar1=br_col[:, m:m + 1],
                                    scalar2=None, op0=OP.add))

        @blk.tensor
        def _(t, m=m):
            dve.wait(t)
            pe.inc(t.matmul(PSW[:], lhsT=cst["perm_f32"][:], rhs=QRF[:], start=True, stop=True))

        @blk.vector
        def _(v, m=m):
            pe.wait(v)
            dma.wait(v)
            v.tensor_tensor(T1[:], QRF[:], C2B[:], OP.mult)
            v.tensor_tensor(T2[:], PSW[:], S2B[:], OP.mult)
            v.drain()
            v.tensor_tensor(T1[:], T1[:], T2[:], OP.add)
            v.drain()
            dve.inc(v.scalar_tensor_tensor(out=out_f[:, m, tw], in0=PU[:],
                                           scalar=bu_col[:, m:m + 1], in1=T1[:],
                                           op0=OP.add, op1=OP.add))


def emit_l1_attn_a(ctx, cst, src, dst, w):
    nc = ctx.nc
    pe, act, dve = cst["pe"], cst["act"], cst["dve"]
    dma, wdma, st_sem = cst["dma"], cst["wdma"], cst["st"]
    with contextlib.ExitStack() as sl:
        SB = lambda b, sh, dt: sl.enter_context(nc.sbuf_tensor(ctx.name(b), sh, dt))
        QF = SB("a5_qf", [128, DK, T], BF16)
        KF = SB("a5_kf", [128, DK, T], BF16)
        VT = SB("a5_vt", [128, 16, 1024], BF16)
        SQ = cst["SQ"]
        ROWS = cst["ROWS"]

        for side in ("q", "kv"):
            with contextlib.ExitStack() as sp:
                P = lambda b, sh, dt=F32: sp.enter_context(nc.psum_tensor(ctx.name(b), sh, dt))
                SB2 = lambda b, sh, dt: sp.enter_context(nc.sbuf_tensor(ctx.name(b), sh, dt))
                CP = [P("a5_cp", [128, 512]) for _ in range(2)]
                ps = {"lnrow": P("a5_lnrow", [1, 1024]),
                      "nmb": CP[0][:], "rb": CP[1][:]}
                rps = {"PU": P("a5_pu", [128, 512]), "PR": P("a5_pr", [128, 512]),
                       "PSW": P("a5_psw", [128, 512])}
                XHc = SB2("a5_xhc", [128, DK, 512], BF16)
                CT = SB2("a5_ct", [128, 4, 512], BF16)
                WD = SB2("a5_wd", [128, DK, 512], BF16)
                WU = SB2("a5_wu", [128, 4, 1024], BF16)
                WR = SB2("a5_wr", [128, DK if side == "kv" else 4, 1024], BF16)
                WUV = SB2("a5_wuv", [128, 4, 1024], BF16) if side == "kv" else None
                BD = SB2("a5_bd", [128, 4], F32)
                BU = SB2("a5_bu", [128, DK], F32)
                BR = SB2("a5_br", [128, DK], F32)
                BUVR = SB2("a5_buvr", [1, 1024], BF16) if side == "kv" else None
                rsb = {"QRF": SB2("a5_qrf", [128, 512], F32),
                       "T1": SB2("a5_t1", [128, 512], F32),
                       "T2": SB2("a5_t2", [128, 512], F32),
                       "C2B": SB2("a5_c2b", [128, 512], F32),
                       "S2B": SB2("a5_s2b", [128, 512], F32)}
                blk = sp.enter_context(nc.Block(no_gpsimd_drain=True))

                @blk.gpsimd
                def _(g, side=side):
                    if side == "q":
                        wdma.inc(g.dma_start(out=WD[:], in_=w["Wd_q_w"].rearrange("(k p) m -> p k m", p=128)), 16)
                        wdma.inc(g.dma_start(out=WU[:], in_=w["Wu_q_w"].rearrange("(k p) m -> p k m", p=128)), 16)
                        wdma.inc(g.dma_start(out=WR[:], in_=w["Wq_r_w"].rearrange("(k p) m -> p k m", p=128)), 16)
                        wdma.inc(g.dma_start(out=BD[:], in_=w["bdq"]), 16)
                        wdma.inc(g.dma_start(out=BU[:], in_=w["buq"]), 16)
                        wdma.inc(g.dma_start(out=BR[:], in_=w["bqr"]), 16)
                    else:
                        wdma.inc(g.dma_start(out=WD[:], in_=w["Wd_kv_w"].rearrange("(k p) m -> p k m", p=128)), 16)
                        wdma.inc(g.dma_start(out=WU[:], in_=w["Wu_k_w"].rearrange("(k p) m -> p k m", p=128)), 16)
                        wdma.inc(g.dma_start(out=WR[:], in_=w["Wk_r_w"].rearrange("(k p) m -> p k m", p=128)), 16)
                        wdma.inc(g.dma_start(out=WUV[:], in_=w["Wu_v_w"].rearrange("(k p) m -> p k m", p=128)), 16)
                        wdma.inc(g.dma_start(out=BD[:], in_=w["bdkv"]), 16)
                        wdma.inc(g.dma_start(out=BU[:], in_=w["buk"]), 16)
                        wdma.inc(g.dma_start(out=BR[:], in_=w["bkr"]), 16)
                        wdma.inc(g.dma_start(out=BUVR[:], in_=w["buv_row"]), 16)

                for tb in range(4):
                    tw = slice(tb * 512, (tb + 1) * 512)

                    @blk.sync
                    def _(s, tw=tw):
                        pe.wait(s)
                        act.wait(s)
                        dve.wait(s)
                        dma.inc(s.dma_start(out=cst["RC"][:],
                                            in_=src[:, :, tw].transpose([1, 0, 2])), 16)

                    @blk.tensor
                    def _(t):
                        dma.wait(t)
                        dve.wait(t)
                        wdma.wait(t)

                    @blk.scalar
                    def _(s):
                        dma.wait(s)
                        dve.wait(s)

                    emit_ln(ctx, blk, cst, cst["RC"][:], XHc[:], 512, ps, SQ, ROWS)

                    cdve_base = dve.v

                    @blk.tensor
                    def _(t, cdve_base=cdve_base):
                        dve.wait(t)
                        for m in range(4):
                            if m >= 2:
                                t.wait_ge(dve.h, cdve_base + m - 1)
                            for k in range(DK):
                                ins = t.matmul(CP[m % 2][:], lhsT=WD[:, k, m * 128:(m + 1) * 128],
                                               rhs=XHc[:, k, :], start=(k == 0), stop=(k == DK - 1))
                            pe.inc(ins)

                    @blk.vector
                    def _(v):
                        base = pe.v - 4
                        for m in range(4):
                            v.wait_ge(pe.h, base + m + 1)
                            dve.inc(v.tensor_scalar(out=CT[:, m, :], in0=CP[m % 2][:],
                                                    scalar1=BD[:, m:m + 1], scalar2=None, op0=OP.add))

                    if side == "q":
                        emit_rope_side(ctx, cst, blk, tb, QF, CT, WU, BU, WR, BR,
                                       CT, 4, rps, rsb, w, "rope")
                    else:
                        emit_rope_side(ctx, cst, blk, tb, KF, CT, WU, BU, WR, BR,
                                       XHc, DK, rps, rsb, w, "rope")
                        vdve_base = dve.v

                        @blk.tensor
                        def _(t, tb=tb, vdve_base=vdve_base):
                            dve.wait(t)
                            act.wait(t)
                            for qi in range(4):
                                for vh in range(2):
                                    j = qi * 2 + vh
                                    if j >= 2:
                                        t.wait_ge(dve.h, vdve_base + j - 1)
                                    for kk in range(4):
                                        ins = t.matmul(CP[j % 2][:],
                                                       lhsT=CT[:, kk, qi * 128:(qi + 1) * 128],
                                                       rhs=WUV[:, kk, vh * 512:(vh + 1) * 512],
                                                       start=(kk == 0), stop=False)
                                    ins = t.matmul(CP[j % 2][:], lhsT=cst["ones_row_bf"][:],
                                                   rhs=BUVR[:, vh * 512:(vh + 1) * 512],
                                                   start=False, stop=True)
                                    pe.inc(ins)

                        @blk.vector
                        def _(v, tb=tb):
                            base = pe.v - 8
                            for qi in range(4):
                                for vh in range(2):
                                    j = qi * 2 + vh
                                    v.wait_ge(pe.h, base + j + 1)
                                    dve.inc(v.tensor_copy(VT[:, tb * 4 + qi, vh * 512:(vh + 1) * 512],
                                                          CP[j % 2][:]))

        with contextlib.ExitStack() as sp:
            P = lambda b, sh, dt=F32: sp.enter_context(nc.psum_tensor(ctx.name(b), sh, dt))
            SB2 = lambda b, sh, dt: sp.enter_context(nc.sbuf_tensor(ctx.name(b), sh, dt))
            SPS = P("a5_sps", [128, 2048])
            TPPb = P("a5_tpp", [128, 1024], BF16)
            AVP = P("a5_avp", [64, 128])
            OPP = [P("a5_opp", [128, 512]) for _ in range(2)]
            WOUT = SB2("a5_wo", [128, DK, 1024], BF16)
            WOB = SB2("a5_wob", [128, DK], F32)
            PB = SB2("a5_pb", [128, T], BF16)
            PT = SB2("a5_pt", [128, 16, 128], BF16)
            COLS = SB2("a5_cols", [128, 4], F32)
            YSLAB = SB2("a5_yslab", [128, DK, 512], BF16)
            blk = sp.enter_context(nc.Block(no_gpsimd_drain=True))

            @blk.gpsimd
            def _(g):
                wdma.inc(g.dma_start(out=WOUT[:], in_=w["Wo_w"].rearrange("(k p) m -> p k m", p=128)), 16)
                wdma.inc(g.dma_start(out=WOB[:], in_=w["wob"]), 16)

            bufs = {"KF": KF, "VT": VT, "PB": PB, "PT": PT, "SPS": SPS,
                    "TPPb": TPPb, "AVP": AVP, "COLS": COLS, "QCB": None,
                    "YSLAB": YSLAB, "qoff": 0}
            scale = float(1.0 / np.sqrt(np.float32(1040.0)))

            for qc in range(16):
                tb = qc // 4
                bufs["qoff"] = (qc % 4) * 128
                bufs["QCB"] = QF[:, :, qc * 128:(qc + 1) * 128]

                @blk.tensor
                def _(t):
                    wdma.wait(t)

                emit_attn_qc(ctx, cst, blk, bufs, qc, T, scale, False)

                if qc % 4 == 3:
                    @blk.sync
                    def _(s, tb=tb):
                        st_sem.wait(s)
                        dma.inc(s.dma_start(out=cst["RC"][:],
                                            in_=src[:, :, tb * 512:(tb + 1) * 512].transpose([1, 0, 2])), 16)

                    odve_base = dve.v

                    @blk.tensor
                    def _(t, tb=tb, odve_base=odve_base):
                        dve.wait(t)
                        for mo in range(DK):
                            if mo >= 2:
                                t.wait_ge(dve.h, odve_base + mo - 1)
                            for k in range(DK):
                                ins = t.matmul(OPP[mo % 2][:], lhsT=WOUT[:, k, mo * 128:(mo + 1) * 128],
                                               rhs=YSLAB[:, k, :], start=(k == 0), stop=(k == DK - 1))
                            pe.inc(ins)

                    @blk.vector
                    def _(v, tb=tb):
                        dma.wait(v)
                        base = pe.v - DK
                        v.drain()
                        for mo in range(DK):
                            v.wait_ge(pe.h, base + mo + 1)
                            dve.inc(v.scalar_tensor_tensor(out=cst["RC"][:, mo, :], in0=OPP[mo % 2][:],
                                                           scalar=WOB[:, mo:mo + 1], in1=cst["RC"][:, mo, :],
                                                           op0=OP.add, op1=OP.add))

                    @blk.sync
                    def _(s, tb=tb):
                        dve.wait(s)
                        st_sem.inc(s.dma_start(out=dst[:, :, tb * 512:(tb + 1) * 512].transpose([1, 0, 2]),
                                               in_=cst["RC"][:]), 16)

            @blk.sync
            def _(s):
                st_sem.wait(s)


def emit_l0_attn_a(ctx, cst, src, dst, w):
    nc = ctx.nc
    Tc = 128
    NCH = T // Tc
    DI = 2048
    NDC = 16
    pe, act, dve = cst["pe"], cst["act"], cst["dve"]
    dma, wdma, st_sem = cst["dma"], cst["wdma"], cst["st"]

    with contextlib.ExitStack() as sl:
        SB = lambda b, sh, dt: sl.enter_context(nc.sbuf_tensor(ctx.name(b), sh, dt))
        WIN = SB("m_win", [128, DK, 4096], BF16)
        OUTP = SB("m_outp", [128, NDC, 1024], BF16)
        XPROJ = SB("m_xproj", [128, NDC, 96], BF16)
        DTW = SB("m_dtw", [65, DI], BF16)
        A_SB = SB("m_asb", [128, NDC, 16], F32)
        CONVW = SB("m_convw", [128, NDC, 4], F32)
        CONVB = SB("m_convb", [128, NDC], F32)
        DCOL = SB("m_dcol", [128, NDC], F32)
        XZC = SB("m_xzc", [128, NDC, Tc + 3], F32)
        SILZB = SB("m_silzb", [128, NDC, Tc], BF16)
        XC2 = SB("m_xc2", [128, NDC, Tc], F32)
        SILBF = SB("m_silbf", [128, NDC, Tc], BF16)
        DTBF = SB("m_dtbf", [65, Tc], BF16)
        BC32 = SB("m_bc32", [32, Tc], F32)
        BCFL = SB("m_bcfl", [1, 8 * Tc], F32)
        BBRD = SB("m_bbrd", [128, 8, Tc], F32)
        CBRD = SB("m_cbrd", [128, 8, Tc], F32)
        XB = SB("m_xb", [128, NDC, Tc], F32)
        DELTA = SB("m_delta", [128, NDC, Tc], F32)
        YBFC = SB("m_ybfc", [128, NDC, Tc], BF16)
        XHc = SB("m_xhc", [128, DK, Tc], BF16)
        SQ, ROWS = cst["SQ"], cst["ROWS"]
        MOC = cst["RC"][:, :, 384:384 + Tc]
        DUDC = cst["RC"][:, 0, 128:128 + Tc]
        SILF = cst["RC"][:, 0, 256:256 + Tc]
        SILZF = cst["RC"][:, 1, 128:128 + Tc]
        STATE = cst["RC"][:, 2, 128:384].rearrange("p (a b) -> p a b", b=16)
        XC2F = XC2[:].rearrange("p a b -> p (a b)")
        SCR1 = XC2F[:, 0:8 * Tc].rearrange("p (a b) -> p a b", b=Tc)
        SCR2 = XC2F[:, 8 * Tc:16 * Tc].rearrange("p (a b) -> p a b", b=Tc)
        SCR3 = XB[:].rearrange("p a b -> p (a b)")[:, 0:8 * Tc].rearrange("p (a b) -> p a b", b=Tc)

        with contextlib.ExitStack() as sp:
            P = lambda b, sh, dt=F32: sp.enter_context(nc.psum_tensor(ctx.name(b), sh, dt))
            ps = {"lnrow": P("m_lnrow", [1, 1024]),
                  "nmb": P("m_nmb", [128, 512]),
                  "rb": P("m_rb", [128, 512])}
            IP2 = [P("m_ip2", [128, 512]) for _ in range(2)]
            DBLP = P("m_dblp", [96, Tc])
            blk = sp.enter_context(nc.Block(no_gpsimd_drain=True))

            @blk.gpsimd
            def _(g):
                wdma.inc(g.dma_start(out=WIN[:], in_=w["in_proj_w"].rearrange("(k p) m -> p k m", p=128)), 16)
                wdma.inc(g.dma_start(out=OUTP[:], in_=w["out_proj_w"].rearrange("(k p) m -> p k m", p=128)), 16)
                wdma.inc(g.dma_start(out=XPROJ[:], in_=w["x_proj_w"].rearrange("(k p) m -> p k m", p=128)), 16)
                wdma.inc(g.dma_start(out=DTW[:], in_=w["dt_cat"]), 16)
                wdma.inc(g.dma_start(out=A_SB[:], in_=w["alog_fm"]), 16)
                wdma.inc(g.dma_start(out=CONVW[:], in_=w["conv_fm"]), 16)
                wdma.inc(g.dma_start(out=CONVB[:], in_=w["convb_fm"]), 16)
                wdma.inc(g.dma_start(out=DCOL[:], in_=w["d_fm"]), 16)
                cst["pool"].inc(g.memset(DTBF[64:65, :], 1.0))

            @blk.scalar
            def _(s):
                wdma.wait(s)
                act.inc(s.activation(out=A_SB[:], in_=A_SB[:], func=AF.Exp))

            @blk.vector
            def _(v):
                act.wait(v)
                dve.inc(v.tensor_scalar(out=A_SB[:], in0=A_SB[:], scalar1=-1.0,
                                        scalar2=None, op0=OP.mult))

            for direc in range(2):
                fwd = (direc == 0)
                xzo = 3 if fwd else 0

                @blk.vector
                def _(v, fwd=fwd):
                    dve.wait(v)
                    v.memset(STATE[:], 0.0)
                    if fwd:
                        ins = v.memset(XZC[:, :, 0:3], 0.0)
                    else:
                        ins = v.memset(XZC[:, :, Tc:Tc + 3], 0.0)
                    dve.inc(ins)

                chunks = range(NCH) if fwd else range(NCH - 1, -1, -1)
                for c in chunks:
                    tw = slice(c * Tc, (c + 1) * Tc)

                    @blk.sync
                    def _(s, tw=tw, c=c, fwd=fwd):
                        pe.wait(s)
                        act.wait(s)
                        dve.wait(s)
                        st_sem.wait(s)
                        dma.inc(s.dma_start(out=cst["RC"][:, :, 0:Tc],
                                            in_=src[:, :, tw].transpose([1, 0, 2])), 16)
                        if not fwd:
                            dma.inc(s.dma_start(out=MOC[:], in_=w["mo_s"][:, :, tw].transpose([1, 0, 2])), 16)

                    @blk.tensor
                    def _(t):
                        dma.wait(t)
                        dve.wait(t)

                    @blk.scalar
                    def _(s):
                        dma.wait(s)
                        dve.wait(s)

                    emit_ln(ctx, blk, cst, cst["RC"][:, :, 0:Tc], XHc[:], Tc, ps, SQ, ROWS)

                    dve_eb = dve.v
                    act_eb = act.v
                    cons = []
                    ndve = 0
                    nact = 0
                    for m in range(32):
                        if m < 16:
                            ndve += 1
                            cons.append(("d", dve_eb + ndve))
                        else:
                            nact += 1
                            cons.append(("a", act_eb + nact))

                    @blk.tensor
                    def _(t, cons=cons):
                        dve.wait(t)
                        act.wait(t)
                        for m in range(32):
                            if m >= 2:
                                kind, val = cons[m - 2]
                                t.wait_ge(dve.h if kind == "d" else act.h, val)
                            for k in range(DK):
                                ins = t.matmul(IP2[m % 2][:, 0:Tc], lhsT=WIN[:, k, m * 128:(m + 1) * 128],
                                               rhs=XHc[:, k, :], start=(k == 0), stop=(k == DK - 1))
                            pe.inc(ins)

                    g_pe_base = pe.v - 32

                    @blk.vector
                    def _(v, g_pe_base=g_pe_base):
                        for m in range(16):
                            v.wait_ge(pe.h, g_pe_base + m + 1)
                            dve.inc(v.tensor_copy(XZC[:, m, xzo:xzo + Tc], IP2[m % 2][:, 0:Tc]))

                    @blk.scalar
                    def _(s, g_pe_base=g_pe_base):
                        for m in range(16, 32):
                            s.wait_ge(pe.h, g_pe_base + m + 1)
                            act.inc(s.activation(out=SILZB[:, m - 16, :], in_=IP2[m % 2][:, 0:Tc],
                                                 func=AF.Silu))

                    @blk.vector
                    def _(v, fwd=fwd):
                        dve.wait(v)
                        if fwd:
                            wins = [0, 1, 2, 3]
                            wts = [0, 1, 2, 3]
                        else:
                            wins = [3, 2, 1, 0]
                            wts = [0, 1, 2, 3]
                        for dc in range(NDC):
                            v.tensor_scalar(out=XC2[:, dc, :], in0=XZC[:, dc, wins[0]:wins[0] + Tc],
                                            scalar1=CONVW[:, dc, wts[0]:wts[0] + 1],
                                            scalar2=CONVB[:, dc:dc + 1], op0=OP.mult, op1=OP.add)
                        for tap in range(1, 4):
                            v.drain()
                            for dc in range(NDC):
                                ins = v.scalar_tensor_tensor(out=XC2[:, dc, :],
                                                             in0=XZC[:, dc, wins[tap]:wins[tap] + Tc],
                                                             scalar=CONVW[:, dc, wts[tap]:wts[tap] + 1],
                                                             in1=XC2[:, dc, :], op0=OP.mult, op1=OP.add)
                        v.drain()
                        if fwd:
                            ins = v.tensor_copy(XZC[:, :, 0:3], XZC[:, :, Tc:Tc + 3])
                        else:
                            ins = v.tensor_copy(XZC[:, :, Tc:Tc + 3], XZC[:, :, 0:3])
                        dve.inc(ins)

                    @blk.scalar
                    def _(s):
                        dve.wait(s)
                        act.inc(s.activation(out=SILBF[:], in_=XC2[:], func=AF.Silu))


                    @blk.tensor
                    def _(t):
                        dve.wait(t)
                        act.wait(t)
                        for k in range(NDC):
                            ins = t.matmul(DBLP[:], lhsT=XPROJ[:, k, :], rhs=SILBF[:, k, :],
                                           start=(k == 0), stop=(k == NDC - 1))
                        pe.inc(ins)

                    @blk.vector
                    def _(v):
                        pe.wait(v)
                        v.tensor_copy(DTBF[0:64, :], DBLP[0:64, :])
                        dve.inc(v.tensor_copy(BC32[:], DBLP[64:96, :]))

                    @blk.sync
                    def _(s):
                        dve.wait(s)
                        st_sem.inc(s.dma_start(out=w["bc_s"][:], in_=BC32[:]), 16)

                    dve_eb2 = dve.v

                    @blk.tensor
                    def _(t, dve_eb2=dve_eb2):
                        dve.wait(t)
                        cst["pool"].wait(t)
                        for dcq in range(NDC):
                            if dcq >= 2:
                                t.wait_ge(dve.h, dve_eb2 + dcq - 1)
                            ins = t.matmul(IP2[dcq % 2][:, 0:Tc], lhsT=DTW[:, dcq * 128:(dcq + 1) * 128],
                                           rhs=DTBF[:], start=True, stop=True)
                            pe.inc(ins)

                    @blk.vector
                    def _(v):
                        base = pe.v - NDC
                        for dcq in range(NDC):
                            v.wait_ge(pe.h, base + dcq + 1)
                            dve.inc(v.tensor_copy(XB[:, dcq, :], IP2[dcq % 2][:, 0:Tc]))

                    XBF = XB[:].rearrange("p a b -> p (a b)")
                    DLF = DELTA[:].rearrange("p a b -> p (a b)")
                    SQF = SQ[:].rearrange("p a b -> p (a b)")

                    @blk.scalar
                    def _(s, XBF=XBF, DLF=DLF, SQF=SQF):
                        dve.wait(s)
                        s.activation(out=SQF[:, 0:NDC * Tc], in_=XBF, func=AF.Abs)
                        s.drain()
                        s.activation(out=DLF, in_=SQF[:, 0:NDC * Tc], func=AF.Exp, scale=-1.0)
                        s.drain()
                        act.inc(s.activation(out=SQF[:, 0:NDC * Tc], in_=DLF, func=AF.Ln, bias=1.0))

                    @blk.vector
                    def _(v, XBF=XBF, DLF=DLF, SQF=SQF):
                        act.wait(v)
                        v.tensor_scalar(out=DLF, in0=XBF, scalar1=0.0, scalar2=None, op0=OP.max)
                        v.drain()
                        dve.inc(v.tensor_tensor(DLF, DLF, SQF[:, 0:NDC * Tc], OP.add))

                    for nh in range(2):
                        @blk.sync
                        def _(s, nh=nh):
                            pe.wait(s)
                            dve.wait(s)
                            st_sem.wait(s)
                            bcf = w["bc_s"].rearrange("(u a) b -> u (a b)", u=1)
                            dma.inc(s.dma_start(out=BCFL[:], in_=bcf[:, nh * 8 * Tc:(nh + 1) * 8 * Tc]), 16)

                        @blk.tensor
                        def _(t, nh=nh):
                            dma.wait(t)
                            dve.wait(t)
                            nseg = (8 * Tc) // 512
                            for g in range(nseg):
                                ins = t.matmul(IP2[g % 2][:], lhsT=cst["ones_row"][:],
                                               rhs=BCFL[:, g * 512:(g + 1) * 512], start=True, stop=True)
                                pe.inc(ins)

                        @blk.vector
                        def _(v, nh=nh):
                            nseg = (8 * Tc) // 512
                            base = pe.v - nseg
                            bf = BBRD[:].rearrange("p n t -> p (n t)")
                            for g in range(nseg):
                                v.wait_ge(pe.h, base + g + 1)
                                dve.inc(v.tensor_copy(bf[:, g * 512:(g + 1) * 512], IP2[g % 2][:]))

                        @blk.sync
                        def _(s, nh=nh):
                            dve.wait(s)
                            bcf = w["bc_s"].rearrange("(u a) b -> u (a b)", u=1)
                            dma.inc(s.dma_start(out=BCFL[:], in_=bcf[:, (2 + nh) * 8 * Tc:(3 + nh) * 8 * Tc]), 16)

                        @blk.tensor
                        def _(t, nh=nh):
                            dma.wait(t)
                            nseg = (8 * Tc) // 512
                            for g in range(nseg):
                                ins = t.matmul(IP2[g % 2][:], lhsT=cst["ones_row"][:],
                                               rhs=BCFL[:, g * 512:(g + 1) * 512], start=True, stop=True)
                                pe.inc(ins)

                        @blk.vector
                        def _(v, nh=nh):
                            nseg = (8 * Tc) // 512
                            base = pe.v - nseg
                            cf = CBRD[:].rearrange("p n t -> p (n t)")
                            for g in range(nseg):
                                v.wait_ge(pe.h, base + g + 1)
                                dve.inc(v.tensor_copy(cf[:, g * 512:(g + 1) * 512], IP2[g % 2][:]))

                        pool = cst["pool"]
                        pre_dve = dve.v
                        pre_pool = pool.v
                        YROW = cst["RC"][:, 1, 256:256 + Tc]
                        Y2R = cst["RC"][:, 1, 128:128 + Tc]
                        scan_done = []
                        y_done = []
                        arg_done = []
                        exp_done = []
                        hc_done = []
                        for dc in range(NDC):
                            u = dc

                            @blk.gpsimd
                            def _(g, dc=dc, nh=nh, u=u):
                                g.wait_ge(dve.h, scan_done[u - 1] if u > 0 else pre_dve)
                                d_b = DELTA[:, dc, :].unsqueeze(1).broadcast_to([128, 8, Tc])
                                a_b = A_SB[:, dc, nh * 8:(nh + 1) * 8].unsqueeze(2).broadcast_to([128, 8, Tc])
                                g.tensor_tensor(SCR1[:], d_b, a_b, OP.mult)
                                pool.inc(g.tensor_tensor(DUDC, DELTA[:, dc, :], SILBF[:, dc, :], OP.mult))

                            arg_done.append(pool.v)

                            @blk.scalar
                            def _(s, dc=dc, nh=nh, u=u):
                                s.wait_ge(pool.h, arg_done[u])
                                s.wait_ge(dve.h, y_done[u - 1] if u > 0 else pre_dve)
                                act.inc(s.activation(out=SCR2[:], in_=SCR1[:], func=AF.Exp))

                            exp_done.append(act.v)

                            @blk.vector
                            def _(v, dc=dc, nh=nh, fwd=fwd, u=u):
                                v.wait_ge(act.h, exp_done[u])
                                v.wait_ge(pool.h, hc_done[u - 1] if u > 0 else pre_pool)
                                du_b = DUDC.unsqueeze(1).broadcast_to([128, 8, Tc])
                                v.tensor_tensor(SCR1[:], du_b, BBRD[:], OP.mult)
                                v.drain()
                                for n in range(8):
                                    if fwd:
                                        v.tensor_tensor_scan(SCR3[:, n, :], SCR2[:, n, :], SCR1[:, n, :],
                                                             STATE[:, dc, nh * 8 + n:nh * 8 + n + 1], OP.mult, OP.add)
                                    else:
                                        v.tensor_tensor_scan(SCR3[:, n, ::-1], SCR2[:, n, ::-1], SCR1[:, n, ::-1],
                                                             STATE[:, dc, nh * 8 + n:nh * 8 + n + 1], OP.mult, OP.add)
                                v.drain()
                                if fwd:
                                    ins = v.tensor_copy(STATE[:, dc, nh * 8:(nh + 1) * 8], SCR3[:, :, Tc - 1])
                                else:
                                    ins = v.tensor_copy(STATE[:, dc, nh * 8:(nh + 1) * 8], SCR3[:, :, 0])
                                dve.inc(ins)

                            scan_done.append(dve.v)

                            @blk.gpsimd
                            def _(g, dc=dc, nh=nh, u=u):
                                g.wait_ge(dve.h, scan_done[u])
                                pool.inc(g.tensor_tensor(SCR2[:], SCR3[:], CBRD[:], OP.mult))

                            hc_done.append(pool.v)

                            @blk.vector
                            def _(v, dc=dc, nh=nh, u=u):
                                v.wait_ge(pool.h, hc_done[u])
                                v.tensor_reduce(YROW, SCR2[:].transpose([0, 2, 1]), axis=AX.X, op=OP.add)
                                v.drain()
                                if nh == 0:
                                    dve.inc(v.tensor_tensor(YBFC[:, dc, :], YROW, SILZB[:, dc, :], OP.mult))
                                else:
                                    v.scalar_tensor_tensor(out=Y2R, in0=SILBF[:, dc, :], scalar=DCOL[:, dc:dc + 1],
                                                           in1=YROW, op0=OP.mult, op1=OP.add)
                                    v.drain()
                                    dve.inc(v.tensor_tensor(YBFC[:, dc, :], Y2R, SILZB[:, dc, :], OP.mult))

                            y_done.append(dve.v)

                        dve_eb3 = dve.v

                        @blk.tensor
                        def _(t, dve_eb3=dve_eb3):
                            dve.wait(t)
                            for mo in range(DK):
                                if mo >= 2:
                                    t.wait_ge(dve.h, dve_eb3 + mo - 1)
                                for k in range(NDC):
                                    ins = t.matmul(IP2[mo % 2][:, 0:Tc], lhsT=OUTP[:, k, mo * 128:(mo + 1) * 128],
                                                   rhs=YBFC[:, k, :], start=(k == 0), stop=(k == NDC - 1))
                                pe.inc(ins)

                        @blk.vector
                        def _(v, fwd=fwd, nh=nh):
                            base = pe.v - DK
                            dma.wait(v)
                            st_sem.wait(v)
                            for mo in range(DK):
                                v.wait_ge(pe.h, base + mo + 1)
                                if fwd and nh == 0:
                                    dve.inc(v.tensor_copy(MOC[:, mo, :], IP2[mo % 2][:, 0:Tc]))
                                else:
                                    dve.inc(v.tensor_tensor(MOC[:, mo, :], MOC[:, mo, :], IP2[mo % 2][:, 0:Tc], OP.add))

                    @blk.sync
                    def _(s, tw=tw):
                        dve.wait(s)
                        st_sem.inc(s.dma_start(out=w["mo_s"][:, :, tw].transpose([1, 0, 2]), in_=MOC[:]), 16)

            @blk.sync
            def _(s):
                st_sem.wait(s)

    with contextlib.ExitStack() as sl:
        SB = lambda b, sh, dt: sl.enter_context(nc.sbuf_tensor(ctx.name(b), sh, dt))
        FFW1 = SB("m_ffw1", [128, DK, 4096], BF16)
        FFW2 = SB("m_ffw2", [128, 32, 1024], BF16)
        FFB1 = SB("m_ffb1", [128, 32], F32)
        FFB2 = SB("m_ffb2", [128, DK], F32)
        LNG = SB("m_lng", [128, DK], F32)
        LNB = SB("m_lnb", [128, DK], F32)
        MOC = SB("m_moc2", [128, DK, 256], F32)
        XHF = SB("m_xhf", [128, DK, 256], BF16)
        H1F = SB("m_h1f", [128, 32, 256], BF16)
        SQ, ROWS = cst["SQ"], cst["ROWS"]

        with contextlib.ExitStack() as sp:
            P = lambda b, sh, dt=F32: sp.enter_context(nc.psum_tensor(ctx.name(b), sh, dt))
            ps = {"lnrow": P("f_lnrow", [1, 1024]),
                  "nmb": P("f_nmb", [128, 512]),
                  "rb": P("f_rb", [128, 512])}
            GP = [P("f_gp", [128, 512]) for _ in range(2)]
            blk = sp.enter_context(nc.Block(no_gpsimd_drain=True))

            @blk.gpsimd
            def _(g):
                wdma.inc(g.dma_start(out=FFW1[:], in_=w["ff_w1"].rearrange("(k p) m -> p k m", p=128)), 16)
                wdma.inc(g.dma_start(out=FFW2[:], in_=w["ff_w2"].rearrange("(k p) m -> p k m", p=128)), 16)
                wdma.inc(g.dma_start(out=FFB1[:], in_=w["ffb1_fm"]), 16)
                wdma.inc(g.dma_start(out=FFB2[:], in_=w["ffb2_fm"]), 16)
                wdma.inc(g.dma_start(out=LNG[:], in_=w["lng_fm"]), 16)
                wdma.inc(g.dma_start(out=LNB[:], in_=w["lnb_fm"]), 16)

            for c in range(8):
                tw = slice(c * 256, (c + 1) * 256)

                @blk.sync
                def _(s, tw=tw):
                    pe.wait(s)
                    act.wait(s)
                    dve.wait(s)
                    st_sem.wait(s)
                    dma.inc(s.dma_start(out=MOC[:], in_=w["mo_s"][:, :, tw].transpose([1, 0, 2])), 16)
                    dma.inc(s.dma_start(out=cst["RC"][:, :, 0:256],
                                        in_=src[:, :, tw].transpose([1, 0, 2])), 16)

                @blk.tensor
                def _(t):
                    dma.wait(t)
                    dve.wait(t)
                    wdma.wait(t)

                @blk.scalar
                def _(s):
                    dma.wait(s)
                    dve.wait(s)

                emit_ln(ctx, blk, cst, MOC[:], XHF[:], 256, ps, SQ, ROWS,
                        gcol=LNG, bcol=LNB)

                act_b1 = act.v

                @blk.tensor
                def _(t, act_b1=act_b1):
                    dve.wait(t)
                    act.wait(t)
                    for m in range(32):
                        if m >= 2:
                            t.wait_ge(act.h, act_b1 + m - 1)
                        for k in range(DK):
                            ins = t.matmul(GP[m % 2][:, 0:256], lhsT=FFW1[:, k, m * 128:(m + 1) * 128],
                                           rhs=XHF[:, k, :], start=(k == 0), stop=(k == DK - 1))
                        pe.inc(ins)

                @blk.scalar
                def _(s):
                    base = pe.v - 32
                    for m in range(32):
                        s.wait_ge(pe.h, base + m + 1)
                        act.inc(s.activation(out=H1F[:, m, :], in_=GP[m % 2][:, 0:256],
                                             func=AF.Gelu, bias=FFB1[:, m:m + 1]))

                dve_b2 = dve.v

                @blk.tensor
                def _(t, dve_b2=dve_b2):
                    act.wait(t)
                    dve.wait(t)
                    for mo in range(DK):
                        if mo >= 2:
                            t.wait_ge(dve.h, dve_b2 + mo - 1)
                        for k in range(32):
                            ins = t.matmul(GP[mo % 2][:, 0:256], lhsT=FFW2[:, k, mo * 128:(mo + 1) * 128],
                                           rhs=H1F[:, k, :], start=(k == 0), stop=(k == 31))
                        pe.inc(ins)

                @blk.vector
                def _(v):
                    base = pe.v - DK
                    for mo in range(DK):
                        v.wait_ge(pe.h, base + mo + 1)
                        dve.inc(v.scalar_tensor_tensor(out=cst["RC"][:, mo, 0:256], in0=GP[mo % 2][:, 0:256],
                                                       scalar=FFB2[:, mo:mo + 1], in1=cst["RC"][:, mo, 0:256],
                                                       op0=OP.add, op1=OP.add))

                @blk.sync
                def _(s, tw=tw):
                    dve.wait(s)
                    st_sem.inc(s.dma_start(out=dst[:, :, tw].transpose([1, 0, 2]),
                                           in_=cst["RC"][:, :, 0:256]), 16)

            @blk.sync
            def _(s):
                st_sem.wait(s)


S7_PHASES = "ABCDE"
S7_D_LEVEL = 6
S7_SC_LEVEL = 7


def emit_l1_attn_b(ctx, cst, src, dst, w):
    nc = ctx.nc
    H = 16
    HD = 64
    Tc = 128
    NCH = T // Tc
    SC = 64
    NSC = Tc // SC
    pe, act, dve = cst["pe"], cst["act"], cst["dve"]
    dma, wdma, st_sem = cst["dma"], cst["wdma"], cst["st"]
    SQ, ROWS = cst["SQ"], cst["ROWS"]

    with contextlib.ExitStack() as sl:
        SB = lambda b, sh, dt: sl.enter_context(nc.sbuf_tensor(ctx.name(b), sh, dt))
        DBALL = SB("b7_dball", [64, T], F32)
        P20N = SB("b7_p20n", [128, DK, T], BF16)

        if "A" in S7_PHASES:
         with contextlib.ExitStack() as sp:
            P = lambda b, sh, dt=F32: sp.enter_context(nc.psum_tensor(ctx.name(b), sh, dt))
            SB2 = lambda b, sh, dt: sp.enter_context(nc.sbuf_tensor(ctx.name(b), sh, dt))
            CP = [P("b7_cp", [128, 512]) for _ in range(2)]
            ps = {"lnrow": P("b7_lnrow", [1, 1024]), "nmb": CP[0][:, 0:Tc], "rb": CP[1][:, 0:Tc]}
            DBP = P("b7_dbp", [64, Tc])
            CTRLW = SB2("b7_ctrlw", [128, DK, 4160], BF16)
            CBQ = SB2("b7_cbq", [128, 24], F32)
            CBG = SB2("b7_cbg", [128, 8], F32)
            CBDB = SB2("b7_cbdb", [64, 1], F32)
            QCONV = SB2("b7_qconv", [128, 24, 4], F32)
            XHc = SB2("b7_xhc", [128, DK, Tc], BF16)
            QKV = SB2("b7_qkv", [128, 24, Tc + 3], F32)
            QKVC = SB2("b7_qkvc", [128, 24, Tc], F32)
            SGC = SB2("b7_sgc", [128, 8, Tc], BF16)
            blk = sp.enter_context(nc.Block(no_gpsimd_drain=True))

            @blk.gpsimd
            def _(g):
                wdma.inc(g.dma_start(out=CTRLW[:], in_=w["ctrlw_re"].rearrange("(k p) m -> p k m", p=128)), 16)
                wdma.inc(g.dma_start(out=CBQ[:], in_=w["cb_qkv"]), 16)
                wdma.inc(g.dma_start(out=CBG[:], in_=w["cb_gate"]), 16)
                wdma.inc(g.dma_start(out=CBDB[:], in_=w["cb_db"]), 16)
                wdma.inc(g.dma_start(out=QCONV[:], in_=w["conv3_fm"]), 16)

            TAIL = SB2("b7_tail", [128, 24, 3], F32)

            @blk.vector
            def _(v):
                dve.inc(v.memset(TAIL[:], 0.0))

            for c in range(NCH):
                tw = slice(c * Tc, (c + 1) * Tc)

                @blk.sync
                def _(s, tw=tw):
                    pe.wait(s)
                    act.wait(s)
                    dve.wait(s)
                    st_sem.wait(s)
                    dma.inc(s.dma_start(out=cst["RC"][:, :, 0:Tc],
                                        in_=src[:, :, tw].transpose([1, 0, 2])), 16)

                @blk.tensor
                def _(t):
                    dma.wait(t)
                    dve.wait(t)
                    wdma.wait(t)

                @blk.scalar
                def _(s):
                    dma.wait(s)
                    dve.wait(s)

                emit_ln(ctx, blk, cst, cst["RC"][:, :, 0:Tc], XHc[:], Tc, ps, SQ, ROWS)

                @blk.vector
                def _(v):
                    dve.wait(v)
                    act.wait(v)
                    dve.inc(v.tensor_copy(QKV[:, :, 0:3], TAIL[:]))

                dve_eb = dve.v
                act_eb = act.v
                cons = []
                nd = 0
                na = 0
                for m in range(33):
                    if m < 24 or m == 32:
                        nd += 1
                        cons.append(("d", dve_eb + nd))
                    else:
                        na += 1
                        cons.append(("a", act_eb + na))

                @blk.tensor
                def _(t, cons=cons):
                    dve.wait(t)
                    act.wait(t)
                    for m in range(33):
                        if m >= 2:
                            kind, val = cons[m - 2]
                            t.wait_ge(dve.h if kind == "d" else act.h, val)
                        dstp = DBP[:] if m == 32 else CP[m % 2][:, 0:Tc]
                        wslice = CTRLW[:, :, m * 128:m * 128 + 64] if m == 32 else CTRLW[:, :, m * 128:(m + 1) * 128]
                        for k in range(DK):
                            ins = t.matmul(dstp, lhsT=wslice[:, k, :], rhs=XHc[:, k, :],
                                           start=(k == 0), stop=(k == DK - 1))
                        pe.inc(ins)

                g_pe = pe.v - 33

                @blk.vector
                def _(v, g_pe=g_pe):
                    for m in range(24):
                        v.wait_ge(pe.h, g_pe + m + 1)
                        dve.inc(v.tensor_scalar(out=QKV[:, m, 3:3 + Tc], in0=CP[m % 2][:, 0:Tc],
                                                scalar1=CBQ[:, m:m + 1], scalar2=None, op0=OP.add))
                    v.wait_ge(pe.h, g_pe + 33)
                    dve.inc(v.tensor_scalar(out=DBALL[:, c * Tc:(c + 1) * Tc], in0=DBP[:],
                                            scalar1=CBDB[:], scalar2=None, op0=OP.add))

                @blk.scalar
                def _(s, g_pe=g_pe):
                    for m in range(24, 32):
                        s.wait_ge(pe.h, g_pe + m + 1)
                        act.inc(s.activation(out=SGC[:, m - 24, :], in_=CP[m % 2][:, 0:Tc],
                                             func=AF.Sigmoid, bias=CBG[:, m - 24:m - 24 + 1]))

                @blk.vector
                def _(v):
                    dve.wait(v)
                    for m in range(24):
                        v.tensor_scalar(out=QKVC[:, m, :], in0=QKV[:, m, 0:Tc],
                                        scalar1=QCONV[:, m, 0:1], scalar2=None, op0=OP.mult)
                    for tap in range(1, 4):
                        v.drain()
                        for m in range(24):
                            ins = v.scalar_tensor_tensor(out=QKVC[:, m, :],
                                                         in0=QKV[:, m, tap:tap + Tc],
                                                         scalar=QCONV[:, m, tap:tap + 1],
                                                         in1=QKVC[:, m, :], op0=OP.mult, op1=OP.add)
                    v.drain()
                    dve.inc(v.tensor_copy(TAIL[:], QKV[:, :, Tc:Tc + 3]))

                @blk.scalar
                def _(s):
                    dve.wait(s)
                    act.inc(s.activation(out=QKV[:, :, 3:3 + Tc], in_=QKVC[:], func=AF.Silu))

                @blk.sync
                def _(s, tw=tw, c=c):
                    act.wait(s)
                    st_sem.inc(s.dma_start(out=w["qkvc_s"][:, :, tw].transpose([1, 0, 2]),
                                           in_=QKV[:, :, 3:3 + Tc]), 16)
                    st_sem.inc(s.dma_start(out=w["sg_s"][:, :, tw].transpose([1, 0, 2]), in_=SGC[:]), 16)

            @blk.sync
            def _(s):
                st_sem.wait(s)

        if "B" in S7_PHASES:
         with contextlib.ExitStack() as spb:
            P20 = spb.enter_context(nc.sbuf_tensor(ctx.name("b7_p20"), [128, DK, T], F32))
            with contextlib.ExitStack() as sp:
                P = lambda b, sh, dt=F32: sp.enter_context(nc.psum_tensor(ctx.name(b), sh, dt))
                SB2 = lambda b, sh, dt: sp.enter_context(nc.sbuf_tensor(ctx.name(b), sh, dt))
                ps = {"lnrow": P("b7_lnrow2", [1, 1024]), "nmb": P("b7_nmb2", [128, 512]),
                      "rb": P("b7_rb2", [128, 512])}
                SWP = P("b7_swp", [128, 2048])
                AROW = SB2("b7_arow", [128, 8], F32)
                AR2 = SB2("b7_ar2", [128, 8], F32)
                AI2 = SB2("b7_ai2", [128, 8], F32)
                ARP = SB2("b7_arp", [128, 11, 8], F32)
                AIP = SB2("b7_aip", [128, 11, 8], F32)
                SBB = SB2("b7_sbb", [128, 2048], F32)
                blk = sp.enter_context(nc.Block(no_gpsimd_drain=True))

                @blk.gpsimd
                def _(g):
                    wdma.inc(g.dma_start(out=AROW[:], in_=w["dl2_fm"]), 16)
                    wdma.inc(g.dma_start(out=AR2[:], in_=w["th2_fm"]), 16)
                    wdma.inc(g.dma_start(out=AI2[:], in_=w["nsign_fm"]), 16)

                @blk.scalar
                def _(s):
                    wdma.wait(s)
                    s.activation(out=AROW[:], in_=AROW[:], func=AF.Exp)
                    s.drain()
                    s.activation(out=AROW[:], in_=AROW[:], func=AF.Exp, scale=-1.0)
                    s.drain()
                    act.inc(s.activation(out=SQ[:, 1, 0:8], in_=AR2[:], func=AF.Sin, scale=0.5))

                @blk.vector
                def _(v):
                    act.wait(v)
                    dve.inc(v.tensor_scalar(out=SQ[:, 2, 0:8], in0=AR2[:], scalar1=0.5,
                                            scalar2=-float(np.pi) / 2.0, op0=OP.mult, op1=OP.add))

                @blk.scalar
                def _(s):
                    dve.wait(s)
                    act.inc(s.activation(out=SQ[:, 0, 0:8], in_=SQ[:, 2, 0:8], func=AF.Sin))

                @blk.vector
                def _(v):
                    act.wait(v)
                    sh = SQ[:, 0, 0:8]
                    ch = SQ[:, 1, 0:8]
                    v.tensor_tensor(SQ[:, 2, 0:8], sh, ch, OP.mult)
                    v.tensor_tensor(SQ[:, 3, 0:8], sh, sh, OP.mult)
                    v.drain()
                    v.tensor_scalar(out=SQ[:, 2, 0:8], in0=SQ[:, 2, 0:8], scalar1=2.0,
                                    scalar2=None, op0=OP.mult)
                    v.tensor_scalar(out=SQ[:, 3, 0:8], in0=SQ[:, 3, 0:8], scalar1=-2.0,
                                    scalar2=1.0, op0=OP.mult, op1=OP.add)
                    v.drain()
                    v.scalar_tensor_tensor(out=AR2[:], in0=AROW[:], scalar=-1.0, in1=SQ[:, 3, 0:8],
                                           op0=OP.mult, op1=OP.mult)
                    v.tensor_tensor(SQ[:, 2, 0:8], AROW[:], SQ[:, 2, 0:8], OP.mult)
                    v.drain()
                    v.tensor_tensor(AI2[:], AI2[:], SQ[:, 2, 0:8], OP.mult)
                    v.drain()
                    v.tensor_copy(ARP[:, 0, :], AR2[:])
                    v.tensor_copy(AIP[:, 0, :], AI2[:])
                    for kk in range(1, 11):
                        v.drain()
                        v.tensor_tensor(SQ[:, 4, 0:8], ARP[:, kk - 1, :], ARP[:, kk - 1, :], OP.mult)
                        v.tensor_tensor(SQ[:, 5, 0:8], AIP[:, kk - 1, :], AIP[:, kk - 1, :], OP.mult)
                        v.tensor_tensor(SQ[:, 6, 0:8], ARP[:, kk - 1, :], AIP[:, kk - 1, :], OP.mult)
                        v.drain()
                        v.tensor_tensor(ARP[:, kk, :], SQ[:, 4, 0:8], SQ[:, 5, 0:8], OP.subtract)
                        v.tensor_scalar(out=AIP[:, kk, :], in0=SQ[:, 6, 0:8], scalar1=2.0,
                                        scalar2=None, op0=OP.mult)
                    v.drain()
                    dve.inc(v.tensor_copy(SQ[:, 7, 0:8], ARP[:, 10, :]))

                for tb in range(4):
                    tw = slice(tb * 512, (tb + 1) * 512)

                    @blk.sync
                    def _(s, tw=tw):
                        pe.wait(s)
                        act.wait(s)
                        dve.wait(s)
                        dma.inc(s.dma_start(out=cst["RC"][:], in_=src[:, :, tw].transpose([1, 0, 2])), 16)

                    @blk.tensor
                    def _(t):
                        dma.wait(t)
                        dve.wait(t)

                    @blk.scalar
                    def _(s):
                        dma.wait(s)
                        dve.wait(s)

                    emit_ln(ctx, blk, cst, cst["RC"][:], P20[:, :, tw], 512, ps, SQ, ROWS)

                for dk in range(DK):
                    for p in range(11):
                        delta = 1 << p
                        srcb = P20[:, dk, :] if p % 2 == 0 else SBB[:]
                        dstb = SBB[:] if p % 2 == 0 else P20[:, dk, :]

                        @blk.tensor
                        def _(t, srcb=srcb, p=p, dk=dk):
                            dve.wait(t)
                            for g in range(4):
                                ins = t.matmul(SWP[:, g * 512:(g + 1) * 512], lhsT=cst["perm_f32"][:],
                                               rhs=srcb[:, g * 512:(g + 1) * 512], start=True, stop=True)
                            pe.inc(ins)

                        @blk.vector
                        def _(v, srcb=srcb, dstb=dstb, delta=delta, dk=dk, p=p):
                            pe.wait(v)
                            v.drain()
                            v.tensor_copy(dstb[:, 0:delta], srcb[:, 0:delta])
                            v.drain()
                            v.scalar_tensor_tensor(out=dstb[:, delta:T], in0=srcb[:, 0:T - delta],
                                                   scalar=ARP[:, p, dk:dk + 1], in1=srcb[:, delta:T],
                                                   op0=OP.mult, op1=OP.add)
                            v.drain()
                            dve.inc(v.scalar_tensor_tensor(out=dstb[:, delta:T], in0=SWP[:, 0:T - delta],
                                                           scalar=AIP[:, p, dk:dk + 1], in1=dstb[:, delta:T],
                                                           op0=OP.mult, op1=OP.add))

                    @blk.vector
                    def _(v, dk=dk):
                        v.drain()
                        dve.inc(v.tensor_copy(P20[:, dk, :], SBB[:]))

            with contextlib.ExitStack() as sp:
                P = lambda b, sh, dt=F32: sp.enter_context(nc.psum_tensor(ctx.name(b), sh, dt))
                RRP = P("b7_rrp", [1, 1024])
                RBB = P("b7_rbb", [128, 512])
                VNC = sp.enter_context(nc.sbuf_tensor(ctx.name("b7_vnc"), [128, 8], F32))
                blk = sp.enter_context(nc.Block(no_gpsimd_drain=True))

                @blk.gpsimd
                def _(g):
                    wdma.inc(g.dma_start(out=VNC[:], in_=w["vnorm_fm"]), 16)

                for tb in range(4):
                    tw = slice(tb * 512, (tb + 1) * 512)

                    @blk.scalar
                    def _(s, tw=tw):
                        dve.wait(s)
                        act.inc(s.activation(out=SQ[:], in_=P20[:, :, tw], func=AF.Square))

                    @blk.tensor
                    def _(t, tw=tw):
                        act.wait(t)
                        dve.wait(t)
                        for k in range(DK):
                            ins = t.matmul(RRP[:, 0:512], lhsT=cst["ones_col"][:], rhs=SQ[:, k, :],
                                           start=(k == 0), stop=(k == DK - 1))
                        pe.inc(ins)

                    @blk.vector
                    def _(v):
                        pe.wait(v)
                        v.tensor_scalar(out=ROWS[:, 0:512], in0=RRP[:, 0:512], scalar1=1.0 / D,
                                        scalar2=1e-6, op0=OP.mult, op1=OP.add)
                        v.drain()
                        dve.inc(v.reciprocal(ROWS[:, 512:1024], ROWS[:, 0:512]))

                    @blk.scalar
                    def _(s):
                        dve.wait(s)
                        act.inc(s.activation(out=ROWS[:, 0:512], in_=ROWS[:, 512:1024], func=AF.Sqrt))

                    @blk.tensor
                    def _(t):
                        act.wait(t)
                        pe.inc(t.matmul(RBB[:], lhsT=cst["ones_row"][:], rhs=ROWS[:, 0:512],
                                        start=True, stop=True))

                    @blk.vector
                    def _(v, tw=tw):
                        pe.wait(v)
                        wdma.wait(v)
                        for k in range(DK):
                            v.tensor_tensor(SQ[:, k, :], P20[:, k, tw], RBB[:], OP.mult)
                        v.drain()
                        for k in range(DK):
                            ins = v.tensor_scalar(out=P20N[:, k, tw], in0=SQ[:, k, :],
                                                  scalar1=VNC[:, k:k + 1], scalar2=None, op0=OP.mult)
                        dve.inc(ins)

        QHB = SB("b7_qhb", [128, DK, T], BF16)
        KHB = SB("b7_khb", [128, DK, T], BF16)
        if "C" in S7_PHASES:
         with contextlib.ExitStack() as sp:
            P = lambda b, sh, dt=F32: sp.enter_context(nc.psum_tensor(ctx.name(b), sh, dt))
            SB2 = lambda b, sh, dt: sp.enter_context(nc.sbuf_tensor(ctx.name(b), sh, dt))
            PQ = [P("b7_pq", [128, 512]) for _ in range(2)]
            SSQP = P("b7_ssqp", [2, 1024])
            RNBP = P("b7_rnbp", [128, 1024])
            P20CW = SB2("b7_p20cw", [128, DK, 3072], BF16)
            RAMP = SB2("b7_ramp", [128, 8], F32)
            QKVC = SB2("b7_qkvc2", [128, 8, Tc], BF16)
            QQ = SB2("b7_qq", [128, 8, Tc], F32)
            VBC = SB2("b7_vbc", [128, 8, Tc], BF16)
            blk = sp.enter_context(nc.Block(no_gpsimd_drain=True))

            @blk.gpsimd
            def _(g):
                wdma.inc(g.dma_start(out=P20CW[:], in_=w["p20c_W"].rearrange("(k p) m -> p k m", p=128)), 16)
                wdma.inc(g.dma_start(out=RAMP[:], in_=w["rampl_fm"]), 16)

            @blk.scalar
            def _(s):
                wdma.wait(s)
                act.inc(s.activation(out=RAMP[:], in_=RAMP[:], func=AF.Sigmoid))

            for c in range(NCH):
                tw = slice(c * Tc, (c + 1) * Tc)

                for grp in range(3):
                    mof = grp * 8

                    @blk.gpsimd
                    def _(g, tw=tw, mof=mof):
                        pe.wait(g)
                        act.wait(g)
                        dve.wait(g)
                        st_sem.wait(g)
                        wdma.inc(g.dma_start(out=QKVC[:],
                                             in_=w["qkvc_s"][mof:mof + 8, :, tw].transpose([1, 0, 2])), 16)

                    dve_eb = dve.v

                    @blk.tensor
                    def _(t, tw=tw, mof=mof, dve_eb=dve_eb):
                        dma.wait(t)
                        wdma.wait(t)
                        dve.wait(t)
                        act.wait(t)
                        for m in range(8):
                            if m >= 2:
                                t.wait_ge(dve.h, dve_eb + m - 1)
                            for k in range(DK):
                                ins = t.matmul(PQ[m % 2][:, 0:Tc],
                                               lhsT=P20CW[:, k, (mof + m) * 128:(mof + m + 1) * 128],
                                               rhs=P20N[:, k, tw], start=(k == 0), stop=(k == DK - 1))
                            pe.inc(ins)

                    @blk.vector
                    def _(v, mof=mof):
                        base = pe.v - 8
                        for m in range(8):
                            v.wait_ge(pe.h, base + m + 1)
                            dve.inc(v.scalar_tensor_tensor(out=QQ[:, m, :], in0=PQ[m % 2][:, 0:Tc],
                                                           scalar=RAMP[:, m:m + 1],
                                                           in1=QKVC[:, m, :], op0=OP.mult, op1=OP.add))

                    if grp < 2:
                        @blk.scalar
                        def _(s):
                            dve.wait(s)
                            act.inc(s.activation(out=SQ[:].rearrange("p a b -> p (a b)")[:, 0:8 * Tc],
                                                 in_=QQ[:].rearrange("p a b -> p (a b)"),
                                                 func=AF.Square))

                        @blk.tensor
                        def _(t):
                            act.wait(t)
                            dve.wait(t)
                            sqf = SQ[:].rearrange("p a b -> p (a b)")
                            t.matmul(SSQP[:, 0:512], lhsT=cst["selh"][:], rhs=sqf[:, 0:512],
                                     start=True, stop=True)
                            pe.inc(t.matmul(SSQP[:, 512:1024], lhsT=cst["selh"][:], rhs=sqf[:, 512:1024],
                                            start=True, stop=True))

                        @blk.vector
                        def _(v):
                            pe.wait(v)
                            sq2a = SQ[0:2, 0:2, :].rearrange("p a b -> p (a b)")
                            sq2b = SQ[0:2, 2:4, :].rearrange("p a b -> p (a b)")
                            v.tensor_scalar(out=sq2a, in0=SSQP[:], scalar1=1e-6,
                                            scalar2=None, op0=OP.add)
                            v.drain()
                            dve.inc(v.reciprocal(sq2b, sq2a))

                        @blk.scalar
                        def _(s):
                            dve.wait(s)
                            act.inc(s.activation(out=SQ[0:2, 0:2, :].rearrange("p a b -> p (a b)"),
                                                 in_=SQ[0:2, 2:4, :].rearrange("p a b -> p (a b)"),
                                                 func=AF.Sqrt))

                        @blk.tensor
                        def _(t):
                            act.wait(t)
                            sq2a = SQ[0:2, 0:2, :].rearrange("p a b -> p (a b)")
                            t.matmul(RNBP[:, 0:512], lhsT=cst["selb2"][:], rhs=sq2a[:, 0:512],
                                     start=True, stop=True)
                            pe.inc(t.matmul(RNBP[:, 512:1024], lhsT=cst["selb2"][:], rhs=sq2a[:, 512:1024],
                                            start=True, stop=True))

                        @blk.vector
                        def _(v, grp=grp, tw=tw):
                            pe.wait(v)
                            dst_b = QHB if grp == 0 else KHB
                            rnb = RNBP[:].rearrange("p (a b) -> p a b", b=Tc)
                            for mm_ in range(8):
                                ins = v.tensor_tensor(dst_b[:, mm_, tw], QQ[:, mm_, :],
                                                      rnb[:, mm_, :], OP.mult)
                            dve.inc(ins)
                    else:
                        @blk.vector
                        def _(v, tw=tw):
                            dve.wait(v)
                            st_sem.wait(v)
                            for mm_ in range(8):
                                ins = v.tensor_copy(VBC[:, mm_, :], QQ[:, mm_, :])
                            dve.inc(ins)

                        @blk.sync
                        def _(s, tw=tw):
                            dve.wait(s)
                            st_sem.inc(s.dma_start(out=w["v_s"][:, :, tw].transpose([1, 0, 2]),
                                                   in_=VBC[:]), 16)

            @blk.sync
            def _(s):
                st_sem.wait(s)

        if "D" in S7_PHASES:
         with contextlib.ExitStack() as sp:
            P = lambda b, sh, dt=F32: sp.enter_context(nc.psum_tensor(ctx.name(b), sh, dt))
            SB2 = lambda b, sh, dt: sp.enter_context(nc.sbuf_tensor(ctx.name(b), sh, dt))
            GBP = P("b7_gbp", [64, 1024])
            TRP2 = P("b7_trp2", [64, 64])
            GPS = [P("b7_gps", [64, 512]) for _ in range(2)]
            GPB = P("b7_gpb", [64, 512], BF16)
            OD2 = [P("b7_od2", [128, 128]) for _ in range(2)]
            MSB = SB2("b7_msb", [64, 64], F32)
            ONES16 = SB2("b7_ones16", [16, Tc], F32)
            SST = SB2("b7_sst", [64, H, 64], F32)
            SSTB = SB2("b7_sstb", [64, H, 64], BF16)
            GROWS = SB2("b7_grows", [16, Tc], F32)
            BROWS = SB2("b7_brows", [16, Tc], F32)
            GF = SB2("b7_gf", [1, 16 * 65], F32)
            BF_ = SB2("b7_bf", [1, 16 * SC], F32)
            GSBS = SB2("b7_gsbs", [64, 16, SC], F32)
            BSBS = SB2("b7_bsbs", [64, 16, SC], F32)
            CSTS = SB2("b7_csts", [64, 16], F32)
            BTTS = SB2("b7_btts", [64, 16], F32)
            PREVB = SB2("b7_prevb", [64, 16], F32)
            ENDR = SB2("b7_endr", [64, 96], F32)
            DIFF = SQ[0:64, 4:6, :].rearrange("p a b -> p (a b)").rearrange("p (a b) -> p a b", b=SC)
            DIFFT = SQ[0:64, 6:8, :].rearrange("p a b -> p (a b)").rearrange("p (a b) -> p a b", b=SC)
            X1R = SQ[0:64, 0:2, :].rearrange("p a b -> p (a b)").rearrange("p (a b) -> p a b", b=SC)
            X2R = SQ[0:64, 2:4, :].rearrange("p a b -> p (a b)").rearrange("p (a b) -> p a b", b=SC)
            DECS = DIFF
            DECTS = DIFFT
            DECTI = X1R
            TFS = X2R
            OALL = X1R
            LAB = SB2("b7_lab", [64, 16, SC], BF16)
            LTAB = SB2("b7_ltab", [64, 16, SC], BF16)
            MTAB = SB2("b7_mtab", [64, 16, SC], BF16)
            LPOW = SB2("b7_lpow", [64, 9, SC], BF16)
            XU = [SB2("b7_xu", [64, SC], BF16) for _ in range(2)]
            KTOK = SB2("b7_ktok", [64, SC], BF16)
            KALL0 = SB2("b7_kall0", [64, H, SC], BF16)
            QALL0 = SB2("b7_qall0", [64, H, SC], BF16)
            VSL0 = SB2("b7_vsl0", [64, SC], BF16)
            VTOK = SB2("b7_vtok", [64, SC], F32)
            DT0 = SB2("b7_dt0", [64, SC], F32)
            ONB = SB2("b7_onb", [64, H, SC], BF16)
            NRF = SB2("b7_nrf", [128, DK, Tc], BF16)
            VC = SB2("b7_vc", [128, 8, Tc], BF16)
            blk = sp.enter_context(nc.Block(no_gpsimd_drain=True))

            @blk.gpsimd
            def _(g):
                wdma.inc(g.dma_start(out=cst["mnorm_row"][:], in_=w["mnorm_in"]), 16)
                g.memset(ONES16[:], 1.0)
                cst["pool"].inc(g.memset(SST[:], 0.0))

            @blk.vector
            def _(v):
                cst["pool"].wait(v)
                dve.inc(v.tensor_copy(SSTB[:], SST[:]))

            @blk.tensor
            def _(t):
                wdma.wait(t)
                cst["pool"].wait(t)
                pe.inc(t.matmul(GPS[0][:, 0:64], lhsT=cst["ones_row"][0:1, 0:64],
                                rhs=cst["mnorm_row"][:], start=True, stop=True))

            @blk.vector
            def _(v):
                pe.wait(v)
                dve.inc(v.tensor_copy(MSB[:], GPS[0][:, 0:64]))

            for c in range(NCH):
                tw = slice(c * Tc, (c + 1) * Tc)

                @blk.sync
                def _(s, tw=tw):
                    pe.wait(s)
                    act.wait(s)
                    dve.wait(s)
                    st_sem.wait(s)
                    dma.inc(s.dma_start(out=VC[:], in_=w["v_s"][:, :, tw].transpose([1, 0, 2])), 16)

                @blk.scalar
                def _(s, c=c):
                    dve.wait(s)
                    dma.wait(s)
                    dr = DBALL[0:16, c * Tc:(c + 1) * Tc]
                    s.activation(out=GROWS[:], in_=dr, func=AF.Abs)
                    s.drain()
                    s.activation(out=GROWS[:], in_=GROWS[:], func=AF.Exp, scale=-1.0)
                    s.drain()
                    s.activation(out=GROWS[:], in_=GROWS[:], func=AF.Ln, bias=1.0)
                    s.drain()
                    act.inc(s.activation(out=BROWS[:], in_=DBALL[32:48, c * Tc:(c + 1) * Tc],
                                         func=AF.Sigmoid))

                @blk.vector
                def _(v, c=c):
                    act.wait(v)
                    dr = DBALL[0:16, c * Tc:(c + 1) * Tc]
                    v.tensor_scalar(out=SQ[0:16, 0, 0:Tc], in0=dr, scalar1=0.0, scalar2=None, op0=OP.min)
                    v.drain()
                    v.tensor_tensor(GROWS[:], SQ[0:16, 0, 0:Tc], GROWS[:], OP.subtract)
                    v.drain()
                    dve.inc(v.tensor_tensor_scan(GROWS[:], ONES16[:], GROWS[:], 0.0, OP.mult, OP.add))

                @blk.sync
                def _(s, c=c):
                    dve.wait(s)
                    st_sem.inc(s.dma_start(out=w["gb_s"][:], in_=GROWS[:]), 16)
                    st_sem.inc(s.dma_start(out=w["bb_s"][:], in_=BROWS[:]), 16)
                    st_sem.wait(s)

                for sc_i in range(NSC if S7_SC_LEVEL >= 1 else 0):
                    base = sc_i * SC
                    hasprev = (sc_i > 0)

                    @blk.sync
                    def _(s, base=base, hasprev=hasprev):
                        pe.wait(s)
                        dve.wait(s)
                        if hasprev:
                            dma.inc(s.dma_start(out=GF[:, 0:16 * 65].rearrange("u (h t) -> u h t", t=65),
                                                in_=w["gb_s"][:, base - 1:base + SC].unsqueeze(0)), 16)
                        else:
                            dma.inc(s.dma_start(out=GF[:, 0:16 * SC].rearrange("u (h t) -> u h t", t=SC),
                                                in_=w["gb_s"][:, base:base + SC].unsqueeze(0)), 16)
                        dma.inc(s.dma_start(out=BF_[:].rearrange("u (h t) -> u h t", t=SC),
                                            in_=w["bb_s"][:, base:base + SC].unsqueeze(0)), 16)

                    wstr = 65 if hasprev else SC
                    off0 = 1 if hasprev else 0
                    gfv = GF[:, 0:16 * wstr].rearrange("u (h t) -> u h t", t=wstr)[:, :, off0:off0 + SC]
                    bfv = BF_[:].rearrange("u (h t) -> u h t", t=SC)

                    if S7_SC_LEVEL < 2:
                        continue
                    @blk.tensor
                    def _(t, gfv=gfv):
                        dma.wait(t)
                        dve.wait(t)
                        act.wait(t)
                        t.matmul(GBP[:, 0:512], lhsT=cst["ones_row"][0:1, 0:64],
                                 rhs=gfv[:, 0:8, :], start=True, stop=True)
                        pe.inc(t.matmul(GBP[:, 512:1024], lhsT=cst["ones_row"][0:1, 0:64],
                                        rhs=gfv[:, 8:16, :], start=True, stop=True))

                    @blk.vector
                    def _(v):
                        pe.wait(v)
                        dve.inc(v.tensor_copy(GSBS[:].rearrange("p a b -> p (a b)"), GBP[:]))

                    @blk.tensor
                    def _(t, bfv=bfv):
                        dve.wait(t)
                        t.matmul(GBP[:, 0:512], lhsT=cst["ones_row"][0:1, 0:64], rhs=bfv[:, 0:8, :],
                                 start=True, stop=True)
                        pe.inc(t.matmul(GBP[:, 512:1024], lhsT=cst["ones_row"][0:1, 0:64],
                                        rhs=bfv[:, 8:16, :], start=True, stop=True))

                    @blk.vector
                    def _(v):
                        pe.wait(v)
                        dve.inc(v.tensor_copy(BSBS[:].rearrange("p a b -> p (a b)"), GBP[:]))

                    if S7_SC_LEVEL < 3:
                        continue
                    @blk.tensor
                    def _(t, base=base, sc_i=sc_i):
                        dve.wait(t)
                        t.transpose(TRP2[:, 0:16], in_=GROWS[:, base:base + SC],
                                    identity=cst["ident_f32"][0:16, 0:16])
                        pe.inc(t.transpose(TRP2[:, 16:32], in_=BROWS[:, base:base + SC],
                                           identity=cst["ident_f32"][0:16, 0:16]))

                    @blk.vector
                    def _(v):
                        pe.wait(v)
                        v.tensor_copy(CSTS[:], TRP2[:, 0:16])
                        dve.inc(v.tensor_copy(BTTS[:], TRP2[:, 16:32]))

                    if not hasprev:
                        @blk.vector
                        def _(v):
                            dve.inc(v.memset(PREVB[:], 0.0))
                    else:
                        @blk.tensor
                        def _(t):
                            dve.wait(t)
                            pe.inc(t.matmul(TRP2[:, 0:16], lhsT=cst["ones_row"][0:1, 0:64],
                                            rhs=GF[:, 0:16 * 65].rearrange("u (h t) -> u h t", t=65)[:, :, 0],
                                            start=True, stop=True))

                        @blk.vector
                        def _(v):
                            pe.wait(v)
                            dve.inc(v.tensor_copy(PREVB[:], TRP2[:, 0:16]))

                    if S7_SC_LEVEL < 4:
                        continue
                    @blk.vector
                    def _(v):
                        dve.wait(v)
                        endb = GSBS[:, :, SC - 1]
                        v.tensor_tensor(ENDR[:, 0:16], endb, CSTS[:], OP.subtract)
                        v.tensor_tensor(ENDR[:, 16:32], CSTS[:], PREVB[:], OP.subtract)
                        dve.inc(v.tensor_tensor(ENDR[:, 32:48], endb, PREVB[:], OP.subtract))

                    @blk.scalar
                    def _(s):
                        dve.wait(s)
                        act.inc(s.activation(out=ENDR[:, 48:96], in_=ENDR[:, 0:48], func=AF.Exp))

                    WEND = ENDR[:, 48:64]
                    GAMT = ENDR[:, 64:80]
                    GNR = ENDR[:, 80:96]

                    if S7_SC_LEVEL < 5:
                        continue
                    @blk.vector
                    def _(v):
                        act.wait(v)
                        cstb = CSTS[:].unsqueeze(2).broadcast_to([64, 16, SC])
                        v.tensor_tensor(DIFF, cstb, GSBS[:], OP.subtract)
                        dve.inc(v.tensor_tensor(DIFFT, GSBS[:], cstb, OP.subtract))

                    @blk.scalar
                    def _(s):
                        dve.wait(s)
                        s.activation(out=X1R.rearrange("p a b -> p (a b)"),
                                     in_=DIFF.rearrange("p a b -> p (a b)"), func=AF.Exp)
                        s.drain()
                        act.inc(s.activation(out=X2R.rearrange("p a b -> p (a b)"),
                                             in_=DIFFT.rearrange("p a b -> p (a b)"), func=AF.Exp))

                    @blk.vector
                    def _(v):
                        act.wait(v)
                        mlt = cst["masklt"][:].unsqueeze(1).broadcast_to([64, 16, SC])
                        mut = cst["maskut"][:].unsqueeze(1).broadcast_to([64, 16, SC])
                        muti = cst["maskuti"][:].unsqueeze(1).broadcast_to([64, 16, SC])
                        v.tensor_tensor(MTAB[:], X2R, muti, OP.mult)
                        v.drain()
                        v.tensor_tensor(DECS, X1R, mlt, OP.mult)
                        dve.inc(v.tensor_tensor(DECTS, X2R, mut, OP.mult))

                    if S7_SC_LEVEL < 6:
                        continue
                    @blk.vector
                    def _(v, base=base, tw=tw):
                        dve.wait(v)
                        for h in range(H):
                            hb = 64 * (h % 2)
                            hm = h // 2
                            v.tensor_copy(KALL0[:, h, :], KHB[hb:hb + 64, hm, tw][:, base:base + SC])
                            ins = v.tensor_copy(QALL0[:, h, :], QHB[hb:hb + 64, hm, tw][:, base:base + SC])
                        dve.inc(ins)

                    @blk.tensor
                    def _(t):
                        dve.wait(t)
                        for h in range(H):
                            ins = t.matmul(GBP[:, h * 64:(h + 1) * 64], lhsT=KALL0[:, h, :],
                                           rhs=KALL0[:, h, :], start=True, stop=True)
                        pe.inc(ins)

                    @blk.vector
                    def _(v):
                        pe.wait(v)
                        btb = BTTS[:].unsqueeze(2).broadcast_to([64, 16, SC])
                        gb = GBP[:].rearrange("p (a b) -> p a b", b=SC)
                        v.tensor_tensor(TFS, gb, DECS, OP.mult)
                        v.drain()
                        v.tensor_tensor(LAB[:], TFS, btb, OP.mult)
                        v.drain()
                        v.tensor_tensor(TFS, gb, DECTS, OP.mult)
                        v.drain()
                        dve.inc(v.tensor_tensor(LTAB[:], TFS, BSBS[:], OP.mult))

                    if S7_SC_LEVEL < 7:
                        continue
                    @blk.tensor
                    def _(t):
                        dve.wait(t)
                        for h in range(H):
                            ins = t.matmul(GBP[:, h * 64:(h + 1) * 64], lhsT=KALL0[:, h, :],
                                           rhs=QALL0[:, h, :], start=True, stop=True)
                        pe.inc(ins)

                    @blk.vector
                    def _(v):
                        pe.wait(v)
                        gb = GBP[:].rearrange("p (a b) -> p a b", b=SC)
                        dve.inc(v.tensor_tensor(MTAB[:], gb, MTAB[:], OP.mult))

                    for h in range(H if S7_D_LEVEL >= 2 else 0):
                        hb = 64 * (h % 2)
                        hm = h // 2
                        ksl = KHB[hb:hb + 64, hm, tw][:, base:base + SC]
                        qsl = QHB[hb:hb + 64, hm, tw][:, base:base + SC]

                        @blk.vector
                        def _(v, hb=hb, hm=hm, base=base):
                            dve.wait(v)
                            dve.inc(v.tensor_copy(VSL0[:], VC[hb:hb + 64, hm, base:base + SC]))

                        lcur_t = LAB[:, h, :]
                        lcur_s = LTAB[:, h, :]
                        for lv in range(5):
                            last_lv = (lv == 4)

                            @blk.tensor
                            def _(t, lv=lv, lcur_t=lcur_t, lcur_s=lcur_s, last_lv=last_lv):
                                dve.wait(t)
                                if last_lv:
                                    pe.inc(t.matmul(GPS[1][:, 0:SC], lhsT=lcur_t, rhs=lcur_s,
                                                    start=True, stop=True))
                                else:
                                    t.matmul(GPS[0][:, 0:SC], lhsT=lcur_s, rhs=lcur_t, start=True, stop=True)
                                    pe.inc(t.matmul(GPS[1][:, 0:SC], lhsT=lcur_t, rhs=lcur_s,
                                                    start=True, stop=True))

                            @blk.vector
                            def _(v, lv=lv, last_lv=last_lv):
                                pe.wait(v)
                                if last_lv:
                                    dve.inc(v.tensor_copy(LPOW[:, 8, :], GPS[1][:, 0:SC]))
                                else:
                                    v.tensor_copy(LPOW[:, 2 * lv, :], GPS[0][:, 0:SC])
                                    dve.inc(v.tensor_copy(LPOW[:, 2 * lv + 1, :], GPS[1][:, 0:SC]))

                            if lv < 4:
                                lcur_t = LPOW[:, 2 * lv, :]
                                lcur_s = LPOW[:, 2 * lv + 1, :]

                        if S7_D_LEVEL < 3:
                            continue
                        @blk.tensor
                        def _(t, h=h):
                            dve.wait(t)
                            t.transpose(GPB[:, 0:SC], in_=KALL0[:, h, :], identity=cst["ident_bf"][0:64, 0:64])
                            pe.inc(t.transpose(GPB[:, 64:64 + SC], in_=VSL0[:],
                                               identity=cst["ident_bf"][0:64, 0:64]))

                        @blk.vector
                        def _(v, h=h):
                            pe.wait(v)
                            v.tensor_scalar(out=KTOK[:], in0=GPB[:, 0:SC], scalar1=WEND[:, h:h + 1],
                                            scalar2=None, op0=OP.mult)
                            dve.inc(v.tensor_copy(VTOK[:], GPB[:, 64:64 + SC]))

                        @blk.tensor
                        def _(t, h=h):
                            dve.wait(t)
                            pe.inc(t.matmul(GPS[0][:, 0:SC], lhsT=KALL0[:, h, :], rhs=SSTB[:, h, :],
                                            start=True, stop=True))

                        @blk.vector
                        def _(v, h=h):
                            pe.wait(v)
                            v.tensor_scalar(out=DT0[:], in0=GPS[0][:, 0:SC], scalar1=GAMT[:, h:h + 1],
                                            scalar2=None, op0=OP.mult)
                            v.drain()
                            v.tensor_tensor(DT0[:], VTOK[:], DT0[:], OP.subtract)
                            v.drain()
                            dve.inc(v.tensor_scalar(out=XU[0][:], in0=DT0[:], scalar1=BTTS[:, h:h + 1],
                                                    scalar2=None, op0=OP.mult))

                        if S7_D_LEVEL < 4:
                            continue
                        cur = 0
                        for fj, (slot, sub) in enumerate([(8, False), (7, False), (5, False),
                                                          (3, False), (1, False), (-1, True)]):
                            @blk.tensor
                            def _(t, slot=slot, cur=cur, h=h):
                                dve.wait(t)
                                lt = LTAB[:, h, :] if slot == -1 else LPOW[:, slot, :]
                                pe.inc(t.matmul(GPS[1][:, 0:SC], lhsT=lt, rhs=XU[cur][:],
                                                start=True, stop=True))

                            @blk.vector
                            def _(v, cur=cur, sub=sub):
                                pe.wait(v)
                                dve.inc(v.tensor_tensor(XU[1 - cur][:], XU[cur][:], GPS[1][:, 0:SC],
                                                        OP.subtract if sub else OP.add))

                            cur = 1 - cur

                        if S7_D_LEVEL < 5:
                            continue
                        @blk.tensor
                        def _(t, h=h, cur=cur):
                            dve.wait(t)
                            t.matmul(GPS[0][:, 0:SC], lhsT=QALL0[:, h, :], rhs=SSTB[:, h, :], start=True, stop=True)
                            pe.inc(t.matmul(GPS[1][:, 0:SC], lhsT=MTAB[:, h, :], rhs=XU[cur][:],
                                            start=True, stop=True))

                        @blk.vector
                        def _(v, h=h):
                            pe.wait(v)
                            v.tensor_scalar(out=DT0[:], in0=GPS[0][:, 0:SC], scalar1=GAMT[:, h:h + 1],
                                            scalar2=None, op0=OP.mult)
                            v.drain()
                            dve.inc(v.tensor_tensor(OALL[:, h, :], DT0[:], GPS[1][:, 0:SC], OP.add))

                        @blk.tensor
                        def _(t, h=h, cur=cur):
                            dve.wait(t)
                            pe.inc(t.matmul(GPS[0][:, 0:SC], lhsT=KTOK[:], rhs=XU[cur][:],
                                            start=True, stop=True))

                        @blk.vector
                        def _(v, h=h):
                            pe.wait(v)
                            v.scalar_tensor_tensor(out=SST[:, h, :], in0=SST[:, h, :],
                                                   scalar=GNR[:, h:h + 1], in1=GPS[0][:, 0:SC],
                                                   op0=OP.mult, op1=OP.add)
                            v.drain()
                            dve.inc(v.tensor_copy(SSTB[:, h, :], SST[:, h, :]))

                    if S7_D_LEVEL < 6:
                        continue
                    @blk.vector
                    def _(v):
                        dve.wait(v)
                        dve.inc(v.tensor_scalar(out=OALL.rearrange("p a b -> p (a b)"),
                                                in0=OALL.rearrange("p a b -> p (a b)"),
                                                scalar1=0.125, scalar2=None, op0=OP.mult))

                    @blk.scalar
                    def _(s):
                        dve.wait(s)
                        act.inc(s.activation(out=SQ[0:64, 4:6, :].rearrange("p a b -> p (a b)"),
                                             in_=OALL.rearrange("p a b -> p (a b)"), func=AF.Square))

                    @blk.vector
                    def _(v):
                        act.wait(v)
                        sqv = SQ[0:64, 4:6, :].rearrange("p a b -> p (a b)").rearrange("p (a b) -> p a b", b=SC)
                        v.tensor_reduce(ENDR[:, 0:16], sqv, axis=AX.X, op=OP.add)
                        v.drain()
                        v.tensor_scalar(out=ENDR[:, 0:16], in0=ENDR[:, 0:16], scalar1=1.0 / HD,
                                        scalar2=1e-6, op0=OP.mult, op1=OP.add)
                        v.drain()
                        dve.inc(v.reciprocal(ENDR[:, 16:32], ENDR[:, 0:16]))

                    @blk.scalar
                    def _(s):
                        dve.wait(s)
                        act.inc(s.activation(out=ENDR[:, 0:16], in_=ENDR[:, 16:32], func=AF.Sqrt))

                    @blk.vector
                    def _(v):
                        act.wait(v)
                        rnv = ENDR[:, 0:16].unsqueeze(2).broadcast_to([64, 16, SC])
                        msv = MSB[:].unsqueeze(1).broadcast_to([64, 16, SC])
                        v.tensor_tensor(OALL, OALL, rnv, OP.mult)
                        v.drain()
                        dve.inc(v.tensor_tensor(ONB[:], OALL, msv, OP.mult))

                    for hw in range(2):
                        @blk.tensor
                        def _(t, hw=hw):
                            dve.wait(t)
                            for hh in range(8):
                                h2 = hw * 8 + hh
                                ins = t.transpose(GPB[:, hh * 64:(hh + 1) * 64], in_=ONB[:, h2, :],
                                                  identity=cst["ident_bf"][0:64, 0:64])
                            pe.inc(ins)

                        @blk.vector
                        def _(v, hw=hw, base=base):
                            pe.wait(v)
                            for hh in range(8):
                                h2 = hw * 8 + hh
                                ins = v.tensor_copy(NRF[64 * (h2 % 2):64 * (h2 % 2) + 64, h2 // 2,
                                                        base:base + SC],
                                                    GPB[:, hh * 64:(hh + 1) * 64])
                            dve.inc(ins)

                @blk.sync
                def _(s, tw=tw):
                    dve.wait(s)
                    st_sem.inc(s.dma_start(out=w["nrf_s"][:, :, tw].transpose([1, 0, 2]), in_=NRF[:]), 16)

            @blk.sync
            def _(s):
                st_sem.wait(s)

        if "E" in S7_PHASES:
         with contextlib.ExitStack() as sp:
            P = lambda b, sh, dt=F32: sp.enter_context(nc.psum_tensor(ctx.name(b), sh, dt))
            SB2 = lambda b, sh, dt: sp.enter_context(nc.sbuf_tensor(ctx.name(b), sh, dt))
            OD2 = [P("b7_od2", [128, 128]) for _ in range(2)]
            OUTW = SB2("b7_outw", [128, 16, 1024], BF16)
            OBC = SB2("b7_obc", [128, 8], F32)
            NRF2 = SB2("b7_nrf2", [128, DK, Tc], BF16)
            SGC2 = SB2("b7_sgc2", [128, 8, Tc], BF16)
            blk = sp.enter_context(nc.Block(no_gpsimd_drain=True))

            @blk.gpsimd
            def _(g):
                wdma.inc(g.dma_start(out=OUTW[:], in_=w["out_W"].rearrange("(k p) m -> p k m", p=128)), 16)
                wdma.inc(g.dma_start(out=OBC[:], in_=w["outb_fm"]), 16)

            for c in range(NCH):
                tw = slice(c * Tc, (c + 1) * Tc)

                @blk.sync
                def _(s, tw=tw):
                    pe.wait(s)
                    dve.wait(s)
                    st_sem.wait(s)
                    dma.inc(s.dma_start(out=NRF2[:], in_=w["nrf_s"][:, :, tw].transpose([1, 0, 2])), 16)
                    dma.inc(s.dma_start(out=SGC2[:], in_=w["sg_s"][:, :, tw].transpose([1, 0, 2])), 16)
                    dma.inc(s.dma_start(out=cst["RC"][:, :, 0:Tc],
                                        in_=src[:, :, tw].transpose([1, 0, 2])), 16)

                dve_od = dve.v

                @blk.tensor
                def _(t, tw=tw, dve_od=dve_od):
                    dma.wait(t)
                    dve.wait(t)
                    wdma.wait(t)
                    for mo in range(DK):
                        if mo >= 2:
                            t.wait_ge(dve.h, dve_od + mo - 1)
                        for k in range(DK):
                            t.matmul(OD2[mo % 2][:], lhsT=OUTW[:, k, mo * 128:(mo + 1) * 128],
                                     rhs=NRF2[:, k, :], start=(k == 0), stop=False)
                        for k in range(DK):
                            ins = t.matmul(OD2[mo % 2][:], lhsT=OUTW[:, 8 + k, mo * 128:(mo + 1) * 128],
                                           rhs=P20N[:, k, tw], start=False, stop=(k == DK - 1))
                        pe.inc(ins)

                @blk.vector
                def _(v, tw=tw):
                    base = pe.v - DK
                    for mo in range(DK):
                        v.wait_ge(pe.h, base + mo + 1)
                        if mo > 0:
                            v.drain()
                        v.scalar_tensor_tensor(out=SQ[:, 0, 0:Tc], in0=OD2[mo % 2][:],
                                               scalar=OBC[:, mo:mo + 1], in1=SGC2[:, mo, :],
                                               op0=OP.add, op1=OP.mult)
                        v.drain()
                        dve.inc(v.tensor_tensor(cst["RC"][:, mo, 0:Tc], cst["RC"][:, mo, 0:Tc],
                                                SQ[:, 0, 0:Tc], OP.add))

                @blk.sync
                def _(s, tw=tw):
                    dve.wait(s)
                    st_sem.inc(s.dma_start(out=dst[:, :, tw].transpose([1, 0, 2]),
                                           in_=cst["RC"][:, :, 0:Tc]), 16)

            @blk.sync
            def _(s):
                st_sem.wait(s)


SUBLAYERS = ["s1", "s2", "s3", "s4", "s5", "s6", "s7", "s8"]


def build_nc(subs):
    nc = bass.Bass("TRN2", target_bir_lowering=False, debug=False)

    dram = {}

    def din(name, shape, dtype=F32):
        dram[name] = nc.dram_tensor(name, shape, dtype, kind="ExternalInput").ap()
        return dram[name]

    res_in = din("res_in", [DK, 128, T])
    res_out = nc.dram_tensor("res_out", [DK, 128, T], F32, kind="ExternalOutput").ap()
    ident = din("ident128", [128, 128])
    cmask_in = din("cmask_in", [128, 128])
    perm_in = din("perm_in", [128, 128])

    need = set(subs)
    if "s2" in need or "s4" in need:
        din("l0f_Wg", [D, 16]); din("l0f_bg", [16]); din("l0f_bias", [16])
        din("l0f_W1", [16, D, 2048]); din("l0f_b1c", [128, 16, 16])
        din("l0f_W2", [16, 2048, D]); din("l0f_b2", [16, D])
    if "s1" in need:
        din("l0a_in_proj_w", [D, 4096]); din("l0a_out_proj_w", [2048, D])
        din("l0a_x_proj_w", [2048, 96]); din("l0a_dt_cat", [65, 2048])
        din("l0a_alog_fm", [128, 16, 16]); din("l0a_conv_fm", [128, 16, 4])
        din("l0a_convb_fm", [128, 16]); din("l0a_d_fm", [128, 16])
        din("l0a_ff_w1", [D, 4096]); din("l0a_ff_w2", [4096, D])
        din("l0a_ffb1_fm", [128, 32]); din("l0a_ffb2_fm", [128, 8])
        din("l0a_lng_fm", [128, 8]); din("l0a_lnb_fm", [128, 8])
    if "s3" in need:
        din("l0b_w_kv_down", [D, 128]); din("l0b_w_kv_up", [128, 2048])
        din("l0b_w_q", [D, D]); din("l0b_w_out", [D, D])
    if "s5" in need:
        din("l1a_Wd_q_w", [D, 512]); din("l1a_Wu_q_w", [512, D]); din("l1a_Wq_r_w", [512, D])
        din("l1a_Wk_r_w", [D, D]); din("l1a_Wd_kv_w", [D, 512]); din("l1a_Wu_k_w", [512, D])
        din("l1a_Wu_v_w", [512, D]); din("l1a_Wo_w", [D, D])
        din("l1a_bdq", [128, 4]); din("l1a_buq", [128, 8]); din("l1a_bqr", [128, 8])
        din("l1a_bdkv", [128, 4]); din("l1a_buk", [128, 8]); din("l1a_bkr", [128, 8])
        din("l1a_buv_row", [1, D]); din("l1a_wob", [128, 8])
        din("rope_cos", [8, 128, T]); din("rope_sin", [8, 128, T])
    if "s7" in need:
        din("l1b_ctrlw_re", [D, 4160]); din("l1b_cb_qkv", [128, 24]); din("l1b_cb_gate", [128, 8])
        din("l1b_cb_db", [64, 1]); din("l1b_conv3_fm", [128, 24, 4])
        din("l1b_dl2_fm", [128, 8]); din("l1b_th2_fm", [128, 8]); din("l1b_nsign_fm", [128, 8])
        din("l1b_vnorm_fm", [128, 8]); din("l1b_rampl_fm", [128, 8])
        din("l1b_p20c_W", [D, 3072]); din("l1b_out_W", [2048, D]); din("l1b_outb_fm", [128, 8])
        din("l1b_mnorm_in", [1, 64])
        din("selh_in", [128, 2]); din("selb2_in", [2, 128])
        din("masklt_in", [64, 64]); din("maskut_in", [64, 64]); din("maskuti_in", [64, 64])
    if "s6" in need or "s8" in need:
        din("l1f_Wr", [D, 8]); din("l1f_br", [8]); din("l1f_bias0", [8])
        din("l1f_W1", [8, D, 4096]); din("l1f_b1c", [128, 32, 8])
        din("l1f_W2", [8, 4096, D]); din("l1f_b2", [8, D])

    scratch = {}
    scratch["qkvc_s"] = nc.dram_tensor("qkvc_s", [24, 128, T], F32, kind="Internal").ap()
    scratch["sg_s"] = nc.dram_tensor("sg_s", [8, 128, T], BF16, kind="Internal").ap()
    scratch["v_s"] = nc.dram_tensor("v_s", [8, 128, T], BF16, kind="Internal").ap()
    scratch["nrf_s"] = nc.dram_tensor("nrf_s", [8, 128, T], BF16, kind="Internal").ap()
    for nm, sh in [("resA", [DK, 128, T]), ("resB", [DK, 128, T]),
                   ("cw16", [16, T]), ("cw8", [8, T]),
                   ("mo_s", [DK, 128, T]), ("bc_s", [32, 128]),
                   ("gb_s", [16, 128]), ("bb_s", [16, 128])]:
        scratch[nm] = nc.dram_tensor(nm, sh, F32, kind="Internal").ap()

    with contextlib.ExitStack() as st:
        ctx = Ctx(nc, st)
        E = st.enter_context
        cst = {}
        for s in ["pe", "act", "dve", "pool", "dma", "qdma", "wdma", "st"]:
            cst[s] = ctx.sem(s)
        cst["ones_col"] = E(nc.sbuf_tensor("ones_col", [128, 1], F32))
        cst["ones_row"] = E(nc.sbuf_tensor("ones_row", [1, 128], F32))
        cst["ones_row_bf"] = E(nc.sbuf_tensor("ones_row_bf", [1, 128], BF16))
        cst["ident_f32"] = E(nc.sbuf_tensor("ident_f32", [128, 128], F32))
        cst["ident_bf"] = E(nc.sbuf_tensor("ident_bf", [128, 128], BF16))
        cst["cmask"] = E(nc.sbuf_tensor("cmask", [128, 128], F32))
        cst["perm_f32"] = E(nc.sbuf_tensor("perm_f32", [128, 128], F32))
        cst["selh"] = E(nc.sbuf_tensor("selh", [128, 2], F32))
        cst["selb2"] = E(nc.sbuf_tensor("selb2", [2, 128], F32))
        cst["masklt"] = E(nc.sbuf_tensor("masklt", [64, 64], F32))
        cst["maskut"] = E(nc.sbuf_tensor("maskut", [64, 64], F32))
        cst["maskuti"] = E(nc.sbuf_tensor("maskuti", [64, 64], F32))
        cst["mnorm_row"] = E(nc.sbuf_tensor("mnorm_row", [1, 64], F32))
        cst["RC"] = E(nc.sbuf_tensor("rc_shared", [128, DK, 512], F32))
        cst["SQ"] = E(nc.sbuf_tensor("sq_shared", [128, DK, 512], F32))
        cst["ROWS"] = E(nc.sbuf_tensor("rows_shared", [1, 2048], F32))

        with nc.Block(no_gpsimd_drain=True) as blk:
            @blk.gpsimd
            def _(g):
                g.memset(cst["ones_col"][:], 1.0)
                g.memset(cst["ones_row"][:], 1.0)
                g.memset(cst["ones_row_bf"][:], 1.0)
                cst["wdma"].inc(g.dma_start(out=cst["ident_f32"][:], in_=ident), 16)
                cst["wdma"].inc(g.dma_start(out=cst["ident_bf"][:], in_=ident), 16)
                cst["wdma"].inc(g.dma_start(out=cst["cmask"][:], in_=cmask_in), 16)
                cst["wdma"].inc(g.dma_start(out=cst["perm_f32"][:], in_=perm_in), 16)
                if "selh_in" in dram:
                    cst["wdma"].inc(g.dma_start(out=cst["selh"][:], in_=dram["selh_in"]), 16)
                    cst["wdma"].inc(g.dma_start(out=cst["selb2"][:], in_=dram["selb2_in"]), 16)
                    cst["wdma"].inc(g.dma_start(out=cst["masklt"][:], in_=dram["masklt_in"]), 16)
                    cst["wdma"].inc(g.dma_start(out=cst["maskut"][:], in_=dram["maskut_in"]), 16)
                    cst["wdma"].inc(g.dma_start(out=cst["maskuti"][:], in_=dram["maskuti_in"]), 16)
                cst["wdma"].wait(g)

        cur = res_in
        bufs = [scratch["resA"], scratch["resB"]]
        for i, s in enumerate(subs):
            dst = res_out if i == len(subs) - 1 else bufs[i % 2]
            if s in ("s2", "s4"):
                w = {"Wg": dram["l0f_Wg"], "bg": dram["l0f_bg"], "bias": dram["l0f_bias"],
                     "W1": dram["l0f_W1"], "b1c": dram["l0f_b1c"], "W2": dram["l0f_W2"],
                     "b2": dram["l0f_b2"], "cw_s": scratch["cw16"]}
                emit_moe(ctx, cst, cur, dst, w, nE=16, F=2048, topk=2, actfn=AF.Relu)
            elif s in ("s6", "s8"):
                w = {"Wg": dram["l1f_Wr"], "bg": dram["l1f_br"], "bias": dram["l1f_bias0"],
                     "W1": dram["l1f_W1"], "b1c": dram["l1f_b1c"], "W2": dram["l1f_W2"],
                     "b2": dram["l1f_b2"], "cw_s": scratch["cw8"]}
                emit_moe(ctx, cst, cur, dst, w, nE=8, F=4096, topk=1, actfn=AF.Gelu)
            elif s == "s1":
                w = {"in_proj_w": dram["l0a_in_proj_w"], "out_proj_w": dram["l0a_out_proj_w"],
                     "x_proj_w": dram["l0a_x_proj_w"], "dt_cat": dram["l0a_dt_cat"],
                     "alog_fm": dram["l0a_alog_fm"], "conv_fm": dram["l0a_conv_fm"],
                     "convb_fm": dram["l0a_convb_fm"], "d_fm": dram["l0a_d_fm"],
                     "ff_w1": dram["l0a_ff_w1"], "ff_w2": dram["l0a_ff_w2"],
                     "ffb1_fm": dram["l0a_ffb1_fm"], "ffb2_fm": dram["l0a_ffb2_fm"],
                     "lng_fm": dram["l0a_lng_fm"], "lnb_fm": dram["l0a_lnb_fm"],
                     "mo_s": scratch["mo_s"], "bc_s": scratch["bc_s"]}
                emit_l0_attn_a(ctx, cst, cur, dst, w)
            elif s == "s3":
                w = {"w_kv_down": dram["l0b_w_kv_down"], "w_kv_up": dram["l0b_w_kv_up"],
                     "w_q": dram["l0b_w_q"], "w_out": dram["l0b_w_out"]}
                emit_l0_attn_b(ctx, cst, cur, dst, w)
            elif s == "s7":
                w = {"ctrlw_re": dram["l1b_ctrlw_re"], "cb_qkv": dram["l1b_cb_qkv"],
                     "cb_gate": dram["l1b_cb_gate"], "cb_db": dram["l1b_cb_db"],
                     "conv3_fm": dram["l1b_conv3_fm"], "dl2_fm": dram["l1b_dl2_fm"],
                     "th2_fm": dram["l1b_th2_fm"], "nsign_fm": dram["l1b_nsign_fm"],
                     "vnorm_fm": dram["l1b_vnorm_fm"], "rampl_fm": dram["l1b_rampl_fm"],
                     "p20c_W": dram["l1b_p20c_W"], "out_W": dram["l1b_out_W"],
                     "outb_fm": dram["l1b_outb_fm"], "mnorm_in": dram["l1b_mnorm_in"],
                     "qkvc_s": scratch["qkvc_s"], "sg_s": scratch["sg_s"], "v_s": scratch["v_s"],
                     "gb_s": scratch["gb_s"], "bb_s": scratch["bb_s"], "nrf_s": scratch["nrf_s"]}
                emit_l1_attn_b(ctx, cst, cur, dst, w)
            elif s == "s5":
                w = {"Wd_q_w": dram["l1a_Wd_q_w"], "Wu_q_w": dram["l1a_Wu_q_w"],
                     "Wq_r_w": dram["l1a_Wq_r_w"], "Wk_r_w": dram["l1a_Wk_r_w"],
                     "Wd_kv_w": dram["l1a_Wd_kv_w"], "Wu_k_w": dram["l1a_Wu_k_w"],
                     "Wu_v_w": dram["l1a_Wu_v_w"], "Wo_w": dram["l1a_Wo_w"],
                     "bdq": dram["l1a_bdq"], "buq": dram["l1a_buq"], "bqr": dram["l1a_bqr"],
                     "bdkv": dram["l1a_bdkv"], "buk": dram["l1a_buk"], "bkr": dram["l1a_bkr"],
                     "buv_row": dram["l1a_buv_row"], "wob": dram["l1a_wob"],
                     "rope_cos": dram["rope_cos"], "rope_sin": dram["rope_sin"]}
                emit_l1_attn_a(ctx, cst, cur, dst, w)
            else:
                raise NotImplementedError(s)
            cur = dst

    return nc


def prep_weight_maps(inputs):
    m = {}
    m["ident128"] = np.eye(128, dtype=np.float32)
    cm = np.where(np.arange(128)[None, :] <= np.arange(128)[:, None], 0.0, -1e30)
    m["cmask_in"] = cm.astype(np.float32)
    pm = np.zeros((128, 128), np.float32)
    for i in range(0, 128, 2):
        pm[i, i + 1] = 1.0
        pm[i + 1, i] = 1.0
    m["perm_in"] = pm
    if "L1_attn_a_Wo_w" in inputs:
        m["l1a_Wd_q_w"] = inputs["L1_attn_a_Wd_q_w"]
        m["l1a_Wu_q_w"] = inputs["L1_attn_a_Wu_q_w"]
        m["l1a_Wq_r_w"] = inputs["L1_attn_a_Wq_r_w"]
        m["l1a_Wk_r_w"] = inputs["L1_attn_a_Wk_r_w"]
        m["l1a_Wd_kv_w"] = inputs["L1_attn_a_Wd_kv_w"]
        m["l1a_Wu_k_w"] = inputs["L1_attn_a_Wu_k_w"]
        m["l1a_Wu_v_w"] = inputs["L1_attn_a_Wu_v_w"]
        m["l1a_Wo_w"] = inputs["L1_attn_a_Wo_w"]
        fm = lambda v, c: np.ascontiguousarray(np.asarray(v, np.float32).reshape(c, 128).T)
        m["l1a_bdq"] = fm(inputs["L1_attn_a_Wd_q_b"], 4)
        m["l1a_buq"] = fm(inputs["L1_attn_a_Wu_q_b"], 8)
        m["l1a_bqr"] = fm(inputs["L1_attn_a_Wq_r_b"], 8)
        m["l1a_bdkv"] = fm(inputs["L1_attn_a_Wd_kv_b"], 4)
        m["l1a_buk"] = fm(inputs["L1_attn_a_Wu_k_b"], 8)
        m["l1a_bkr"] = fm(inputs["L1_attn_a_Wk_r_b"], 8)
        m["l1a_buv_row"] = np.asarray(inputs["L1_attn_a_Wu_v_b"], np.float32).reshape(1, D)
        m["l1a_wob"] = fm(inputs["L1_attn_a_Wo_b"], 8)
        ids = np.arange(512, dtype=np.float64)
        theta = (10000.0 ** (-2.0 * ids)) / 512.0
        r = np.arange(T, dtype=np.float64)[:, None] * theta[None, :]
        cosr = np.cos(r); sinr = np.sin(r)
        cos2 = np.empty((D, T), np.float32); sin2 = np.empty((D, T), np.float32)
        cos2[0::2, :] = cosr.T; cos2[1::2, :] = cosr.T
        sin2[0::2, :] = -sinr.T; sin2[1::2, :] = sinr.T
        m["rope_cos"] = np.ascontiguousarray(cos2.reshape(8, 128, T))
        m["rope_sin"] = np.ascontiguousarray(sin2.reshape(8, 128, T))
    if "L0_attn_a_in_proj_w" in inputs:
        fm16 = lambda v, c: np.ascontiguousarray(np.asarray(v, np.float32).reshape(c, 128).T)
        m["l0a_in_proj_w"] = inputs["L0_attn_a_in_proj_w"]
        m["l0a_out_proj_w"] = inputs["L0_attn_a_out_proj_w"]
        m["l0a_x_proj_w"] = inputs["L0_attn_a_x_proj_w"]
        m["l0a_dt_cat"] = np.ascontiguousarray(np.concatenate(
            [np.asarray(inputs["L0_attn_a_dt_proj_w"], np.float32),
             np.asarray(inputs["L0_attn_a_dt_proj_b"], np.float32)[None, :]], 0))
        m["l0a_alog_fm"] = np.ascontiguousarray(
            np.asarray(inputs["L0_attn_a_A_log"], np.float32).reshape(16, 128, 16).transpose(1, 0, 2))
        m["l0a_conv_fm"] = np.ascontiguousarray(
            np.asarray(inputs["L0_attn_a_conv_w"], np.float32).reshape(16, 128, 4).transpose(1, 0, 2))
        m["l0a_convb_fm"] = fm16(inputs["L0_attn_a_conv_b"], 16)
        m["l0a_d_fm"] = fm16(inputs["L0_attn_a_D"], 16)
        m["l0a_ff_w1"] = inputs["L0_attn_a_ff_w1"]
        m["l0a_ff_w2"] = inputs["L0_attn_a_ff_w2"]
        m["l0a_ffb1_fm"] = fm16(inputs["L0_attn_a_ff_b1"], 32)
        m["l0a_ffb2_fm"] = fm16(inputs["L0_attn_a_ff_b2"], 8)
        m["l0a_lng_fm"] = fm16(inputs["L0_attn_a_ln_g"], 8)
        m["l0a_lnb_fm"] = fm16(inputs["L0_attn_a_ln_b"], 8)
    if "L1_attn_b_ctrl_W" in inputs:
        fmc = lambda v, c: np.ascontiguousarray(np.asarray(v, np.float32).reshape(c, 128).T)
        cw = np.asarray(inputs["L1_attn_b_ctrl_W"], np.float32)
        cb = np.asarray(inputs["L1_attn_b_ctrl_b"], np.float32)
        perm = np.concatenate([np.arange(0, 3072), np.arange(3104, 4128),
                               np.arange(3072, 3104)])
        cwr = cw[:, perm]
        cbr = cb[perm]
        z16 = np.zeros((D, 16), np.float32)
        m["l1b_ctrlw_re"] = np.ascontiguousarray(np.concatenate(
            [cwr[:, 0:4096], cwr[:, 4096:4112], z16, cwr[:, 4112:4128], z16], 1))
        zb = np.zeros((16,), np.float32)
        dbcat = np.concatenate([cbr[4096:4112], zb, cbr[4112:4128], zb])
        m["l1b_cb_qkv"] = fmc(cbr[0:3072], 24)
        m["l1b_cb_gate"] = fmc(cbr[3072:4096], 8)
        m["l1b_cb_db"] = np.ascontiguousarray(dbcat.reshape(64, 1))
        c3 = np.stack([np.asarray(inputs["L1_attn_b_qconv_w"], np.float32),
                       np.asarray(inputs["L1_attn_b_kconv_w"], np.float32),
                       np.asarray(inputs["L1_attn_b_vconv_w"], np.float32)], 0)
        m["l1b_conv3_fm"] = np.ascontiguousarray(
            c3.reshape(3, 8, 128, 4).transpose(2, 0, 1, 3).reshape(128, 24, 4))
        m["l1b_dl2_fm"] = fmc(np.repeat(np.asarray(inputs["L1_attn_b_p20_decay_log"], np.float32), 2), 8)
        m["l1b_th2_fm"] = fmc(np.repeat(np.asarray(inputs["L1_attn_b_p20_theta"], np.float32), 2), 8)
        nsign = np.tile(np.array([1.0, -1.0], np.float32), 512)
        m["l1b_nsign_fm"] = fmc(nsign, 8)
        m["l1b_vnorm_fm"] = fmc(inputs["L1_attn_b_vnorm_scale"], 8)
        m["l1b_rampl_fm"] = fmc(inputs["L1_attn_b_ramp_logit"], 8)
        m["l1b_p20c_W"] = inputs["L1_attn_b_p20c_W"]
        m["l1b_out_W"] = inputs["L1_attn_b_out_W"]
        m["l1b_outb_fm"] = fmc(inputs["L1_attn_b_out_b"], 8)
        m["l1b_mnorm_in"] = np.asarray(inputs["L1_attn_b_mnorm_scale"], np.float32).reshape(1, 64)
        m["selh_in"] = np.ascontiguousarray(
            (np.arange(128)[:, None] // 64 == np.arange(2)[None, :]).astype(np.float32))
        m["selb2_in"] = np.ascontiguousarray(m["selh_in"].T)
        ii = np.arange(64)
        m["masklt_in"] = (ii[None, :] < ii[:, None]).astype(np.float32)
        m["maskut_in"] = (ii[None, :] > ii[:, None]).astype(np.float32)
        m["maskuti_in"] = (ii[None, :] >= ii[:, None]).astype(np.float32)
    if "L0_attn_b_w_q" in inputs:
        m["l0b_w_kv_down"] = inputs["L0_attn_b_w_kv_down"]
        m["l0b_w_kv_up"] = inputs["L0_attn_b_w_kv_up"]
        m["l0b_w_q"] = inputs["L0_attn_b_w_q"]
        m["l0b_w_out"] = inputs["L0_attn_b_w_out"]
    if "L0_ffn_Wg" in inputs:
        m["l0f_Wg"] = inputs["L0_ffn_Wg"]
        m["l0f_bg"] = inputs["L0_ffn_bg"]
        m["l0f_bias"] = inputs["L0_ffn_bias"]
        m["l0f_W1"] = inputs["L0_ffn_W1"]
        m["l0f_b1c"] = np.ascontiguousarray(
            inputs["L0_ffn_b1"].reshape(16, 16, 128).transpose(2, 1, 0))
        m["l0f_W2"] = inputs["L0_ffn_W2"]
        m["l0f_b2"] = inputs["L0_ffn_b2"]
    if "L1_ffn_W1" in inputs:
        m["l1f_Wr"] = inputs["L1_ffn_Wr"]
        m["l1f_br"] = inputs["L1_ffn_br"]
        m["l1f_bias0"] = np.zeros((8,), np.float32)
        m["l1f_W1"] = inputs["L1_ffn_W1"]
        m["l1f_b1c"] = np.ascontiguousarray(
            inputs["L1_ffn_b1"].reshape(8, 32, 128).transpose(2, 1, 0))
        m["l1f_W2"] = inputs["L1_ffn_W2"]
        m["l1f_b2"] = inputs["L1_ffn_b2"]
    return m


def to_fm(x_core):
    return np.ascontiguousarray(x_core.T.reshape(DK, 128, T))


def from_fm(y_fm):
    return np.ascontiguousarray(y_fm.reshape(D, T).T)


def run_subs(subs, res_in_full, inputs):
    nc = build_nc(subs)
    wm = prep_weight_maps(inputs)
    in_maps = []
    for c in range(NCORES):
        d = dict(wm)
        d["res_in"] = to_fm(np.asarray(res_in_full[c], np.float32))
        in_maps.append(d)
    res = run_bass_kernel_spmd(nc, in_maps, core_ids=list(range(NCORES)))
    return np.stack([from_fm(r["res_out"]) for r in res.results])


def kernel(**inputs):
    cur = np.asarray(inputs["L0_attn_a_x"], np.float32)
    return run_subs(SUBLAYERS, cur, inputs)
